# version 01
# baseline (speedup 1.0000x reference)
import numpy as np
import ml_dtypes
from contextlib import ExitStack
import concourse.bass as bass
import concourse.mybir as mybir
from concourse.bass_utils import run_bass_kernel_spmd

F32 = mybir.dt.float32
BF16 = mybir.dt.bfloat16
AF = mybir.ActivationFunctionType
ALU = mybir.AluOpType
AX = mybir.AxisListType

D = 1024
L = 2048
NSEQ = 2
NT = 16
NG = 4
NDC = 8
E = 16
G = 4
INTER = 512
NEXP = 18
INV_SQRT_D = 1.0 / 32.0
EPS = 1e-5
DRAIN_DIST = 3
EDBG_E = 0
EDBG_Y = 1
EDBG_IC = 1
BARRIER_P2B = False
DIRECT_VV = False
P2A_PARTS = {'sc','pv','proj','x1','vvt'}
_BARN = [0]
EXPERT_ORDER = list(range(18))

ENGINES = ("sync", "tensor", "vector", "scalar", "gpsimd")


class _Op:
    __slots__ = ("eng", "fn", "reads", "writes", "dma", "pos", "epos", "waits",
                 "drain", "need_inc", "semval", "clock", "tag", "qslot", "qcount")

    def __init__(self):
        self.need_inc = False
        self.semval = None


class Prog:
    def __init__(self):
        self.ops = []
        self.epos = {e: 0 for e in ENGINES}
        self.freed = []
        self.alias_events = []

    def op(self, eng, fn, r=(), w=(), dma=False, tag=""):
        o = _Op()
        o.eng = eng
        o.fn = fn
        o.reads = tuple(r)
        o.writes = tuple(w)
        o.dma = dma
        o.tag = tag
        o.pos = len(self.ops)
        o.epos = self.epos[eng]
        self.epos[eng] += 1
        self.ops.append(o)
        return o

    def retire(self, *keys):
        self.freed.extend(keys)

    def new_buf(self, key):
        self.alias_events.append((len(self.ops), key, len(self.freed)))

    def finalize(self, nc, block, stack):
        POOL = 8
        sems = {}
        for e in ENGINES:
            sems[("c", e)] = stack.enter_context(nc.semaphore(f"cs_{e}"))
        for e in ("sync", "gpsimd", "scalar", "vector"):
            for p in range(POOL):
                sems[("q", e, p)] = stack.enter_context(nc.semaphore(f"qs_{e}{p}"))

        qidx = {e: 0 for e in ENGINES}
        qslotcnt = {}
        for o in self.ops:
            if o.dma:
                i = qidx[o.eng]
                qidx[o.eng] += 1
                o.qslot = i % POOL
                kqs = (o.eng, o.qslot)
                qslotcnt[kqs] = qslotcnt.get(kqs, 0) + 1
                o.qcount = qslotcnt[kqs]
                o.semval = 16 * o.qcount

        writers = {}
        readers = {}
        clocks = {e: {} for e in ENGINES}
        ev_i = 0
        pending_alias = {}

        def skey(o):
            return ("q", o.eng, o.qslot) if o.dma else o.eng

        def covered(clk, d):
            return clk.get(skey(d), -1) >= d.epos

        for o in self.ops:
            while ev_i < len(self.alias_events) and self.alias_events[ev_i][0] <= o.pos:
                _, _k, _n = self.alias_events[ev_i]
                pending_alias[_k] = _n
                ev_i += 1
            deps = {}
            for k in o.reads:
                for d in writers.get(k, ()):
                    deps[d.pos] = d
            for k in o.writes:
                for d in writers.get(k, ()):
                    deps[d.pos] = d
                for r in readers.get(k, ()):
                    deps[r.pos] = r
            for k in o.writes + o.reads:
                n = pending_alias.pop(k, None)
                if n is not None:
                    for fk in self.freed[:n]:
                        for d in writers.get(fk, ()):
                            deps[d.pos] = d
                        for r in readers.get(fk, ()):
                            deps[r.pos] = r
            deps.pop(o.pos, None)

            o.waits = []
            o.drain = False
            myclock = dict(clocks[o.eng])
            for p in sorted(deps):
                d = deps[p]
                dk = skey(d)
                if d.eng == o.eng and not d.dma:
                    if o.epos - d.epos <= DRAIN_DIST:
                        o.drain = True
                    for e2, v in d.clock.items():
                        if v > myclock.get(e2, -1):
                            myclock[e2] = v
                    continue
                if myclock.get(dk, -1) >= d.epos:
                    continue
                o.waits.append(d)
                d.need_inc = True
                for e2, v in d.clock.items():
                    if v > myclock.get(e2, -1):
                        myclock[e2] = v
                myclock[dk] = d.epos
            o.clock = dict(myclock)
            o.clock[skey(o)] = o.epos
            clocks[o.eng] = dict(myclock) if o.dma else dict(o.clock)

            for k in o.reads:
                readers.setdefault(k, []).append(o)
            for k in o.writes:
                prev = writers.get(k, [])
                writers[k] = [d for d in prev if not covered(o.clock, d)] + [o]
                readers[k] = []

        ccnt = {e: 0 for e in ENGINES}
        for o in self.ops:
            if o.dma:
                o.need_inc = True
            elif o.need_inc:
                ccnt[o.eng] += 1
                o.semval = ccnt[o.eng]

        per_eng = {e: [o for o in self.ops if o.eng == e] for e in ENGINES}
        qfinal = {(e, p): 16 * c for (e, p), c in qslotcnt.items()}

        def mk_body(ename):
            ops = per_eng[ename]

            def body(eng):
                for o in ops:
                    if o.drain:
                        eng.drain()
                    best = {}
                    if o.waits:
                        for d in o.waits:
                            kk = ("q", d.eng, d.qslot) if d.dma else ("c", d.eng)
                            if best.get(kk, -1) < d.semval:
                                best[kk] = d.semval
                    if o.dma and o.qcount > 1:
                        kk = ("q", o.eng, o.qslot)
                        if best.get(kk, -1) < 16 * (o.qcount - 1):
                            best[kk] = 16 * (o.qcount - 1)
                    for kk, v in best.items():
                        eng.wait_ge(sems[kk], v)
                    inst = o.fn(eng)
                    if o.dma:
                        inst.then_inc(sems[("q", o.eng, o.qslot)], 16)
                    elif o.need_inc:
                        inst.then_inc(sems[("c", o.eng)], 1)
                if ename == "sync":
                    for (e, p), v in qfinal.items():
                        eng.wait_ge(sems[("q", e, p)], v)
            return body

        for ename in ENGINES:
            if per_eng[ename] or ename == "sync":
                getattr(block, ename)(mk_body(ename))



def build_nc(taps=()):
    taps = set(taps)
    nc = bass.Bass("TRN2", target_bir_lowering=False)
    P = Prog()

    xs = nc.dram_tensor("xs", [NSEQ, L, D], F32, kind="ExternalInput")
    Wqkv = nc.dram_tensor("Wqkv", [D, 3 * D], F32, kind="ExternalInput")
    bqkv = nc.dram_tensor("bqkv", [3 * D], F32, kind="ExternalInput")
    Wproj = nc.dram_tensor("Wproj", [D, D], F32, kind="ExternalInput")
    bproj = nc.dram_tensor("bproj", [D], F32, kind="ExternalInput")
    gamma = nc.dram_tensor("gamma", [1, 1], F32, kind="ExternalInput")
    gate_w = nc.dram_tensor("gate_w", [E, D], F32, kind="ExternalInput")
    gate_b = nc.dram_tensor("gate_b", [E], F32, kind="ExternalInput")
    We1 = nc.dram_tensor("We1", [E, D, INTER], F32, kind="ExternalInput")
    be1 = nc.dram_tensor("be1", [E, INTER], F32, kind="ExternalInput")
    We2 = nc.dram_tensor("We2", [E, INTER, D], F32, kind="ExternalInput")
    be2 = nc.dram_tensor("be2", [E, D], F32, kind="ExternalInput")
    We3 = nc.dram_tensor("We3", [E, D, INTER], F32, kind="ExternalInput")
    be3 = nc.dram_tensor("be3", [E, INTER], F32, kind="ExternalInput")
    ws1 = nc.dram_tensor("ws1", [D, 2 * INTER], F32, kind="ExternalInput")
    bs1 = nc.dram_tensor("bs1", [2 * INTER], F32, kind="ExternalInput")
    ws2 = nc.dram_tensor("ws2", [2 * INTER, D], F32, kind="ExternalInput")
    bs2 = nc.dram_tensor("bs2", [D], F32, kind="ExternalInput")
    ws3 = nc.dram_tensor("ws3", [D, 2 * INTER], F32, kind="ExternalInput")
    bs3 = nc.dram_tensor("bs3", [2 * INTER], F32, kind="ExternalInput")
    identc = nc.dram_tensor("identc", [128, 128], BF16, kind="ExternalInput")
    onesb = nc.dram_tensor("onesb", [1, 512], BF16, kind="ExternalInput")
    onesf = nc.dram_tensor("onesf", [1, 128], F32, kind="ExternalInput")
    dmD = nc.dram_tensor("dmD", [128, 128], F32, kind="ExternalInput")
    dmC = nc.dram_tensor("dmC", [128, 128], F32, kind="ExternalInput")

    out = nc.dram_tensor("out", [NSEQ, L, D], F32, kind="ExternalOutput")
    vvdbg = nc.dram_tensor("vvdbg", [128, NDC * L], BF16, kind="ExternalInput") if DIRECT_VV else None
    x1kind = "ExternalOutput" if "x1" in taps else "Internal"
    x1s = nc.dram_tensor("x1s", [NSEQ, L, D], F32, kind=x1kind)

    tap_t = {}
    if "qkv" in taps:
        tap_t["tq"] = nc.dram_tensor("tap_q", [128, NDC * L], BF16, kind="ExternalOutput")
        tap_t["tk"] = nc.dram_tensor("tap_k", [128, NDC * L], BF16, kind="ExternalOutput")
        tap_t["tv"] = nc.dram_tensor("tap_v", [128, NT * D], BF16, kind="ExternalOutput")
    if "vv" in taps:
        tap_t["tvv"] = nc.dram_tensor("tap_vv", [128, NDC * L], BF16, kind="ExternalOutput")
    if "comb" in taps:
        tap_t["tcomb"] = nc.dram_tensor("tap_comb", [128, NT * E], F32, kind="ExternalOutput")
    if "vv2" in taps:
        tap_t["tvv2"] = nc.dram_tensor("tap_vv2", [128, NDC * L], BF16, kind="ExternalOutput")
    if "y" in taps:
        tap_t["ty"] = nc.dram_tensor("tap_y", [128, NT * D], F32, kind="ExternalOutput")
    if "rdbg" in taps:
        tap_t["trd"] = nc.dram_tensor("tap_rdbg", [128, 128], F32, kind="ExternalOutput")
    if "edbg" in taps:
        tap_t["th"] = nc.dram_tensor("tap_h", [128, 2048], BF16, kind="ExternalOutput")
        tap_t["ty0"] = nc.dram_tensor("tap_y0", [128, D], F32, kind="ExternalOutput")
        tap_t["tsl"] = nc.dram_tensor("tap_sl", [128, 4096], BF16, kind="ExternalOutput")
        tap_t["tp1"] = nc.dram_tensor("tap_p1", [128, 1024], F32, kind="ExternalOutput")

    A = nc.alloc_sbuf_tensor
    ident = A("ident", [128, 128], BF16)
    ones_b = A("ones_b", [1, 512], BF16)
    ones_f = A("ones_f", [1, 128], F32)
    dD = A("dD", [128, 128], F32)
    dC = A("dC", [128, 128], F32)
    dmask = A("dmask", [128, 128], F32)
    lg1 = A("lg1", [1, 1], F32)
    lgc = A("lgc", [128, 1], F32)
    bqkv_v = A("bqkv_v", [1, 1024], BF16)
    bproj_r = A("bproj_r", [1, 1024], BF16)
    gws = A("gws", [128, NDC * E], BF16)
    gbb = A("gbb", [128, E], F32)
    x_t = [A(f"x_t{i}", [128, D], F32) for i in range(2)]
    sqs = [A(f"sqs{i}", [128, D], F32) for i in range(2)]
    lns = [A(f"lns{i}", [128, 8], F32) for i in range(2)]
    sms = [A(f"sms{i}", [128, 4], F32) for i in range(2)]
    comb = A("comb", [128, NT * E], F32)
    r_orig = A("r_orig", [128, E], F32)
    r_s = A("r_s", [128, E], F32)
    r_gs = A("r_gs", [128, G], F32)
    r_t4 = A("r_t4", [128, G], F32)
    r_ms = A("r_ms", [128, E], F32)
    r_sc = A("r_sc", [128, 8], F32)
    rdbg = A("rdbg", [128, 128], F32) if "rdbg" in taps else None

    psS = nc.alloc_psum_tensor("psS", [128, 2048], F32)
    pbx = [nc.alloc_psum_tensor(f"pb{i}", [128, 512], F32) for i in range(2)]
    ptx = [nc.alloc_psum_tensor(f"pt{i}", [128, 1024], BF16) for i in range(2)]

    def PS(i):
        if i < 4:
            return psS[:, i * 512:(i + 1) * 512], f"bk{i}"
        return pbx[i - 4][:, :], f"bk{i}"

    def PS128(i, j):
        if i < 4:
            return psS[:, i * 512 + j * 128: i * 512 + (j + 1) * 128]
        return pbx[i - 4][:, j * 128:(j + 1) * 128]

    def PSB(i):
        return ptx[i - 6][:, :], f"bk{i}"

    def PSB128(i, j):
        return ptx[i - 6][:, j * 128:(j + 1) * 128]

    def fm3(src2d, p=128):
        return src2d.rearrange("(c p) o -> p c o", p=p)

    def sb3(t, c):
        return t[:, :].rearrange("p (c o) -> p c o", c=c)

    def dma(queue, outap, inap, r, w, tag=""):
        P.op(queue, lambda e, o=outap, i=inap: e.dma_start(o, i), r=r, w=w,
             dma=True, tag=tag)

    def dma_slow(queue, outap, inap, r, w, tag=""):
        P.op(queue, lambda e, o=outap, i=inap: e.dma_start(
            o, i, allow_slow_non_contiguous=True), r=r, w=w, dma=True, tag=tag)

    def mm(outap, bank, lhsT, rhs, r, start, stop):
        P.op("tensor", lambda e, o=outap, l=lhsT, rr=rhs, s1=start, s2=stop:
             e.matmul(o, l, rr, start=s1, stop=s2), r=r, w=[bank])

    def tp(outap, bank, inap, r):
        P.op("tensor", lambda e, o=outap, i=inap: e.transpose(o, i, ident[:, :]),
             r=list(r) + ["ident"], w=[bank])

    def vop(fn, r, w, tag=""):
        P.op("vector", fn, r=r, w=w, tag=tag)

    def sop(fn, r, w, tag=""):
        P.op("scalar", fn, r=r, w=w, tag=tag)

    def ln_tile(src_ap, src_keys, dst_ap, dst_key, slot, extra_scale=1.0):
        ls = lns[slot]
        lk = f"lns{slot}"
        sq = sqs[slot]
        sk = f"sqs{slot}"
        su, mean, msq, var, inv, nb = (ls[:, 0:1], ls[:, 1:2], ls[:, 2:3],
                                       ls[:, 3:4], ls[:, 4:5], ls[:, 5:6])
        sop(lambda e: e.activation(sq[:, :], src_ap, AF.Square, accum_out=su),
            r=src_keys, w=[sk, lk])
        vop(lambda e: e.reduce_sum(mean, src_ap, axis=AX.X), r=src_keys, w=[lk])
        vop(lambda e: e.tensor_scalar(mean, mean, 1.0 / D, None, ALU.mult),
            r=[lk], w=[lk])
        vop(lambda e: e.tensor_scalar(msq, mean, mean, -EPS, ALU.mult, op1=ALU.add),
            r=[lk], w=[lk])
        vop(lambda e: e.scalar_tensor_tensor(var, su, 1.0 / D, msq, ALU.mult,
                                             ALU.subtract), r=[lk], w=[lk])
        sop(lambda e: e.activation(var, var, AF.Ln), r=[lk], w=[lk])
        sop(lambda e: e.activation(inv, var, AF.Exp, scale=-0.5), r=[lk], w=[lk])
        if extra_scale != 1.0:
            vop(lambda e: e.tensor_scalar(inv, inv, extra_scale, None, ALU.mult),
                r=[lk], w=[lk])
        vop(lambda e: e.tensor_scalar(nb, mean, inv, -1.0, ALU.mult, op1=ALU.mult),
            r=[lk], w=[lk])
        sop(lambda e: e.activation(dst_ap, src_ap, AF.Identity, bias=nb, scale=inv),
            r=list(src_keys) + [lk], w=[dst_key])

    dma("sync", ident[:, :], identc.ap(), r=[], w=["ident"])
    dma("sync", ones_b[:, :], onesb.ap(), r=[], w=["ones_b"])
    dma("sync", ones_f[:, :], onesf.ap(), r=[], w=["ones_f"])
    dma("sync", dD[:, :], dmD.ap(), r=[], w=["dD"])
    dma("sync", dC[:, :], dmC.ap(), r=[], w=["dC"])
    rowify = lambda apx, n: apx.rearrange("(a b) -> a b", a=1)
    dma("gpsimd", bqkv_v[:, :], bqkv.ap()[2048:3072].rearrange("(a b) -> a b", a=1),
        r=[], w=["bqkv_v"])
    dma("gpsimd", bproj_r[:, :], rowify(bproj.ap(), D), r=[], w=["bproj_r"])

    dma("sync", lgc[:, :], bass.AP(gamma, 0, [[0, 128], [1, 1]]),
        r=[], w=["lgc"])
    sop(lambda e: e.activation(lgc[:, :], lgc[:, :], AF.Ln), r=["lgc"], w=["lgc"])
    sop(lambda e: e.activation(dmask[:, :], dD[:, :], AF.Exp, scale=lgc[:, :]),
        r=["dD", "lgc"], w=["dmask"])
    vop(lambda e: e.tensor_tensor(dmask[:, :], dmask[:, :], dC[:, :], ALU.mult),
        r=["dmask", "dC"], w=["dmask"])

    dma("sync", gbb[:, :], bass.AP(gate_b, 0, [[0, 128], [1, E]]),
        r=[], w=["gbb"])

    bqkv_qk = A("bqkv_qk", [1, 2048], BF16)
    dma("gpsimd", bqkv_qk[:, :], bqkv.ap()[0:2048].rearrange("(a b) -> a b", a=1),
        r=[], w=["bqkv_qk"])

    gwtmp_cm = nc.sbuf_tensor("gwtmp", [16, D], BF16, side="right")
    gwtmp = gwtmp_cm.__enter__()
    P.new_buf("gwtmp")
    dma("gpsimd", gwtmp[:, :], gate_w.ap(), r=[], w=["gwtmp"])
    for dc in range(NDC):
        apo, ko = PSB(6 + dc % 2)
        P.op("tensor", lambda e, dc=dc, apo=apo: e.transpose(
            apo[:, 0:16], gwtmp[0:16, dc * 128:(dc + 1) * 128], ident[0:16, 0:16]),
            r=["gwtmp", "ident"], w=[ko])
        vop(lambda e, dc=dc, apo=apo: e.tensor_copy(
            gws[:, dc * E:(dc + 1) * E], apo[:, 0:16]), r=[ko], w=["gws"])
    gwtmp_cm.__exit__(None, None, None)
    P.retire("gwtmp")

    for s in range([1] if DIRECT_VV else [NSEQ])[0] if False else range(1 if DIRECT_VV else NSEQ):
        xseq = xs.ap()[s]
        x1seq = x1s.ap()[s]
        outseq = out.ap()[s]

        k_cm = nc.sbuf_tensor(f"k_fm{s}", [128, NDC * L], BF16, side="left")
        k_fm = k_cm.__enter__(); P.new_buf("k_fm")
        v_cm = nc.sbuf_tensor(f"v_tm{s}", [128, NT * D], BF16, side="left")
        v_tm = v_cm.__enter__(); P.new_buf("v_tm")
        q_cm = nc.sbuf_tensor(f"q_fm{s}", [128, NDC * L], BF16, side="left")
        q_fm = q_cm.__enter__(); P.new_buf("q_fm")
        wq_cm = nc.sbuf_tensor(f"wqkv{s}", [128, NDC * 3 * D], BF16, side="right")
        wqkv_sb = wq_cm.__enter__(); P.new_buf("wqkv")
        uf_cm = nc.sbuf_tensor(f"u_fm{s}", [128, NDC * 512], BF16, side="right")
        u_fm = uf_cm.__enter__(); P.new_buf("u_fm")
        ut_cm = nc.sbuf_tensor(f"u_tm{s}", [128, 2 * D], BF16, side="right")
        u_tm = ut_cm.__enter__(); P.new_buf("u_tm")

        if DIRECT_VV == 1:
            P1_RANGE = []
        elif DIRECT_VV == 2:
            P1_RANGE = list(range(NG))
        else:
            P1_RANGE = list(range(NG))
        dma("gpsimd", sb3(wqkv_sb, NDC), fm3(Wqkv.ap()),
            r=[], w=["wqkv"], tag="wqkv")

        for g in P1_RANGE:
            for tt in range(4):
                t = g * 4 + tt
                xk = f"x_t{t % 2}"
                dma("sync", x_t[t % 2][:, :], xseq[t * 128:(t + 1) * 128, :],
                    r=[], w=[xk], tag="x")
                uk = f"u_tm{tt % 2}"
                u_ap = u_tm[:, (tt % 2) * D:(tt % 2) * D + D]
                ln_tile(x_t[t % 2][:, :], [xk], u_ap, "u_tm", tt % 2)
                bi = 6 + tt % 2
                apo, ko = PSB(bi)
                for dc in range(NDC):
                    tp(PSB128(bi, dc), ko, u_ap[:, dc * 128:(dc + 1) * 128],
                       r=["u_tm"])
                dst = u_fm[:, :].rearrange("p (c t) -> p c t", c=NDC)[
                    :, :, tt * 128:(tt + 1) * 128]
                src = apo.rearrange("p (c t) -> p c t", c=NDC)
                vop(lambda e, d0=dst, s0=src: e.tensor_copy(d0, s0),
                    r=[ko], w=["u_fm"])
            for which, base, kdst in (("q", 0, q_fm), ("k", D, k_fm)):
                for oc in range(NDC):
                    bi = 4 + oc % 2
                    apo, ko = PS(bi)
                    for dc in range(NDC):
                        mm(apo, ko,
                           wqkv_sb[:, dc * 3 * D + base + oc * 128:
                                   dc * 3 * D + base + (oc + 1) * 128],
                           u_fm[:, dc * 512:(dc + 1) * 512],
                           r=["wqkv", "u_fm"], start=(dc == 0), stop=False)
                    mm(apo, ko, bqkv_qk[:, base + oc * 128: base + (oc + 1) * 128],
                       ones_b[:, :], r=["bqkv_qk", "ones_b"], start=False, stop=True)
                    dst = kdst[:, oc * L + g * 512: oc * L + (g + 1) * 512]
                    scl = INV_SQRT_D if which == "q" else 1.0
                    sop(lambda e, d0=dst, a0=apo, s0=scl: e.activation(
                        d0, a0, AF.Identity, scale=s0),
                        r=[ko], w=["q_fm" if which == "q" else "k_fm"])
            for tt in range(4):
                t = g * 4 + tt
                for half in range(2):
                    bi = (tt * 2 + half) % 4
                    apo, ko = PS(bi)
                    for dc in range(NDC):
                        mm(apo, ko, u_fm[:, dc * 512 + tt * 128: dc * 512 + (tt + 1) * 128],
                           wqkv_sb[:, dc * 3 * D + 2 * D + half * 512:
                                   dc * 3 * D + 2 * D + (half + 1) * 512],
                           r=["wqkv", "u_fm"], start=(dc == 0), stop=False)
                    mm(apo, ko, ones_b[:, 0:128], bqkv_v[:, half * 512:(half + 1) * 512],
                       r=["ones_b", "bqkv_v"], start=False, stop=True)
                    dst = v_tm[:, t * D + half * 512: t * D + (half + 1) * 512]
                    vop(lambda e, d0=dst, a0=apo: e.tensor_copy(d0, a0),
                        r=[ko], w=["v_tm"])

        ut_cm.__exit__(None, None, None); P.retire("u_tm")
        uf_cm.__exit__(None, None, None); P.retire("u_fm")
        wq_cm.__exit__(None, None, None); P.retire("wqkv")

        if "qkv" in taps and s == 0:
            dma("sync", tap_t["tq"].ap(), q_fm[:, :], r=["q_fm"], w=["tapq"])
            dma("sync", tap_t["tk"].ap(), k_fm[:, :], r=["k_fm"], w=["tapk"])
            dma("sync", tap_t["tv"].ap(), v_tm[:, :], r=["v_tm"], w=["tapv"])

        vv_cm = nc.sbuf_tensor(f"vv_fm{s}", [128, NDC * L], BF16, side="right")
        vv_fm = vv_cm.__enter__(); P.new_buf("vv_fm")
        wp_cm = nc.sbuf_tensor(f"wproj{s}", [128, NDC * D], BF16, side="right")
        wproj_sb = wp_cm.__enter__(); P.new_buf("wproj")
        P_cm = nc.sbuf_tensor(f"P_sb{s}", [128, L], BF16, side="right")
        P_sb = P_cm.__enter__(); P.new_buf("P_sb")
        PT_cm = nc.sbuf_tensor(f"PT_sb{s}", [128, L], BF16, side="right")
        PT_sb = PT_cm.__enter__(); P.new_buf("PT_sb")
        atm_cm = nc.sbuf_tensor(f"a_tm{s}", [128, D], BF16, side="right")
        a_tm = atm_cm.__enter__(); P.new_buf("a_tm")
        afm_cm = nc.sbuf_tensor(f"a_fm{s}", [128, D], BF16, side="right")
        a_fm = afm_cm.__enter__(); P.new_buf("a_fm")
        a2_cm = nc.sbuf_tensor(f"a2{s}", [128, D], F32, side="right")
        a2 = a2_cm.__enter__(); P.new_buf("a2")
        a2n_cm = nc.sbuf_tensor(f"a2n{s}", [128, D], F32, side="right")
        a2n = a2n_cm.__enter__(); P.new_buf("a2n")

        dma("gpsimd", sb3(wproj_sb, NDC), fm3(Wproj.ap()),
            r=[], w=["wproj"], tag="wproj")

        if DIRECT_VV in (1, 2) and s == 0:
            dma("sync", vv_fm[:, :], vvdbg.ap(), r=[], w=["vv_fm"])
        P2A_RANGE = [] if DIRECT_VV in (1, 2) else list(range(NT))
        for t in P2A_RANGE:
            for tc in (range(4) if 'sc' in P2A_PARTS else []):
                apo, ko = PS(tc)
                for dc in range(NDC):
                    mm(apo, ko,
                       q_fm[:, dc * L + t * 128: dc * L + (t + 1) * 128],
                       k_fm[:, dc * L + tc * 512: dc * L + (tc + 1) * 512],
                       r=["q_fm", "k_fm"], start=(dc == 0), stop=(dc == NDC - 1))
            dtc, dj = t // 4, t % 4
            apd = PS128(dtc, dj)
            if 'sc' in P2A_PARTS:
                vop(lambda e, a0=apd: e.tensor_tensor(a0, a0, dmask[:, :], ALU.mult),
                    r=[f"bk{dtc}", "dmask"], w=[f"bk{dtc}"])
            sm = sms[t % 2]
            smk = f"sms{t % 2}"
            nmax, ssum, rs = sm[:, 0:1], sm[:, 1:2], sm[:, 2:3]
            if 'sc' in P2A_PARTS:
                vop(lambda e, n0=nmax: e.reduce_max(n0, psS[:, :], axis=AX.X,
                                                    negate=True),
                    r=["bk0", "bk1", "bk2", "bk3"], w=[smk])
                sop(lambda e, n0=nmax, s0=ssum: e.activation(
                    P_sb[:, :], psS[:, :], AF.Exp, bias=n0, accum_out=s0),
                    r=["bk0", "bk1", "bk2", "bk3", smk], w=["P_sb", smk])
                vop(lambda e, s0=ssum, r0=rs: e.reciprocal(r0, s0), r=[smk], w=[smk])
            for b in (range(2) if 'pv' in P2A_PARTS else []):
                apo, ko = PSB(6 + b)
                for j in range(8):
                    tk = b * 8 + j
                    tp(PSB128(6 + b, j), ko,
                       P_sb[:, tk * 128:(tk + 1) * 128], r=["P_sb"])
                vop(lambda e, b=b, a0=apo: e.tensor_copy(
                    PT_sb[:, b * 1024:(b + 1) * 1024], a0), r=[ko], w=["PT_sb"])
            for half in (range(2) if 'pv' in P2A_PARTS else []):
                apo, ko = PS(half)
                for tk in range(NT):
                    mm(apo, ko, PT_sb[:, tk * 128:(tk + 1) * 128],
                       v_tm[:, tk * D + half * 512: tk * D + (half + 1) * 512],
                       r=["PT_sb", "v_tm"], start=(tk == 0), stop=(tk == NT - 1))
                sop(lambda e, a0=apo, h=half, r0=rs: e.activation(
                    a_tm[:, h * 512:(h + 1) * 512], a0, AF.Identity, scale=r0),
                    r=[ko, smk], w=["a_tm"])
            if 'proj' in P2A_PARTS:
                apo, ko = PSB(6)
                for dc in range(NDC):
                    tp(PSB128(6, dc), ko, a_tm[:, dc * 128:(dc + 1) * 128],
                       r=["a_tm"])
                vop(lambda e, a0=apo: e.tensor_copy(a_fm[:, :], a0),
                    r=[ko], w=["a_fm"])
            for half in (range(2) if 'proj' in P2A_PARTS else []):
                apo, ko = PS(4 + half)
                for dc in range(NDC):
                    mm(apo, ko, a_fm[:, dc * 128:(dc + 1) * 128],
                       wproj_sb[:, dc * D + half * 512: dc * D + (half + 1) * 512],
                       r=["a_fm", "wproj"], start=(dc == 0), stop=False)
                mm(apo, ko, ones_b[:, 0:128], bproj_r[:, half * 512:(half + 1) * 512],
                   r=["ones_b", "bproj_r"], start=False, stop=True)
                vop(lambda e, a0=apo, h=half: e.tensor_copy(
                    a2[:, h * 512:(h + 1) * 512], a0), r=[ko], w=["a2"])
            xk = f"x_t{t % 2}"
            if 'x1' in P2A_PARTS:
                dma("sync", x_t[t % 2][:, :], xseq[t * 128:(t + 1) * 128, :],
                    r=[], w=[xk], tag="x2")
                ln_tile(a2[:, :], ["a2"], a2n[:, :], "a2n", t % 2)
                vop(lambda e, xa=x_t[t % 2]: e.tensor_tensor(
                    xa[:, :], a2n[:, :], xa[:, :], ALU.add), r=["a2n", xk], w=[xk])
                dma("sync", x1seq[t * 128:(t + 1) * 128, :], x_t[t % 2][:, :],
                    r=[xk], w=[f"x1s{s}_{t}"], tag="x1w")
                ln_tile(x_t[t % 2][:, :], [xk], u2 := a_tm[:, :], "a_tm", t % 2)
            if 'vvt' in P2A_PARTS:
                apo, ko = PSB(7)
                for dc in range(NDC):
                    tp(PSB128(7, dc), ko, a_tm[:, dc * 128:(dc + 1) * 128],
                       r=["a_tm"])
                dst = vv_fm[:, :].rearrange("p (c t) -> p c t", c=NDC)[
                    :, :, t * 128:(t + 1) * 128]
                src = apo.rearrange("p (c t) -> p c t", c=NDC)
                vop(lambda e, d0=dst, s0=src: e.tensor_copy(d0, s0),
                    r=[ko], w=["vv_fm"])

        if DIRECT_VV == 3 and s == 0:
            dma("sync", vv_fm[:, :], vvdbg.ap(), r=["vv_fm"], w=["vv_fm"])
        a2n_cm.__exit__(None, None, None); P.retire("a2n")
        a2_cm.__exit__(None, None, None); P.retire("a2")
        afm_cm.__exit__(None, None, None); P.retire("a_fm")
        atm_cm.__exit__(None, None, None); P.retire("a_tm")
        PT_cm.__exit__(None, None, None); P.retire("PT_sb")
        P_cm.__exit__(None, None, None); P.retire("P_sb")
        wp_cm.__exit__(None, None, None); P.retire("wproj")
        q_cm.__exit__(None, None, None); P.retire("q_fm")

        if "vv" in taps and s == 0:
            dma("sync", tap_t["tvv"].ap(), vv_fm[:, :], r=["vv_fm"], w=["tapvv"])

        apg, kg = PS(4)
        for t in range(NT):
            for dc in range(NDC):
                mm(apg[:, t * E:(t + 1) * E], kg,
                   vv_fm[:, dc * L + t * 128: dc * L + (t + 1) * 128],
                   gws[:, dc * E:(dc + 1) * E],
                   r=["vv_fm", "gws"], start=(dc == 0), stop=(dc == NDC - 1))
        for t in range(NT):
            gsl = apg[:, t * E:(t + 1) * E]
            nmax, ssum, rs = r_sc[:, 0:1], r_sc[:, 1:2], r_sc[:, 2:3]
            thr = r_sc[:, 3:4]
            m1, m2 = r_sc[:, 4:5], r_sc[:, 5:6]
            vop(lambda e, g0=gsl, n0=nmax: e.reduce_max(n0, g0, axis=AX.X,
                                                        negate=True),
                r=[kg], w=["r_sc"])
            sop(lambda e, g0=gsl, n0=nmax, s0=ssum: e.activation(
                r_orig[:, :], g0, AF.Exp, bias=n0, accum_out=s0),
                r=[kg, "r_sc"], w=["r_orig", "r_sc"])
            vop(lambda e, s0=ssum, r0=rs: e.reciprocal(r0, s0),
                r=["r_sc"], w=["r_sc"])
            vop(lambda e, r0=rs: e.tensor_scalar(r_orig[:, :], r_orig[:, :], r0,
                                                 None, ALU.mult),
                r=["r_orig", "r_sc"], w=["r_orig"])
            vop(lambda e: e.tensor_tensor(r_s[:, :], r_orig[:, :], gbb[:, :],
                                          ALU.add), r=["r_orig", "gbb"], w=["r_s"])
            first = True
            for (i, j) in ((0, 1), (0, 2), (0, 3), (1, 2), (1, 3), (2, 3)):
                a0 = r_s[:, i::4]
                b0 = r_s[:, j::4]
                if first:
                    vop(lambda e, a0=a0, b0=b0: e.tensor_tensor(
                        r_gs[:, :], a0, b0, ALU.add), r=["r_s"], w=["r_gs"])
                    first = False
                else:
                    vop(lambda e, a0=a0, b0=b0: e.tensor_tensor(
                        r_t4[:, :], a0, b0, ALU.add), r=["r_s"], w=["r_t4"])
                    vop(lambda e: e.tensor_tensor(r_gs[:, :], r_gs[:, :],
                                                  r_t4[:, :], ALU.max),
                        r=["r_gs", "r_t4"], w=["r_gs"])
            first = True
            for (i, j) in ((0, 1), (0, 2), (0, 3), (1, 2), (1, 3), (2, 3)):
                vop(lambda e, i=i, j=j: e.tensor_tensor(
                    r_t4[:, 0:1], r_gs[:, i:i + 1], r_gs[:, j:j + 1], ALU.min),
                    r=["r_gs"], w=["r_t4"])
                if first:
                    vop(lambda e: e.tensor_copy(thr, r_t4[:, 0:1]),
                        r=["r_t4"], w=["r_sc"])
                    first = False
                else:
                    vop(lambda e: e.tensor_tensor(thr, thr, r_t4[:, 0:1], ALU.max),
                        r=["r_sc", "r_t4"], w=["r_sc"])
            vop(lambda e: e.tensor_scalar(r_t4[:, :], r_gs[:, :], thr, None,
                                          ALU.is_ge), r=["r_gs", "r_sc"], w=["r_t4"])
            for i in range(4):
                vop(lambda e, i=i: e.tensor_scalar(r_ms[:, i::4], r_t4[:, :], 1.0,
                                                   1e30, ALU.subtract, op1=ALU.mult),
                    r=["r_t4"], w=["r_ms"])
            vop(lambda e: e.tensor_tensor(r_ms[:, :], r_ms[:, :], r_s[:, :],
                                          ALU.add), r=["r_ms", "r_s"], w=["r_ms"])
            vop(lambda e, m0=m1: e.reduce_max(m0, r_ms[:, :], axis=AX.X),
                r=["r_ms"], w=["r_sc"])
            vop(lambda e, m0=m1: e.tensor_scalar(r_s[:, :], r_ms[:, :], m0, -2e30,
                                                 ALU.is_ge, op1=ALU.mult),
                r=["r_ms", "r_sc"], w=["r_s"])
            vop(lambda e: e.tensor_tensor(r_s[:, :], r_s[:, :], r_ms[:, :],
                                          ALU.add), r=["r_s", "r_ms"], w=["r_s"])
            vop(lambda e, m0=m2: e.reduce_max(m0, r_s[:, :], axis=AX.X),
                r=["r_s"], w=["r_sc"])
            vop(lambda e, m0=m2: e.tensor_scalar(r_s[:, :], r_ms[:, :], m0, None,
                                                 ALU.is_ge), r=["r_ms", "r_sc"],
                w=["r_s"])
            vop(lambda e, t=t: e.tensor_tensor(comb[:, t * E:(t + 1) * E],
                                               r_orig[:, :], r_s[:, :], ALU.mult),
                r=["r_orig", "r_s"], w=["comb"])
            if "rdbg" in taps and s == 0 and t == 0:
                for i_, (tt_, kk_, n_) in enumerate([
                        (r_orig, "r_orig", E), (r_s, "r_s", E), (r_gs, "r_gs", G),
                        (r_t4, "r_t4", G), (r_ms, "r_ms", E), (gbb, "gbb", E),
                        (r_sc, "r_sc", 8)]):
                    vop(lambda e, i_=i_, tt_=tt_, n_=n_: e.tensor_copy(
                        rdbg[:, 16 * i_:16 * i_ + n_], tt_[:, 0:n_]),
                        r=[kk_], w=["rdbg"])
                dma("sync", tap_t["trd"].ap(), rdbg[:, :], r=["rdbg"], w=["taprd"])

        if "comb" in taps and s == 0:
            dma("sync", tap_t["tcomb"].ap(), comb[:, :], r=["comb"], w=["tapc"])

        v_cm.__exit__(None, None, None); P.retire("v_tm")
        k_cm.__exit__(None, None, None); P.retire("k_fm")
        ylo_cm = nc.sbuf_tensor(f"ylo{s}", [128, 8 * D], F32, side="right")
        y_lo = ylo_cm.__enter__(); P.new_buf("y_lo")
        yhi_cm = nc.sbuf_tensor(f"yhi{s}", [128, 8 * D], F32, side="right")
        y_hi = yhi_cm.__enter__(); P.new_buf("y_hi")

        def y_ap(t):
            if t < 8:
                return y_lo[:, t * D:(t + 1) * D], "y_lo"
            return y_hi[:, (t - 8) * D:(t - 8) * D + D], "y_hi"

        es_cm = [nc.sbuf_tensor(f"eslot{s}_{i}", [128, 3 * 4096], BF16, side="right")
                 for i in range(2)]
        eslot = [c.__enter__() for c in es_cm]
        for _sl in range(2):
            for _kk in ("w1", "w3", "w2"):
                P.new_buf(f"eslot{_sl}_{_kk}")
        br_cm = [nc.sbuf_tensor(f"brow{s}_{i}", [1, 2048], BF16, side="right")
                 for i in range(2)]
        brow = [c.__enter__() for c in br_cm]
        for _sl in range(2):
            for _kk in ("b1", "b3", "b2"):
                P.new_buf(f"brow{_sl}_{_kk}")
        h_cm = nc.sbuf_tensor(f"h_sb{s}", [128, 4 * 512], BF16, side="right")
        h_sb = h_cm.__enter__(); P.new_buf("h_sb")
        s1_cm = nc.sbuf_tensor(f"s1_sb{s}", [128, 512], BF16, side="right")
        s1_sb = s1_cm.__enter__(); P.new_buf("s1_sb")
        fn_cm = nc.sbuf_tensor(f"fn_sb{s}", [128, D], F32, side="right")
        fn_sb = fn_cm.__enter__(); P.new_buf("fn_sb")

        def load_expert(e_i, slot):
            sl = eslot[slot]
            k1w = f"eslot{slot}_w1"
            k3w = f"eslot{slot}_w3"
            k2w = f"eslot{slot}_w2"
            kb1 = f"brow{slot}_b1"
            kb3 = f"brow{slot}_b3"
            kb2 = f"brow{slot}_b2"
            if e_i < E:
                w1 = fm3(We1.ap()[e_i])
                w3 = fm3(We3.ap()[e_i])
                w2 = fm3(We2.ap()[e_i])
                dma("gpsimd", sl[:, 0:4096].rearrange("p (c o) -> p c o", c=8), w1,
                    r=[], w=[k1w], tag="we1")
                dma("gpsimd", sl[:, 4096:8192].rearrange("p (c o) -> p c o", c=8), w3,
                    r=[], w=[k3w], tag="we3")
                dma("gpsimd", sl[:, 8192:12288].rearrange("p (c o) -> p c o", c=4), w2,
                    r=[], w=[k2w], tag="we2")
                dma("gpsimd", brow[slot][:, 0:512],
                    rowify(be1.ap()[e_i], INTER), r=[], w=[kb1])
                dma("gpsimd", brow[slot][:, 512:1024],
                    rowify(be3.ap()[e_i], INTER), r=[], w=[kb3])
                dma("gpsimd", brow[slot][:, 1024:2048],
                    rowify(be2.ap()[e_i], D), r=[], w=[kb2])
            else:
                hh = e_i - E
                w1 = fm3(ws1.ap()[:, hh * 512:(hh + 1) * 512])
                w3 = fm3(ws3.ap()[:, hh * 512:(hh + 1) * 512])
                w2 = fm3(ws2.ap()[hh * 512:(hh + 1) * 512, :])
                dma("gpsimd", sl[:, 0:4096].rearrange("p (c o) -> p c o", c=8), w1,
                    r=[], w=[k1w], tag="ws1")
                dma("gpsimd", sl[:, 4096:8192].rearrange("p (c o) -> p c o", c=8), w3,
                    r=[], w=[k3w], tag="ws3")
                dma("gpsimd", sl[:, 8192:12288].rearrange("p (c o) -> p c o", c=4), w2,
                    r=[], w=[k2w], tag="ws2")
                dma("gpsimd", brow[slot][:, 0:512],
                    bs1.ap()[hh * 512:(hh + 1) * 512].rearrange("(a b) -> a b", a=1),
                    r=[], w=[kb1])
                dma("gpsimd", brow[slot][:, 512:1024],
                    bs3.ap()[hh * 512:(hh + 1) * 512].rearrange("(a b) -> a b", a=1),
                    r=[], w=[kb3])
                if hh == 0:
                    dma("gpsimd", brow[slot][:, 1024:2048], rowify(bs2.ap(), D),
                        r=[], w=[kb2])

        if BARRIER_P2B:
            _BARN[0] += 1
            bn = _BARN[0]
            for _e in ("tensor", "vector", "scalar"):
                P.op(_e, lambda e: e.nop(), r=[], w=[f"bar{bn}_{_e}"])
            for _e in ("tensor", "vector", "scalar"):
                P.op(_e, lambda e: e.nop(),
                     r=[f"bar{bn}_tensor", f"bar{bn}_vector", f"bar{bn}_scalar"], w=[])
        order = EXPERT_ORDER
        load_expert(order[0], 0)
        for idx in range(len(order)):
            e_i = order[idx]
            slot = idx % 2
            sl = eslot[slot]
            k1w = f"eslot{slot}_w1"
            k3w = f"eslot{slot}_w3"
            k2w = f"eslot{slot}_w2"
            kb1 = f"brow{slot}_b1"
            kb3 = f"brow{slot}_b3"
            kb2 = f"brow{slot}_b2"
            if idx + 1 < len(order):
                load_expert(order[idx + 1], 1 - slot)
            if "edbg" in taps and s == 0 and e_i == EDBG_Y:
                dma("sync", tap_t["ty0"].ap(), y_lo[:, 0:D], r=["y_lo"], w=["tapy0"])
            first = idx == 0
            shared = e_i >= E
            sh_h = e_i - E
            for g in range(NG):
                for ic in range(4):
                    b1, k1 = PS(ic % 2)
                    b3, k3 = PS(2 + ic % 2)
                    for dc in range(NDC):
                        mm(b1, k1, sl[:, dc * 512 + ic * 128: dc * 512 + (ic + 1) * 128],
                           vv_fm[:, dc * L + g * 512: dc * L + (g + 1) * 512],
                           r=[k1w, "vv_fm"], start=(dc == 0), stop=False)
                    mm(b1, k1, brow[slot][:, ic * 128:(ic + 1) * 128],
                       ones_b[:, :], r=[kb1, "ones_b"], start=False, stop=True)
                    for dc in range(NDC):
                        mm(b3, k3, sl[:, 4096 + dc * 512 + ic * 128:
                                      4096 + dc * 512 + (ic + 1) * 128],
                           vv_fm[:, dc * L + g * 512: dc * L + (g + 1) * 512],
                           r=[k3w, "vv_fm"], start=(dc == 0), stop=False)
                    mm(b3, k3, brow[slot][:, 512 + ic * 128: 512 + (ic + 1) * 128],
                       ones_b[:, :], r=[kb3, "ones_b"], start=False, stop=True)
                    sop(lambda e, a0=b1: e.activation(s1_sb[:, :], a0, AF.Silu),
                        r=[k1], w=["s1_sb"])
                    vop(lambda e, a0=b3, ic=ic: e.tensor_tensor(
                        h_sb[:, ic * 512:(ic + 1) * 512], s1_sb[:, :], a0,
                        ALU.mult), r=["s1_sb", k3], w=["h_sb"])
                if "edbg" in taps and s == 0 and e_i == EDBG_E and g == 0:
                    dma("sync", tap_t["th"].ap(), h_sb[:, :], r=["h_sb"], w=["taph"])
                    dma("sync", tap_t["tsl"].ap(), sl[:, 0:4096], r=[k1w], w=["tapsl"])
                for tt in range(4):
                    t = g * 4 + tt
                    ya, yk = y_ap(t)
                    addb = (not shared) or sh_h == 0
                    for half in range(2):
                        bo, ko = PS(4 + half)
                        for ic in range(4):
                            mm(bo, ko, h_sb[:, ic * 512 + tt * 128:
                                            ic * 512 + (tt + 1) * 128],
                               sl[:, 8192 + ic * D + half * 512:
                                  8192 + ic * D + (half + 1) * 512],
                               r=["h_sb", k2w], start=(ic == 0),
                               stop=(ic == 3 and not addb))
                        if addb:
                            mm(bo, ko, ones_b[:, 0:128],
                               brow[slot][:, 1024 + half * 512: 1024 + (half + 1) * 512],
                               r=["ones_b", kb2], start=False, stop=True)
                        dst = ya[:, half * 512:(half + 1) * 512]
                        if first and shared:
                            vop(lambda e, d0=dst, a0=bo: e.tensor_copy(d0, a0),
                                r=[ko], w=[yk])
                        elif first:
                            vop(lambda e, d0=dst, a0=bo, t=t, e_i=e_i: e.tensor_scalar(
                                d0, a0, comb[:, t * E + e_i: t * E + e_i + 1], None, ALU.mult),
                                r=[ko, "comb"], w=[yk])
                        elif shared:
                            vop(lambda e, d0=dst, a0=bo: e.tensor_tensor(
                                d0, a0, d0, ALU.add), r=[ko, yk], w=[yk])
                        else:
                            vop(lambda e, d0=dst, a0=bo, t=t, e_i=e_i:
                                e.scalar_tensor_tensor(
                                    d0, a0, comb[:, t * E + e_i: t * E + e_i + 1],
                                    d0, ALU.mult, ALU.add),
                                r=[ko, "comb", yk], w=[yk])

        if "vv2" in taps and s == 0:
            dma("sync", tap_t["tvv2"].ap(), vv_fm[:, :], r=["vv_fm"], w=["tapvv2"])
        if "y" in taps and s == 0:
            dma("sync", tap_t["ty"].ap()[:, 0:8 * D], y_lo[:, :], r=["y_lo"], w=["tapy"])
            dma("sync", tap_t["ty"].ap()[:, 8 * D:], y_hi[:, :], r=["y_hi"], w=["tapy"])

        for t in ([] if DIRECT_VV else range(NT)):
            ya, yk = y_ap(t)
            ln_tile(ya, [yk], fn_sb[:, :], "fn_sb", t % 2)
            xk = f"x_t{t % 2}"
            dma("sync", x_t[t % 2][:, :], x1seq[t * 128:(t + 1) * 128, :],
                r=[f"x1s{s}_{t}"], w=[xk], tag="x1r")
            vop(lambda e, xa=x_t[t % 2]: e.tensor_tensor(
                xa[:, :], fn_sb[:, :], xa[:, :], ALU.add),
                r=["fn_sb", xk], w=[xk])
            dma("sync", outseq[t * 128:(t + 1) * 128, :], x_t[t % 2][:, :],
                r=[xk], w=[f"outhbm{s}_{t}"], tag="outw")

        fn_cm.__exit__(None, None, None); P.retire("fn_sb")
        s1_cm.__exit__(None, None, None); P.retire("s1_sb")
        h_cm.__exit__(None, None, None); P.retire("h_sb")
        for c in reversed(br_cm):
            c.__exit__(None, None, None)
        P.retire(*[f"brow{i}_{k}" for i in range(2) for k in ("b1", "b3", "b2")])
        for c in reversed(es_cm):
            c.__exit__(None, None, None)
        P.retire(*[f"eslot{i}_{k}" for i in range(2) for k in ("w1", "w3", "w2")])
        yhi_cm.__exit__(None, None, None); P.retire("y_hi")
        ylo_cm.__exit__(None, None, None); P.retire("y_lo")
        vv_cm.__exit__(None, None, None); P.retire("vv_fm")

    with ExitStack() as stack:
        block = stack.enter_context(nc.Block())
        P.finalize(nc, block, stack)
    return nc


_HOST_CONSTS = None


def _host_consts():
    global _HOST_CONSTS
    if _HOST_CONSTS is None:
        i = np.arange(128)
        dD_ = np.maximum(i[:, None] - i[None, :], 0).astype(np.float32)
        dC_ = (i[:, None] >= i[None, :]).astype(np.float32)
        _HOST_CONSTS = {
            "identc": np.eye(128, dtype=ml_dtypes.bfloat16),
            "onesb": np.ones((1, 512), dtype=ml_dtypes.bfloat16),
            "onesf": np.ones((1, 128), dtype=np.float32),
            "dmD": dD_,
            "dmC": dC_,
        }
    return _HOST_CONSTS


def make_in_maps(inputs):
    x = np.ascontiguousarray(np.asarray(inputs["L0_attn_x"], dtype=np.float32))
    consts = _host_consts()
    common = {
        "Wqkv": np.ascontiguousarray(inputs["L0_attn_Wqkv"]),
        "bqkv": np.ascontiguousarray(inputs["L0_attn_bqkv"]),
        "Wproj": np.ascontiguousarray(inputs["L0_attn_Wproj"]),
        "bproj": np.ascontiguousarray(inputs["L0_attn_bproj"]),
        "gamma": np.asarray(inputs["L0_attn_gamma"], dtype=np.float32).reshape(1, 1),
        "gate_w": np.ascontiguousarray(inputs["L0_ffn_gate_w"]),
        "gate_b": np.ascontiguousarray(inputs["L0_ffn_gate_b"]),
        "We1": np.ascontiguousarray(inputs["L0_ffn_We1"]),
        "be1": np.ascontiguousarray(inputs["L0_ffn_be1"]),
        "We2": np.ascontiguousarray(inputs["L0_ffn_We2"]),
        "be2": np.ascontiguousarray(inputs["L0_ffn_be2"]),
        "We3": np.ascontiguousarray(inputs["L0_ffn_We3"]),
        "be3": np.ascontiguousarray(inputs["L0_ffn_be3"]),
        "ws1": np.ascontiguousarray(inputs["L0_ffn_ws1"]),
        "bs1": np.ascontiguousarray(inputs["L0_ffn_bs1"]),
        "ws2": np.ascontiguousarray(inputs["L0_ffn_ws2"]),
        "bs2": np.ascontiguousarray(inputs["L0_ffn_bs2"]),
        "ws3": np.ascontiguousarray(inputs["L0_ffn_ws3"]),
        "bs3": np.ascontiguousarray(inputs["L0_ffn_bs3"]),
    }
    common.update(consts)
    in_maps = []
    for c in range(8):
        m = dict(common)
        m["xs"] = np.ascontiguousarray(x[2 * c: 2 * c + 2])
        in_maps.append(m)
    return in_maps


def kernel(**inputs):
    nc = build_nc()
    in_maps = make_in_maps(inputs)
    res = run_bass_kernel_spmd(nc, in_maps, core_ids=list(range(8)))
    return np.concatenate([r["out"] for r in res.results], axis=0)


# version 02
# speedup vs baseline: 2.3117x; 2.3117x over previous
import numpy as np
import ml_dtypes
from contextlib import ExitStack
import concourse.bass as bass
import concourse.mybir as mybir
from concourse.bass_utils import run_bass_kernel_spmd

F32 = mybir.dt.float32
BF16 = mybir.dt.bfloat16
AF = mybir.ActivationFunctionType
ALU = mybir.AluOpType
AX = mybir.AxisListType

D = 1024
L = 2048
NSEQ = 2
NT = 16
NG = 4
NDC = 8
E = 16
G = 4
INTER = 512
NEXP = 18
INV_SQRT_D = 1.0 / 32.0
EPS = 1e-5
DRAIN_DIST = 3
EDBG_E = 0
EDBG_Y = 1
EDBG_IC = 1
BARRIER_P2B = False
DIRECT_VV = False
P2A_PARTS = {'sc','pv','proj','x1','vvt'}
_BARN = [0]
EXPERT_ORDER = list(range(18))

ENGINES = ("sync", "tensor", "vector", "scalar", "gpsimd")


class _Op:
    __slots__ = ("eng", "fn", "reads", "writes", "dma", "pos", "epos", "waits",
                 "drain", "need_inc", "semval", "clock", "tag", "qslot", "qcount")

    def __init__(self):
        self.need_inc = False
        self.semval = None


class Prog:
    def __init__(self):
        self.ops = []
        self.epos = {e: 0 for e in ENGINES}
        self.freed = []
        self.alias_events = []

    def op(self, eng, fn, r=(), w=(), dma=False, tag=""):
        o = _Op()
        o.eng = eng
        o.fn = fn
        o.reads = tuple(r)
        o.writes = tuple(w)
        o.dma = dma
        o.tag = tag
        o.pos = len(self.ops)
        o.epos = self.epos[eng]
        self.epos[eng] += 1
        self.ops.append(o)
        return o

    def retire(self, *keys):
        self.freed.extend(keys)

    def new_buf(self, key):
        self.alias_events.append((len(self.ops), key, len(self.freed)))

    def finalize(self, nc, block, stack):
        POOL = 8
        sems = {}
        for e in ENGINES:
            sems[("c", e)] = stack.enter_context(nc.semaphore(f"cs_{e}"))
        for e in ("sync", "gpsimd", "scalar", "vector"):
            for p in range(POOL):
                sems[("q", e, p)] = stack.enter_context(nc.semaphore(f"qs_{e}{p}"))

        qidx = {e: 0 for e in ENGINES}
        qslotcnt = {}
        for o in self.ops:
            if o.dma:
                i = qidx[o.eng]
                qidx[o.eng] += 1
                o.qslot = i % POOL
                kqs = (o.eng, o.qslot)
                qslotcnt[kqs] = qslotcnt.get(kqs, 0) + 1
                o.qcount = qslotcnt[kqs]
                o.semval = 16 * o.qcount

        writers = {}
        readers = {}
        clocks = {e: {} for e in ENGINES}
        ev_i = 0
        pending_alias = {}

        def skey(o):
            return ("q", o.eng, o.qslot) if o.dma else o.eng

        def covered(clk, d):
            return clk.get(skey(d), -1) >= d.epos

        for o in self.ops:
            while ev_i < len(self.alias_events) and self.alias_events[ev_i][0] <= o.pos:
                _, _k, _n = self.alias_events[ev_i]
                pending_alias[_k] = _n
                ev_i += 1
            deps = {}
            for k in o.reads:
                for d in writers.get(k, ()):
                    deps[d.pos] = d
            for k in o.writes:
                for d in writers.get(k, ()):
                    deps[d.pos] = d
                for r in readers.get(k, ()):
                    deps[r.pos] = r
            for k in o.writes + o.reads:
                n = pending_alias.pop(k, None)
                if n is not None:
                    for fk in self.freed[:n]:
                        for d in writers.get(fk, ()):
                            deps[d.pos] = d
                        for r in readers.get(fk, ()):
                            deps[r.pos] = r
            deps.pop(o.pos, None)

            o.waits = []
            o.drain = False
            myclock = dict(clocks[o.eng])
            for p in sorted(deps):
                d = deps[p]
                dk = skey(d)
                if d.eng == o.eng and not d.dma:
                    if o.eng != "tensor" and o.epos - d.epos <= DRAIN_DIST:
                        o.drain = True
                    for e2, v in d.clock.items():
                        if v > myclock.get(e2, -1):
                            myclock[e2] = v
                    continue
                if myclock.get(dk, -1) >= d.epos:
                    continue
                o.waits.append(d)
                d.need_inc = True
                for e2, v in d.clock.items():
                    if v > myclock.get(e2, -1):
                        myclock[e2] = v
                myclock[dk] = d.epos
            o.clock = dict(myclock)
            o.clock[skey(o)] = o.epos
            clocks[o.eng] = dict(myclock) if o.dma else dict(o.clock)

            for k in o.reads:
                readers.setdefault(k, []).append(o)
            for k in o.writes:
                prev = writers.get(k, [])
                writers[k] = [d for d in prev if not covered(o.clock, d)] + [o]
                readers[k] = []

        ccnt = {e: 0 for e in ENGINES}
        for o in self.ops:
            if o.dma:
                o.need_inc = True
            elif o.need_inc:
                ccnt[o.eng] += 1
                o.semval = ccnt[o.eng]

        per_eng = {e: [o for o in self.ops if o.eng == e] for e in ENGINES}
        qfinal = {(e, p): 16 * c for (e, p), c in qslotcnt.items()}

        def mk_body(ename):
            ops = per_eng[ename]

            def body(eng):
                for o in ops:
                    if o.drain:
                        eng.drain()
                    best = {}
                    if o.waits:
                        for d in o.waits:
                            kk = ("q", d.eng, d.qslot) if d.dma else ("c", d.eng)
                            if best.get(kk, -1) < d.semval:
                                best[kk] = d.semval
                    if o.dma and o.qcount > 1:
                        kk = ("q", o.eng, o.qslot)
                        if best.get(kk, -1) < 16 * (o.qcount - 1):
                            best[kk] = 16 * (o.qcount - 1)
                    for kk, v in best.items():
                        eng.wait_ge(sems[kk], v)
                    inst = o.fn(eng)
                    if o.dma:
                        inst.then_inc(sems[("q", o.eng, o.qslot)], 16)
                    elif o.need_inc:
                        inst.then_inc(sems[("c", o.eng)], 1)
                if ename == "sync":
                    for (e, p), v in qfinal.items():
                        eng.wait_ge(sems[("q", e, p)], v)
            return body

        for ename in ENGINES:
            if per_eng[ename] or ename == "sync":
                getattr(block, ename)(mk_body(ename))



def build_nc(taps=()):
    taps = set(taps)
    nc = bass.Bass("TRN2", target_bir_lowering=False)
    P = Prog()

    xs = nc.dram_tensor("xs", [NSEQ, L, D], F32, kind="ExternalInput")
    Wqkv = nc.dram_tensor("Wqkv", [D, 3 * D], F32, kind="ExternalInput")
    bqkv = nc.dram_tensor("bqkv", [3 * D], F32, kind="ExternalInput")
    Wproj = nc.dram_tensor("Wproj", [D, D], F32, kind="ExternalInput")
    bproj = nc.dram_tensor("bproj", [D], F32, kind="ExternalInput")
    gamma = nc.dram_tensor("gamma", [1, 1], F32, kind="ExternalInput")
    gate_w = nc.dram_tensor("gate_w", [E, D], F32, kind="ExternalInput")
    gate_b = nc.dram_tensor("gate_b", [E], F32, kind="ExternalInput")
    We1 = nc.dram_tensor("We1", [E, D, INTER], F32, kind="ExternalInput")
    be1 = nc.dram_tensor("be1", [E, INTER], F32, kind="ExternalInput")
    We2 = nc.dram_tensor("We2", [E, INTER, D], F32, kind="ExternalInput")
    be2 = nc.dram_tensor("be2", [E, D], F32, kind="ExternalInput")
    We3 = nc.dram_tensor("We3", [E, D, INTER], F32, kind="ExternalInput")
    be3 = nc.dram_tensor("be3", [E, INTER], F32, kind="ExternalInput")
    ws1 = nc.dram_tensor("ws1", [D, 2 * INTER], F32, kind="ExternalInput")
    bs1 = nc.dram_tensor("bs1", [2 * INTER], F32, kind="ExternalInput")
    ws2 = nc.dram_tensor("ws2", [2 * INTER, D], F32, kind="ExternalInput")
    bs2 = nc.dram_tensor("bs2", [D], F32, kind="ExternalInput")
    ws3 = nc.dram_tensor("ws3", [D, 2 * INTER], F32, kind="ExternalInput")
    bs3 = nc.dram_tensor("bs3", [2 * INTER], F32, kind="ExternalInput")
    identc = nc.dram_tensor("identc", [128, 128], BF16, kind="ExternalInput")
    onesb = nc.dram_tensor("onesb", [1, 512], BF16, kind="ExternalInput")
    onesf = nc.dram_tensor("onesf", [1, 128], F32, kind="ExternalInput")
    dmD = nc.dram_tensor("dmD", [128, 128], F32, kind="ExternalInput")
    dmC = nc.dram_tensor("dmC", [128, 128], F32, kind="ExternalInput")

    out = nc.dram_tensor("out", [NSEQ, L, D], F32, kind="ExternalOutput")
    vvdbg = nc.dram_tensor("vvdbg", [128, NDC * L], BF16, kind="ExternalInput") if DIRECT_VV else None
    x1kind = "ExternalOutput" if "x1" in taps else "Internal"
    x1s = nc.dram_tensor("x1s", [NSEQ, L, D], F32, kind=x1kind)

    tap_t = {}
    if "qkv" in taps:
        tap_t["tq"] = nc.dram_tensor("tap_q", [128, NDC * L], BF16, kind="ExternalOutput")
        tap_t["tk"] = nc.dram_tensor("tap_k", [128, NDC * L], BF16, kind="ExternalOutput")
        tap_t["tv"] = nc.dram_tensor("tap_v", [128, NT * D], BF16, kind="ExternalOutput")
    if "vv" in taps:
        tap_t["tvv"] = nc.dram_tensor("tap_vv", [128, NDC * L], BF16, kind="ExternalOutput")
    if "comb" in taps:
        tap_t["tcomb"] = nc.dram_tensor("tap_comb", [128, NT * E], F32, kind="ExternalOutput")
    if "vv2" in taps:
        tap_t["tvv2"] = nc.dram_tensor("tap_vv2", [128, NDC * L], BF16, kind="ExternalOutput")
    if "y" in taps:
        tap_t["ty"] = nc.dram_tensor("tap_y", [128, NT * D], F32, kind="ExternalOutput")
    if "rdbg" in taps:
        tap_t["trd"] = nc.dram_tensor("tap_rdbg", [128, 128], F32, kind="ExternalOutput")
    if "edbg" in taps:
        tap_t["th"] = nc.dram_tensor("tap_h", [128, 2048], BF16, kind="ExternalOutput")
        tap_t["ty0"] = nc.dram_tensor("tap_y0", [128, D], F32, kind="ExternalOutput")
        tap_t["tsl"] = nc.dram_tensor("tap_sl", [128, 4096], BF16, kind="ExternalOutput")
        tap_t["tp1"] = nc.dram_tensor("tap_p1", [128, 1024], F32, kind="ExternalOutput")

    A = nc.alloc_sbuf_tensor
    ident = A("ident", [128, 128], BF16)
    ones_b = A("ones_b", [1, 512], BF16)
    ones_f = A("ones_f", [1, 128], F32)
    dD = A("dD", [128, 128], F32)
    dC = A("dC", [128, 128], F32)
    dmask = A("dmask", [128, 128], F32)
    lg1 = A("lg1", [1, 1], F32)
    lgc = A("lgc", [128, 1], F32)
    bqkv_v = A("bqkv_v", [1, 1024], BF16)
    bproj_r = A("bproj_r", [1, 1024], BF16)
    gws = A("gws", [128, NDC * E], BF16)
    gbb = A("gbb", [128, E], F32)
    x_t = [A(f"x_t{i}", [128, D], F32) for i in range(2)]
    sqs = [A(f"sqs{i}", [128, D], F32) for i in range(2)]
    lns = [A(f"lns{i}", [128, 8], F32) for i in range(2)]
    sms = [A(f"sms{i}", [128, 4], F32) for i in range(2)]
    comb = A("comb", [128, NT * E], F32)
    r_orig = A("r_orig", [128, E], F32)
    r_s = A("r_s", [128, E], F32)
    r_gs = A("r_gs", [128, G], F32)
    r_t4 = A("r_t4", [128, G], F32)
    r_ms = A("r_ms", [128, E], F32)
    r_sc = A("r_sc", [128, 8], F32)
    rdbg = A("rdbg", [128, 128], F32) if "rdbg" in taps else None

    psS = nc.alloc_psum_tensor("psS", [128, 2048], F32)
    pbx = [nc.alloc_psum_tensor(f"pb{i}", [128, 512], F32) for i in range(2)]
    ptx = [nc.alloc_psum_tensor(f"pt{i}", [128, 1024], BF16) for i in range(2)]

    def PS(i):
        if i < 4:
            return psS[:, i * 512:(i + 1) * 512], f"bk{i}"
        return pbx[i - 4][:, :], f"bk{i}"

    def PS128(i, j):
        if i < 4:
            return psS[:, i * 512 + j * 128: i * 512 + (j + 1) * 128]
        return pbx[i - 4][:, j * 128:(j + 1) * 128]

    def PSB(i):
        return ptx[i - 6][:, :], f"bk{i}"

    def PSB128(i, j):
        return ptx[i - 6][:, j * 128:(j + 1) * 128]

    def fm3(src2d, p=128):
        return src2d.rearrange("(c p) o -> p c o", p=p)

    def sb3(t, c):
        return t[:, :].rearrange("p (c o) -> p c o", c=c)

    def dma(queue, outap, inap, r, w, tag=""):
        P.op(queue, lambda e, o=outap, i=inap: e.dma_start(o, i), r=r, w=w,
             dma=True, tag=tag)

    def dma_slow(queue, outap, inap, r, w, tag=""):
        P.op(queue, lambda e, o=outap, i=inap: e.dma_start(
            o, i, allow_slow_non_contiguous=True), r=r, w=w, dma=True, tag=tag)

    def mm(outap, bank, lhsT, rhs, r, start, stop):
        P.op("tensor", lambda e, o=outap, l=lhsT, rr=rhs, s1=start, s2=stop:
             e.matmul(o, l, rr, start=s1, stop=s2), r=r, w=[bank])

    def tp(outap, bank, inap, r):
        P.op("tensor", lambda e, o=outap, i=inap: e.transpose(o, i, ident[:, :]),
             r=list(r) + ["ident"], w=[bank])

    def vop(fn, r, w, tag=""):
        P.op("vector", fn, r=r, w=w, tag=tag)

    def sop(fn, r, w, tag=""):
        P.op("scalar", fn, r=r, w=w, tag=tag)

    def ln_tile(src_ap, src_keys, dst_ap, dst_key, slot, extra_scale=1.0):
        ls = lns[slot]
        lk = f"lns{slot}"
        sq = sqs[slot]
        sk = f"sqs{slot}"
        su, mean, msq, var, inv, nb = (ls[:, 0:1], ls[:, 1:2], ls[:, 2:3],
                                       ls[:, 3:4], ls[:, 4:5], ls[:, 5:6])
        sop(lambda e: e.activation(sq[:, :], src_ap, AF.Square, accum_out=su),
            r=src_keys, w=[sk, lk])
        vop(lambda e: e.reduce_sum(mean, src_ap, axis=AX.X), r=src_keys, w=[lk])
        vop(lambda e: e.tensor_scalar(mean, mean, 1.0 / D, None, ALU.mult),
            r=[lk], w=[lk])
        vop(lambda e: e.tensor_scalar(msq, mean, mean, -EPS, ALU.mult, op1=ALU.add),
            r=[lk], w=[lk])
        vop(lambda e: e.scalar_tensor_tensor(var, su, 1.0 / D, msq, ALU.mult,
                                             ALU.subtract), r=[lk], w=[lk])
        sop(lambda e: e.activation(var, var, AF.Ln), r=[lk], w=[lk])
        sop(lambda e: e.activation(inv, var, AF.Exp, scale=-0.5), r=[lk], w=[lk])
        if extra_scale != 1.0:
            vop(lambda e: e.tensor_scalar(inv, inv, extra_scale, None, ALU.mult),
                r=[lk], w=[lk])
        vop(lambda e: e.tensor_scalar(nb, mean, inv, -1.0, ALU.mult, op1=ALU.mult),
            r=[lk], w=[lk])
        sop(lambda e: e.activation(dst_ap, src_ap, AF.Identity, bias=nb, scale=inv),
            r=list(src_keys) + [lk], w=[dst_key])

    dma("sync", ident[:, :], identc.ap(), r=[], w=["ident"])
    dma("sync", ones_b[:, :], onesb.ap(), r=[], w=["ones_b"])
    dma("sync", ones_f[:, :], onesf.ap(), r=[], w=["ones_f"])
    dma("sync", dD[:, :], dmD.ap(), r=[], w=["dD"])
    dma("sync", dC[:, :], dmC.ap(), r=[], w=["dC"])
    rowify = lambda apx, n: apx.rearrange("(a b) -> a b", a=1)
    dma("gpsimd", bqkv_v[:, :], bqkv.ap()[2048:3072].rearrange("(a b) -> a b", a=1),
        r=[], w=["bqkv_v"])
    dma("gpsimd", bproj_r[:, :], rowify(bproj.ap(), D), r=[], w=["bproj_r"])

    dma("sync", lgc[:, :], bass.AP(gamma, 0, [[0, 128], [1, 1]]),
        r=[], w=["lgc"])
    sop(lambda e: e.activation(lgc[:, :], lgc[:, :], AF.Ln), r=["lgc"], w=["lgc"])
    sop(lambda e: e.activation(dmask[:, :], dD[:, :], AF.Exp, scale=lgc[:, :]),
        r=["dD", "lgc"], w=["dmask"])
    vop(lambda e: e.tensor_tensor(dmask[:, :], dmask[:, :], dC[:, :], ALU.mult),
        r=["dmask", "dC"], w=["dmask"])

    dma("sync", gbb[:, :], bass.AP(gate_b, 0, [[0, 128], [1, E]]),
        r=[], w=["gbb"])

    bqkv_qk = A("bqkv_qk", [1, 2048], BF16)
    dma("gpsimd", bqkv_qk[:, :], bqkv.ap()[0:2048].rearrange("(a b) -> a b", a=1),
        r=[], w=["bqkv_qk"])

    gwtmp_cm = nc.sbuf_tensor("gwtmp", [16, D], BF16, side="right")
    gwtmp = gwtmp_cm.__enter__()
    P.new_buf("gwtmp")
    dma("gpsimd", gwtmp[:, :], gate_w.ap(), r=[], w=["gwtmp"])
    for dc in range(NDC):
        apo, ko = PSB(6 + dc % 2)
        P.op("tensor", lambda e, dc=dc, apo=apo: e.transpose(
            apo[:, 0:16], gwtmp[0:16, dc * 128:(dc + 1) * 128], ident[0:16, 0:16]),
            r=["gwtmp", "ident"], w=[ko])
        vop(lambda e, dc=dc, apo=apo: e.tensor_copy(
            gws[:, dc * E:(dc + 1) * E], apo[:, 0:16]), r=[ko], w=["gws"])
    gwtmp_cm.__exit__(None, None, None)
    P.retire("gwtmp")

    for s in range([1] if DIRECT_VV else [NSEQ])[0] if False else range(1 if DIRECT_VV else NSEQ):
        xseq = xs.ap()[s]
        x1seq = x1s.ap()[s]
        outseq = out.ap()[s]

        k_cm = nc.sbuf_tensor(f"k_fm{s}", [128, NDC * L], BF16, side="left")
        k_fm = k_cm.__enter__(); P.new_buf("k_fm")
        v_cm = nc.sbuf_tensor(f"v_tm{s}", [128, NT * D], BF16, side="left")
        v_tm = v_cm.__enter__(); P.new_buf("v_tm")
        q_cm = nc.sbuf_tensor(f"q_fm{s}", [128, NDC * L], BF16, side="left")
        q_fm = q_cm.__enter__(); P.new_buf("q_fm")
        wq_cm = nc.sbuf_tensor(f"wqkv{s}", [128, NDC * 3 * D], BF16, side="right")
        wqkv_sb = wq_cm.__enter__(); P.new_buf("wqkv")
        uf_cm = nc.sbuf_tensor(f"u_fm{s}", [128, NDC * 512], BF16, side="right")
        u_fm = uf_cm.__enter__(); P.new_buf("u_fm")
        ut_cm = nc.sbuf_tensor(f"u_tm{s}", [128, 2 * D], BF16, side="right")
        u_tm = ut_cm.__enter__(); P.new_buf("u_tm")

        if DIRECT_VV == 1:
            P1_RANGE = []
        elif DIRECT_VV == 2:
            P1_RANGE = list(range(NG))
        else:
            P1_RANGE = list(range(NG))
        dma("gpsimd", sb3(wqkv_sb, NDC), fm3(Wqkv.ap()),
            r=[], w=["wqkv"], tag="wqkv")

        for g in P1_RANGE:
            for tt in range(4):
                t = g * 4 + tt
                xk = f"x_t{t % 2}"
                dma("sync", x_t[t % 2][:, :], xseq[t * 128:(t + 1) * 128, :],
                    r=[], w=[xk], tag="x")
                uk = f"u_tm{tt % 2}"
                u_ap = u_tm[:, (tt % 2) * D:(tt % 2) * D + D]
                ln_tile(x_t[t % 2][:, :], [xk], u_ap, "u_tm", tt % 2)
                bi = 6 + tt % 2
                apo, ko = PSB(bi)
                for dc in range(NDC):
                    tp(PSB128(bi, dc), ko, u_ap[:, dc * 128:(dc + 1) * 128],
                       r=["u_tm"])
                dst = u_fm[:, :].rearrange("p (c t) -> p c t", c=NDC)[
                    :, :, tt * 128:(tt + 1) * 128]
                src = apo.rearrange("p (c t) -> p c t", c=NDC)
                vop(lambda e, d0=dst, s0=src: e.tensor_copy(d0, s0),
                    r=[ko], w=["u_fm"])
            for which, base, kdst in (("q", 0, q_fm), ("k", D, k_fm)):
                for oc in range(NDC):
                    bi = 4 + oc % 2
                    apo, ko = PS(bi)
                    for dc in range(NDC):
                        mm(apo, ko,
                           wqkv_sb[:, dc * 3 * D + base + oc * 128:
                                   dc * 3 * D + base + (oc + 1) * 128],
                           u_fm[:, dc * 512:(dc + 1) * 512],
                           r=["wqkv", "u_fm"], start=(dc == 0), stop=False)
                    mm(apo, ko, bqkv_qk[:, base + oc * 128: base + (oc + 1) * 128],
                       ones_b[:, :], r=["bqkv_qk", "ones_b"], start=False, stop=True)
                    dst = kdst[:, oc * L + g * 512: oc * L + (g + 1) * 512]
                    scl = INV_SQRT_D if which == "q" else 1.0
                    sop(lambda e, d0=dst, a0=apo, s0=scl: e.activation(
                        d0, a0, AF.Identity, scale=s0),
                        r=[ko], w=["q_fm" if which == "q" else "k_fm"])
            for tt in range(4):
                t = g * 4 + tt
                for half in range(2):
                    bi = (tt * 2 + half) % 4
                    apo, ko = PS(bi)
                    for dc in range(NDC):
                        mm(apo, ko, u_fm[:, dc * 512 + tt * 128: dc * 512 + (tt + 1) * 128],
                           wqkv_sb[:, dc * 3 * D + 2 * D + half * 512:
                                   dc * 3 * D + 2 * D + (half + 1) * 512],
                           r=["wqkv", "u_fm"], start=(dc == 0), stop=False)
                    mm(apo, ko, ones_b[:, 0:128], bqkv_v[:, half * 512:(half + 1) * 512],
                       r=["ones_b", "bqkv_v"], start=False, stop=True)
                    dst = v_tm[:, t * D + half * 512: t * D + (half + 1) * 512]
                    vop(lambda e, d0=dst, a0=apo: e.tensor_copy(d0, a0),
                        r=[ko], w=["v_tm"])

        ut_cm.__exit__(None, None, None); P.retire("u_tm")
        uf_cm.__exit__(None, None, None); P.retire("u_fm")
        wq_cm.__exit__(None, None, None); P.retire("wqkv")

        if "qkv" in taps and s == 0:
            dma("sync", tap_t["tq"].ap(), q_fm[:, :], r=["q_fm"], w=["tapq"])
            dma("sync", tap_t["tk"].ap(), k_fm[:, :], r=["k_fm"], w=["tapk"])
            dma("sync", tap_t["tv"].ap(), v_tm[:, :], r=["v_tm"], w=["tapv"])

        vv_cm = nc.sbuf_tensor(f"vv_fm{s}", [128, NDC * L], BF16, side="right")
        vv_fm = vv_cm.__enter__(); P.new_buf("vv_fm")
        wp_cm = nc.sbuf_tensor(f"wproj{s}", [128, NDC * D], BF16, side="right")
        wproj_sb = wp_cm.__enter__(); P.new_buf("wproj")
        P_cm = nc.sbuf_tensor(f"P_sb{s}", [128, L], BF16, side="right")
        P_sb = P_cm.__enter__(); P.new_buf("P_sb")
        PT_cm = nc.sbuf_tensor(f"PT_sb{s}", [128, L], BF16, side="right")
        PT_sb = PT_cm.__enter__(); P.new_buf("PT_sb")
        atm_cm = nc.sbuf_tensor(f"a_tm{s}", [128, D], BF16, side="right")
        a_tm = atm_cm.__enter__(); P.new_buf("a_tm")
        afm_cm = nc.sbuf_tensor(f"a_fm{s}", [128, D], BF16, side="right")
        a_fm = afm_cm.__enter__(); P.new_buf("a_fm")
        a2_cm = nc.sbuf_tensor(f"a2{s}", [128, D], F32, side="right")
        a2 = a2_cm.__enter__(); P.new_buf("a2")
        a2n_cm = nc.sbuf_tensor(f"a2n{s}", [128, D], F32, side="right")
        a2n = a2n_cm.__enter__(); P.new_buf("a2n")

        dma("gpsimd", sb3(wproj_sb, NDC), fm3(Wproj.ap()),
            r=[], w=["wproj"], tag="wproj")

        if DIRECT_VV in (1, 2) and s == 0:
            dma("sync", vv_fm[:, :], vvdbg.ap(), r=[], w=["vv_fm"])
        P2A_RANGE = [] if DIRECT_VV in (1, 2) else list(range(NT))
        for t in P2A_RANGE:
            for tc in (range(4) if 'sc' in P2A_PARTS else []):
                apo, ko = PS(tc)
                for dc in range(NDC):
                    mm(apo, ko,
                       q_fm[:, dc * L + t * 128: dc * L + (t + 1) * 128],
                       k_fm[:, dc * L + tc * 512: dc * L + (tc + 1) * 512],
                       r=["q_fm", "k_fm"], start=(dc == 0), stop=(dc == NDC - 1))
            dtc, dj = t // 4, t % 4
            apd = PS128(dtc, dj)
            if 'sc' in P2A_PARTS:
                vop(lambda e, a0=apd: e.tensor_tensor(a0, a0, dmask[:, :], ALU.mult),
                    r=[f"bk{dtc}", "dmask"], w=[f"bk{dtc}"])
            sm = sms[t % 2]
            smk = f"sms{t % 2}"
            nmax, ssum, rs = sm[:, 0:1], sm[:, 1:2], sm[:, 2:3]
            if 'sc' in P2A_PARTS:
                vop(lambda e, n0=nmax: e.reduce_max(n0, psS[:, :], axis=AX.X,
                                                    negate=True),
                    r=["bk0", "bk1", "bk2", "bk3"], w=[smk])
                sop(lambda e, n0=nmax, s0=ssum: e.activation(
                    P_sb[:, :], psS[:, :], AF.Exp, bias=n0, accum_out=s0),
                    r=["bk0", "bk1", "bk2", "bk3", smk], w=["P_sb", smk])
                vop(lambda e, s0=ssum, r0=rs: e.reciprocal(r0, s0), r=[smk], w=[smk])
            for b in (range(2) if 'pv' in P2A_PARTS else []):
                apo, ko = PSB(6 + b)
                for j in range(8):
                    tk = b * 8 + j
                    tp(PSB128(6 + b, j), ko,
                       P_sb[:, tk * 128:(tk + 1) * 128], r=["P_sb"])
                vop(lambda e, b=b, a0=apo: e.tensor_copy(
                    PT_sb[:, b * 1024:(b + 1) * 1024], a0), r=[ko], w=["PT_sb"])
            for half in (range(2) if 'pv' in P2A_PARTS else []):
                apo, ko = PS(half)
                for tk in range(NT):
                    mm(apo, ko, PT_sb[:, tk * 128:(tk + 1) * 128],
                       v_tm[:, tk * D + half * 512: tk * D + (half + 1) * 512],
                       r=["PT_sb", "v_tm"], start=(tk == 0), stop=(tk == NT - 1))
                sop(lambda e, a0=apo, h=half, r0=rs: e.activation(
                    a_tm[:, h * 512:(h + 1) * 512], a0, AF.Identity, scale=r0),
                    r=[ko, smk], w=["a_tm"])
            if 'proj' in P2A_PARTS:
                apo, ko = PSB(6)
                for dc in range(NDC):
                    tp(PSB128(6, dc), ko, a_tm[:, dc * 128:(dc + 1) * 128],
                       r=["a_tm"])
                vop(lambda e, a0=apo: e.tensor_copy(a_fm[:, :], a0),
                    r=[ko], w=["a_fm"])
            for half in (range(2) if 'proj' in P2A_PARTS else []):
                apo, ko = PS(4 + half)
                for dc in range(NDC):
                    mm(apo, ko, a_fm[:, dc * 128:(dc + 1) * 128],
                       wproj_sb[:, dc * D + half * 512: dc * D + (half + 1) * 512],
                       r=["a_fm", "wproj"], start=(dc == 0), stop=False)
                mm(apo, ko, ones_b[:, 0:128], bproj_r[:, half * 512:(half + 1) * 512],
                   r=["ones_b", "bproj_r"], start=False, stop=True)
                vop(lambda e, a0=apo, h=half: e.tensor_copy(
                    a2[:, h * 512:(h + 1) * 512], a0), r=[ko], w=["a2"])
            xk = f"x_t{t % 2}"
            if 'x1' in P2A_PARTS:
                dma("sync", x_t[t % 2][:, :], xseq[t * 128:(t + 1) * 128, :],
                    r=[], w=[xk], tag="x2")
                ln_tile(a2[:, :], ["a2"], a2n[:, :], "a2n", t % 2)
                vop(lambda e, xa=x_t[t % 2]: e.tensor_tensor(
                    xa[:, :], a2n[:, :], xa[:, :], ALU.add), r=["a2n", xk], w=[xk])
                dma("sync", x1seq[t * 128:(t + 1) * 128, :], x_t[t % 2][:, :],
                    r=[xk], w=[f"x1s{s}_{t}"], tag="x1w")
                ln_tile(x_t[t % 2][:, :], [xk], u2 := a_tm[:, :], "a_tm", t % 2)
            if 'vvt' in P2A_PARTS:
                apo, ko = PSB(7)
                for dc in range(NDC):
                    tp(PSB128(7, dc), ko, a_tm[:, dc * 128:(dc + 1) * 128],
                       r=["a_tm"])
                dst = vv_fm[:, :].rearrange("p (c t) -> p c t", c=NDC)[
                    :, :, t * 128:(t + 1) * 128]
                src = apo.rearrange("p (c t) -> p c t", c=NDC)
                vop(lambda e, d0=dst, s0=src: e.tensor_copy(d0, s0),
                    r=[ko], w=["vv_fm"])

        if DIRECT_VV == 3 and s == 0:
            dma("sync", vv_fm[:, :], vvdbg.ap(), r=["vv_fm"], w=["vv_fm"])
        a2n_cm.__exit__(None, None, None); P.retire("a2n")
        a2_cm.__exit__(None, None, None); P.retire("a2")
        afm_cm.__exit__(None, None, None); P.retire("a_fm")
        atm_cm.__exit__(None, None, None); P.retire("a_tm")
        PT_cm.__exit__(None, None, None); P.retire("PT_sb")
        P_cm.__exit__(None, None, None); P.retire("P_sb")
        wp_cm.__exit__(None, None, None); P.retire("wproj")
        q_cm.__exit__(None, None, None); P.retire("q_fm")

        if "vv" in taps and s == 0:
            dma("sync", tap_t["tvv"].ap(), vv_fm[:, :], r=["vv_fm"], w=["tapvv"])

        apg, kg = PS(4)
        for t in range(NT):
            for dc in range(NDC):
                mm(apg[:, t * E:(t + 1) * E], kg,
                   vv_fm[:, dc * L + t * 128: dc * L + (t + 1) * 128],
                   gws[:, dc * E:(dc + 1) * E],
                   r=["vv_fm", "gws"], start=(dc == 0), stop=(dc == NDC - 1))
        for t in range(NT):
            gsl = apg[:, t * E:(t + 1) * E]
            nmax, ssum, rs = r_sc[:, 0:1], r_sc[:, 1:2], r_sc[:, 2:3]
            thr = r_sc[:, 3:4]
            m1, m2 = r_sc[:, 4:5], r_sc[:, 5:6]
            vop(lambda e, g0=gsl, n0=nmax: e.reduce_max(n0, g0, axis=AX.X,
                                                        negate=True),
                r=[kg], w=["r_sc"])
            sop(lambda e, g0=gsl, n0=nmax, s0=ssum: e.activation(
                r_orig[:, :], g0, AF.Exp, bias=n0, accum_out=s0),
                r=[kg, "r_sc"], w=["r_orig", "r_sc"])
            vop(lambda e, s0=ssum, r0=rs: e.reciprocal(r0, s0),
                r=["r_sc"], w=["r_sc"])
            vop(lambda e, r0=rs: e.tensor_scalar(r_orig[:, :], r_orig[:, :], r0,
                                                 None, ALU.mult),
                r=["r_orig", "r_sc"], w=["r_orig"])
            vop(lambda e: e.tensor_tensor(r_s[:, :], r_orig[:, :], gbb[:, :],
                                          ALU.add), r=["r_orig", "gbb"], w=["r_s"])
            first = True
            for (i, j) in ((0, 1), (0, 2), (0, 3), (1, 2), (1, 3), (2, 3)):
                a0 = r_s[:, i::4]
                b0 = r_s[:, j::4]
                if first:
                    vop(lambda e, a0=a0, b0=b0: e.tensor_tensor(
                        r_gs[:, :], a0, b0, ALU.add), r=["r_s"], w=["r_gs"])
                    first = False
                else:
                    vop(lambda e, a0=a0, b0=b0: e.tensor_tensor(
                        r_t4[:, :], a0, b0, ALU.add), r=["r_s"], w=["r_t4"])
                    vop(lambda e: e.tensor_tensor(r_gs[:, :], r_gs[:, :],
                                                  r_t4[:, :], ALU.max),
                        r=["r_gs", "r_t4"], w=["r_gs"])
            first = True
            for (i, j) in ((0, 1), (0, 2), (0, 3), (1, 2), (1, 3), (2, 3)):
                vop(lambda e, i=i, j=j: e.tensor_tensor(
                    r_t4[:, 0:1], r_gs[:, i:i + 1], r_gs[:, j:j + 1], ALU.min),
                    r=["r_gs"], w=["r_t4"])
                if first:
                    vop(lambda e: e.tensor_copy(thr, r_t4[:, 0:1]),
                        r=["r_t4"], w=["r_sc"])
                    first = False
                else:
                    vop(lambda e: e.tensor_tensor(thr, thr, r_t4[:, 0:1], ALU.max),
                        r=["r_sc", "r_t4"], w=["r_sc"])
            vop(lambda e: e.tensor_scalar(r_t4[:, :], r_gs[:, :], thr, None,
                                          ALU.is_ge), r=["r_gs", "r_sc"], w=["r_t4"])
            for i in range(4):
                vop(lambda e, i=i: e.tensor_scalar(r_ms[:, i::4], r_t4[:, :], 1.0,
                                                   1e30, ALU.subtract, op1=ALU.mult),
                    r=["r_t4"], w=["r_ms"])
            vop(lambda e: e.tensor_tensor(r_ms[:, :], r_ms[:, :], r_s[:, :],
                                          ALU.add), r=["r_ms", "r_s"], w=["r_ms"])
            vop(lambda e, m0=m1: e.reduce_max(m0, r_ms[:, :], axis=AX.X),
                r=["r_ms"], w=["r_sc"])
            vop(lambda e, m0=m1: e.tensor_scalar(r_s[:, :], r_ms[:, :], m0, -2e30,
                                                 ALU.is_ge, op1=ALU.mult),
                r=["r_ms", "r_sc"], w=["r_s"])
            vop(lambda e: e.tensor_tensor(r_s[:, :], r_s[:, :], r_ms[:, :],
                                          ALU.add), r=["r_s", "r_ms"], w=["r_s"])
            vop(lambda e, m0=m2: e.reduce_max(m0, r_s[:, :], axis=AX.X),
                r=["r_s"], w=["r_sc"])
            vop(lambda e, m0=m2: e.tensor_scalar(r_s[:, :], r_ms[:, :], m0, None,
                                                 ALU.is_ge), r=["r_ms", "r_sc"],
                w=["r_s"])
            vop(lambda e, t=t: e.tensor_tensor(comb[:, t * E:(t + 1) * E],
                                               r_orig[:, :], r_s[:, :], ALU.mult),
                r=["r_orig", "r_s"], w=["comb"])
            if "rdbg" in taps and s == 0 and t == 0:
                for i_, (tt_, kk_, n_) in enumerate([
                        (r_orig, "r_orig", E), (r_s, "r_s", E), (r_gs, "r_gs", G),
                        (r_t4, "r_t4", G), (r_ms, "r_ms", E), (gbb, "gbb", E),
                        (r_sc, "r_sc", 8)]):
                    vop(lambda e, i_=i_, tt_=tt_, n_=n_: e.tensor_copy(
                        rdbg[:, 16 * i_:16 * i_ + n_], tt_[:, 0:n_]),
                        r=[kk_], w=["rdbg"])
                dma("sync", tap_t["trd"].ap(), rdbg[:, :], r=["rdbg"], w=["taprd"])

        if "comb" in taps and s == 0:
            dma("sync", tap_t["tcomb"].ap(), comb[:, :], r=["comb"], w=["tapc"])

        v_cm.__exit__(None, None, None); P.retire("v_tm")
        k_cm.__exit__(None, None, None); P.retire("k_fm")
        ylo_cm = nc.sbuf_tensor(f"ylo{s}", [128, 8 * D], F32, side="right")
        y_lo = ylo_cm.__enter__(); P.new_buf("y_lo")
        yhi_cm = nc.sbuf_tensor(f"yhi{s}", [128, 8 * D], F32, side="right")
        y_hi = yhi_cm.__enter__(); P.new_buf("y_hi")

        def y_ap(t):
            if t < 8:
                return y_lo[:, t * D:(t + 1) * D], "y_lo"
            return y_hi[:, (t - 8) * D:(t - 8) * D + D], "y_hi"

        es_cm = [nc.sbuf_tensor(f"eslot{s}_{i}", [128, 3 * 4096], BF16, side="right")
                 for i in range(2)]
        eslot = [c.__enter__() for c in es_cm]
        for _sl in range(2):
            for _kk in ("w1", "w3", "w2"):
                P.new_buf(f"eslot{_sl}_{_kk}")
        br_cm = [nc.sbuf_tensor(f"brow{s}_{i}", [1, 2048], BF16, side="right")
                 for i in range(2)]
        brow = [c.__enter__() for c in br_cm]
        for _sl in range(2):
            for _kk in ("b1", "b3", "b2"):
                P.new_buf(f"brow{_sl}_{_kk}")
        h_cm = nc.sbuf_tensor(f"h_sb{s}", [128, 4 * 512], BF16, side="right")
        h_sb = h_cm.__enter__(); P.new_buf("h_sb")
        s1_cm = nc.sbuf_tensor(f"s1_sb{s}", [128, 512], BF16, side="right")
        s1_sb = s1_cm.__enter__(); P.new_buf("s1_sb")
        fn_cm = nc.sbuf_tensor(f"fn_sb{s}", [128, D], F32, side="right")
        fn_sb = fn_cm.__enter__(); P.new_buf("fn_sb")

        def load_expert(e_i, slot):
            sl = eslot[slot]
            k1w = f"eslot{slot}_w1"
            k3w = f"eslot{slot}_w3"
            k2w = f"eslot{slot}_w2"
            kb1 = f"brow{slot}_b1"
            kb3 = f"brow{slot}_b3"
            kb2 = f"brow{slot}_b2"
            if e_i < E:
                w1 = fm3(We1.ap()[e_i])
                w3 = fm3(We3.ap()[e_i])
                w2 = fm3(We2.ap()[e_i])
                dma("gpsimd", sl[:, 0:4096].rearrange("p (c o) -> p c o", c=8), w1,
                    r=[], w=[k1w], tag="we1")
                dma("gpsimd", sl[:, 4096:8192].rearrange("p (c o) -> p c o", c=8), w3,
                    r=[], w=[k3w], tag="we3")
                dma("gpsimd", sl[:, 8192:12288].rearrange("p (c o) -> p c o", c=4), w2,
                    r=[], w=[k2w], tag="we2")
                dma("gpsimd", brow[slot][:, 0:512],
                    rowify(be1.ap()[e_i], INTER), r=[], w=[kb1])
                dma("gpsimd", brow[slot][:, 512:1024],
                    rowify(be3.ap()[e_i], INTER), r=[], w=[kb3])
                dma("gpsimd", brow[slot][:, 1024:2048],
                    rowify(be2.ap()[e_i], D), r=[], w=[kb2])
            else:
                hh = e_i - E
                w1 = fm3(ws1.ap()[:, hh * 512:(hh + 1) * 512])
                w3 = fm3(ws3.ap()[:, hh * 512:(hh + 1) * 512])
                w2 = fm3(ws2.ap()[hh * 512:(hh + 1) * 512, :])
                dma("gpsimd", sl[:, 0:4096].rearrange("p (c o) -> p c o", c=8), w1,
                    r=[], w=[k1w], tag="ws1")
                dma("gpsimd", sl[:, 4096:8192].rearrange("p (c o) -> p c o", c=8), w3,
                    r=[], w=[k3w], tag="ws3")
                dma("gpsimd", sl[:, 8192:12288].rearrange("p (c o) -> p c o", c=4), w2,
                    r=[], w=[k2w], tag="ws2")
                dma("gpsimd", brow[slot][:, 0:512],
                    bs1.ap()[hh * 512:(hh + 1) * 512].rearrange("(a b) -> a b", a=1),
                    r=[], w=[kb1])
                dma("gpsimd", brow[slot][:, 512:1024],
                    bs3.ap()[hh * 512:(hh + 1) * 512].rearrange("(a b) -> a b", a=1),
                    r=[], w=[kb3])
                if hh == 0:
                    dma("gpsimd", brow[slot][:, 1024:2048], rowify(bs2.ap(), D),
                        r=[], w=[kb2])

        if BARRIER_P2B:
            _BARN[0] += 1
            bn = _BARN[0]
            for _e in ("tensor", "vector", "scalar"):
                P.op(_e, lambda e: e.nop(), r=[], w=[f"bar{bn}_{_e}"])
            for _e in ("tensor", "vector", "scalar"):
                P.op(_e, lambda e: e.nop(),
                     r=[f"bar{bn}_tensor", f"bar{bn}_vector", f"bar{bn}_scalar"], w=[])
        order = EXPERT_ORDER
        load_expert(order[0], 0)
        for idx in range(len(order)):
            e_i = order[idx]
            slot = idx % 2
            sl = eslot[slot]
            k1w = f"eslot{slot}_w1"
            k3w = f"eslot{slot}_w3"
            k2w = f"eslot{slot}_w2"
            kb1 = f"brow{slot}_b1"
            kb3 = f"brow{slot}_b3"
            kb2 = f"brow{slot}_b2"
            if idx + 1 < len(order):
                load_expert(order[idx + 1], 1 - slot)
            if "edbg" in taps and s == 0 and e_i == EDBG_Y:
                dma("sync", tap_t["ty0"].ap(), y_lo[:, 0:D], r=["y_lo"], w=["tapy0"])
            first = idx == 0
            shared = e_i >= E
            sh_h = e_i - E
            for g in range(NG):
                for ic in range(4):
                    b1, k1 = PS(ic % 2)
                    b3, k3 = PS(2 + ic % 2)
                    for dc in range(NDC):
                        mm(b1, k1, sl[:, dc * 512 + ic * 128: dc * 512 + (ic + 1) * 128],
                           vv_fm[:, dc * L + g * 512: dc * L + (g + 1) * 512],
                           r=[k1w, "vv_fm"], start=(dc == 0), stop=False)
                    mm(b1, k1, brow[slot][:, ic * 128:(ic + 1) * 128],
                       ones_b[:, :], r=[kb1, "ones_b"], start=False, stop=True)
                    for dc in range(NDC):
                        mm(b3, k3, sl[:, 4096 + dc * 512 + ic * 128:
                                      4096 + dc * 512 + (ic + 1) * 128],
                           vv_fm[:, dc * L + g * 512: dc * L + (g + 1) * 512],
                           r=[k3w, "vv_fm"], start=(dc == 0), stop=False)
                    mm(b3, k3, brow[slot][:, 512 + ic * 128: 512 + (ic + 1) * 128],
                       ones_b[:, :], r=[kb3, "ones_b"], start=False, stop=True)
                    sop(lambda e, a0=b1: e.activation(s1_sb[:, :], a0, AF.Silu),
                        r=[k1], w=["s1_sb"])
                    vop(lambda e, a0=b3, ic=ic: e.tensor_tensor(
                        h_sb[:, ic * 512:(ic + 1) * 512], s1_sb[:, :], a0,
                        ALU.mult), r=["s1_sb", k3], w=["h_sb"])
                if "edbg" in taps and s == 0 and e_i == EDBG_E and g == 0:
                    dma("sync", tap_t["th"].ap(), h_sb[:, :], r=["h_sb"], w=["taph"])
                    dma("sync", tap_t["tsl"].ap(), sl[:, 0:4096], r=[k1w], w=["tapsl"])
                for tt in range(4):
                    t = g * 4 + tt
                    ya, yk = y_ap(t)
                    addb = (not shared) or sh_h == 0
                    for half in range(2):
                        bo, ko = PS(4 + half)
                        for ic in range(4):
                            mm(bo, ko, h_sb[:, ic * 512 + tt * 128:
                                            ic * 512 + (tt + 1) * 128],
                               sl[:, 8192 + ic * D + half * 512:
                                  8192 + ic * D + (half + 1) * 512],
                               r=["h_sb", k2w], start=(ic == 0),
                               stop=(ic == 3 and not addb))
                        if addb:
                            mm(bo, ko, ones_b[:, 0:128],
                               brow[slot][:, 1024 + half * 512: 1024 + (half + 1) * 512],
                               r=["ones_b", kb2], start=False, stop=True)
                        dst = ya[:, half * 512:(half + 1) * 512]
                        if first and shared:
                            vop(lambda e, d0=dst, a0=bo: e.tensor_copy(d0, a0),
                                r=[ko], w=[yk])
                        elif first:
                            vop(lambda e, d0=dst, a0=bo, t=t, e_i=e_i: e.tensor_scalar(
                                d0, a0, comb[:, t * E + e_i: t * E + e_i + 1], None, ALU.mult),
                                r=[ko, "comb"], w=[yk])
                        elif shared:
                            vop(lambda e, d0=dst, a0=bo: e.tensor_tensor(
                                d0, a0, d0, ALU.add), r=[ko, yk], w=[yk])
                        else:
                            vop(lambda e, d0=dst, a0=bo, t=t, e_i=e_i:
                                e.scalar_tensor_tensor(
                                    d0, a0, comb[:, t * E + e_i: t * E + e_i + 1],
                                    d0, ALU.mult, ALU.add),
                                r=[ko, "comb", yk], w=[yk])

        if "vv2" in taps and s == 0:
            dma("sync", tap_t["tvv2"].ap(), vv_fm[:, :], r=["vv_fm"], w=["tapvv2"])
        if "y" in taps and s == 0:
            dma("sync", tap_t["ty"].ap()[:, 0:8 * D], y_lo[:, :], r=["y_lo"], w=["tapy"])
            dma("sync", tap_t["ty"].ap()[:, 8 * D:], y_hi[:, :], r=["y_hi"], w=["tapy"])

        for t in ([] if DIRECT_VV else range(NT)):
            ya, yk = y_ap(t)
            ln_tile(ya, [yk], fn_sb[:, :], "fn_sb", t % 2)
            xk = f"x_t{t % 2}"
            dma("sync", x_t[t % 2][:, :], x1seq[t * 128:(t + 1) * 128, :],
                r=[f"x1s{s}_{t}"], w=[xk], tag="x1r")
            vop(lambda e, xa=x_t[t % 2]: e.tensor_tensor(
                xa[:, :], fn_sb[:, :], xa[:, :], ALU.add),
                r=["fn_sb", xk], w=[xk])
            dma("sync", outseq[t * 128:(t + 1) * 128, :], x_t[t % 2][:, :],
                r=[xk], w=[f"outhbm{s}_{t}"], tag="outw")

        fn_cm.__exit__(None, None, None); P.retire("fn_sb")
        s1_cm.__exit__(None, None, None); P.retire("s1_sb")
        h_cm.__exit__(None, None, None); P.retire("h_sb")
        for c in reversed(br_cm):
            c.__exit__(None, None, None)
        P.retire(*[f"brow{i}_{k}" for i in range(2) for k in ("b1", "b3", "b2")])
        for c in reversed(es_cm):
            c.__exit__(None, None, None)
        P.retire(*[f"eslot{i}_{k}" for i in range(2) for k in ("w1", "w3", "w2")])
        yhi_cm.__exit__(None, None, None); P.retire("y_hi")
        ylo_cm.__exit__(None, None, None); P.retire("y_lo")
        vv_cm.__exit__(None, None, None); P.retire("vv_fm")

    with ExitStack() as stack:
        block = stack.enter_context(nc.Block())
        P.finalize(nc, block, stack)
    return nc


_HOST_CONSTS = None


def _host_consts():
    global _HOST_CONSTS
    if _HOST_CONSTS is None:
        i = np.arange(128)
        dD_ = np.maximum(i[:, None] - i[None, :], 0).astype(np.float32)
        dC_ = (i[:, None] >= i[None, :]).astype(np.float32)
        _HOST_CONSTS = {
            "identc": np.eye(128, dtype=ml_dtypes.bfloat16),
            "onesb": np.ones((1, 512), dtype=ml_dtypes.bfloat16),
            "onesf": np.ones((1, 128), dtype=np.float32),
            "dmD": dD_,
            "dmC": dC_,
        }
    return _HOST_CONSTS


def make_in_maps(inputs):
    x = np.ascontiguousarray(np.asarray(inputs["L0_attn_x"], dtype=np.float32))
    consts = _host_consts()
    common = {
        "Wqkv": np.ascontiguousarray(inputs["L0_attn_Wqkv"]),
        "bqkv": np.ascontiguousarray(inputs["L0_attn_bqkv"]),
        "Wproj": np.ascontiguousarray(inputs["L0_attn_Wproj"]),
        "bproj": np.ascontiguousarray(inputs["L0_attn_bproj"]),
        "gamma": np.asarray(inputs["L0_attn_gamma"], dtype=np.float32).reshape(1, 1),
        "gate_w": np.ascontiguousarray(inputs["L0_ffn_gate_w"]),
        "gate_b": np.ascontiguousarray(inputs["L0_ffn_gate_b"]),
        "We1": np.ascontiguousarray(inputs["L0_ffn_We1"]),
        "be1": np.ascontiguousarray(inputs["L0_ffn_be1"]),
        "We2": np.ascontiguousarray(inputs["L0_ffn_We2"]),
        "be2": np.ascontiguousarray(inputs["L0_ffn_be2"]),
        "We3": np.ascontiguousarray(inputs["L0_ffn_We3"]),
        "be3": np.ascontiguousarray(inputs["L0_ffn_be3"]),
        "ws1": np.ascontiguousarray(inputs["L0_ffn_ws1"]),
        "bs1": np.ascontiguousarray(inputs["L0_ffn_bs1"]),
        "ws2": np.ascontiguousarray(inputs["L0_ffn_ws2"]),
        "bs2": np.ascontiguousarray(inputs["L0_ffn_bs2"]),
        "ws3": np.ascontiguousarray(inputs["L0_ffn_ws3"]),
        "bs3": np.ascontiguousarray(inputs["L0_ffn_bs3"]),
    }
    common.update(consts)
    in_maps = []
    for c in range(8):
        m = dict(common)
        m["xs"] = np.ascontiguousarray(x[2 * c: 2 * c + 2])
        in_maps.append(m)
    return in_maps


def kernel(**inputs):
    nc = build_nc()
    in_maps = make_in_maps(inputs)
    res = run_bass_kernel_spmd(nc, in_maps, core_ids=list(range(8)))
    return np.concatenate([r["out"] for r in res.results], axis=0)


# version 04
# speedup vs baseline: 2.3329x; 1.0092x over previous
import numpy as np
import ml_dtypes
from contextlib import ExitStack
import concourse.bass as bass
import concourse.mybir as mybir
from concourse.bass_utils import run_bass_kernel_spmd

F32 = mybir.dt.float32
BF16 = mybir.dt.bfloat16
AF = mybir.ActivationFunctionType
ALU = mybir.AluOpType
AX = mybir.AxisListType

D = 1024
L = 2048
NSEQ = 2
NT = 16
NG = 4
NDC = 8
E = 16
G = 4
INTER = 512
NEXP = 18
INV_SQRT_D = 1.0 / 32.0
EPS = 1e-5
DRAIN_DIST = 3
EDBG_E = 0
EDBG_Y = 1
EDBG_IC = 1
BARRIER_P2B = False
DIRECT_VV = False
P2A_PARTS = {'sc','pv','proj','x1','vvt'}
_BARN = [0]
EXPERT_ORDER = list(range(18))

ENGINES = ("sync", "tensor", "vector", "scalar", "gpsimd")


class _Op:
    __slots__ = ("eng", "fn", "reads", "writes", "dma", "pos", "epos", "waits",
                 "drain", "need_inc", "semval", "clock", "tag", "qslot", "qcount")

    def __init__(self):
        self.need_inc = False
        self.semval = None


class Prog:
    def __init__(self):
        self.ops = []
        self.epos = {e: 0 for e in ENGINES}
        self.freed = []
        self.alias_events = []

    def op(self, eng, fn, r=(), w=(), dma=False, tag=""):
        o = _Op()
        o.eng = eng
        o.fn = fn
        o.reads = tuple(r)
        o.writes = tuple(w)
        o.dma = dma
        o.tag = tag
        o.pos = len(self.ops)
        o.epos = self.epos[eng]
        self.epos[eng] += 1
        self.ops.append(o)
        return o

    def retire(self, *keys):
        self.freed.extend(keys)

    def new_buf(self, key):
        self.alias_events.append((len(self.ops), key, len(self.freed)))

    def finalize(self, nc, block, stack):
        POOL = 8
        sems = {}
        for e in ENGINES:
            sems[("c", e)] = stack.enter_context(nc.semaphore(f"cs_{e}"))
        for e in ("sync", "gpsimd", "scalar", "vector"):
            for p in range(POOL):
                sems[("q", e, p)] = stack.enter_context(nc.semaphore(f"qs_{e}{p}"))

        qidx = {e: 0 for e in ENGINES}
        qslotcnt = {}
        for o in self.ops:
            if o.dma:
                i = qidx[o.eng]
                qidx[o.eng] += 1
                o.qslot = i % POOL
                kqs = (o.eng, o.qslot)
                qslotcnt[kqs] = qslotcnt.get(kqs, 0) + 1
                o.qcount = qslotcnt[kqs]
                o.semval = 16 * o.qcount

        writers = {}
        readers = {}
        clocks = {e: {} for e in ENGINES}
        ev_i = 0
        pending_alias = {}

        def skey(o):
            return ("q", o.eng, o.qslot) if o.dma else o.eng

        def covered(clk, d):
            return clk.get(skey(d), -1) >= d.epos

        for o in self.ops:
            while ev_i < len(self.alias_events) and self.alias_events[ev_i][0] <= o.pos:
                _, _k, _n = self.alias_events[ev_i]
                pending_alias[_k] = _n
                ev_i += 1
            deps = {}
            for k in o.reads:
                for d in writers.get(k, ()):
                    deps[d.pos] = d
            for k in o.writes:
                for d in writers.get(k, ()):
                    deps[d.pos] = d
                for r in readers.get(k, ()):
                    deps[r.pos] = r
            for k in o.writes + o.reads:
                n = pending_alias.pop(k, None)
                if n is not None:
                    for fk in self.freed[:n]:
                        for d in writers.get(fk, ()):
                            deps[d.pos] = d
                        for r in readers.get(fk, ()):
                            deps[r.pos] = r
            deps.pop(o.pos, None)

            o.waits = []
            o.drain = False
            myclock = dict(clocks[o.eng])
            for p in sorted(deps):
                d = deps[p]
                dk = skey(d)
                if d.eng == o.eng and not d.dma:
                    if o.eng != "tensor" and o.epos - d.epos <= DRAIN_DIST:
                        o.drain = True
                    for e2, v in d.clock.items():
                        if v > myclock.get(e2, -1):
                            myclock[e2] = v
                    continue
                if myclock.get(dk, -1) >= d.epos:
                    continue
                o.waits.append(d)
                d.need_inc = True
                for e2, v in d.clock.items():
                    if v > myclock.get(e2, -1):
                        myclock[e2] = v
                myclock[dk] = d.epos
            o.clock = dict(myclock)
            o.clock[skey(o)] = o.epos
            clocks[o.eng] = dict(myclock) if o.dma else dict(o.clock)

            for k in o.reads:
                readers.setdefault(k, []).append(o)
            for k in o.writes:
                prev = writers.get(k, [])
                writers[k] = [d for d in prev if not covered(o.clock, d)] + [o]
                readers[k] = []

        ccnt = {e: 0 for e in ENGINES}
        for o in self.ops:
            if o.dma:
                o.need_inc = True
            elif o.need_inc:
                ccnt[o.eng] += 1
                o.semval = ccnt[o.eng]

        per_eng = {e: [o for o in self.ops if o.eng == e] for e in ENGINES}
        qfinal = {(e, p): 16 * c for (e, p), c in qslotcnt.items()}

        def mk_body(ename):
            ops = per_eng[ename]

            def body(eng):
                for o in ops:
                    if o.drain:
                        eng.drain()
                    best = {}
                    if o.waits:
                        for d in o.waits:
                            kk = ("q", d.eng, d.qslot) if d.dma else ("c", d.eng)
                            if best.get(kk, -1) < d.semval:
                                best[kk] = d.semval
                    if o.dma and o.qcount > 1:
                        kk = ("q", o.eng, o.qslot)
                        if best.get(kk, -1) < 16 * (o.qcount - 1):
                            best[kk] = 16 * (o.qcount - 1)
                    for kk, v in best.items():
                        eng.wait_ge(sems[kk], v)
                    inst = o.fn(eng)
                    if o.dma:
                        inst.then_inc(sems[("q", o.eng, o.qslot)], 16)
                    elif o.need_inc:
                        inst.then_inc(sems[("c", o.eng)], 1)
                if ename == "sync":
                    for (e, p), v in qfinal.items():
                        eng.wait_ge(sems[("q", e, p)], v)
            return body

        for ename in ENGINES:
            if per_eng[ename] or ename == "sync":
                getattr(block, ename)(mk_body(ename))



def build_nc(taps=()):
    taps = set(taps)
    nc = bass.Bass("TRN2", target_bir_lowering=False)
    P = Prog()

    xs = nc.dram_tensor("xs", [NSEQ, L, D], F32, kind="ExternalInput")
    Wqkv = nc.dram_tensor("Wqkv", [D, 3 * D], F32, kind="ExternalInput")
    bqkv = nc.dram_tensor("bqkv", [3 * D], F32, kind="ExternalInput")
    Wproj = nc.dram_tensor("Wproj", [D, D], F32, kind="ExternalInput")
    bproj = nc.dram_tensor("bproj", [D], F32, kind="ExternalInput")
    gamma = nc.dram_tensor("gamma", [1, 1], F32, kind="ExternalInput")
    gate_w = nc.dram_tensor("gate_w", [E, D], F32, kind="ExternalInput")
    gate_b = nc.dram_tensor("gate_b", [E], F32, kind="ExternalInput")
    We1 = nc.dram_tensor("We1", [E, D, INTER], F32, kind="ExternalInput")
    be1 = nc.dram_tensor("be1", [E, INTER], F32, kind="ExternalInput")
    We2 = nc.dram_tensor("We2", [E, INTER, D], F32, kind="ExternalInput")
    be2 = nc.dram_tensor("be2", [E, D], F32, kind="ExternalInput")
    We3 = nc.dram_tensor("We3", [E, D, INTER], F32, kind="ExternalInput")
    be3 = nc.dram_tensor("be3", [E, INTER], F32, kind="ExternalInput")
    ws1 = nc.dram_tensor("ws1", [D, 2 * INTER], F32, kind="ExternalInput")
    bs1 = nc.dram_tensor("bs1", [2 * INTER], F32, kind="ExternalInput")
    ws2 = nc.dram_tensor("ws2", [2 * INTER, D], F32, kind="ExternalInput")
    bs2 = nc.dram_tensor("bs2", [D], F32, kind="ExternalInput")
    ws3 = nc.dram_tensor("ws3", [D, 2 * INTER], F32, kind="ExternalInput")
    bs3 = nc.dram_tensor("bs3", [2 * INTER], F32, kind="ExternalInput")
    identc = nc.dram_tensor("identc", [128, 128], BF16, kind="ExternalInput")
    onesb = nc.dram_tensor("onesb", [1, 512], BF16, kind="ExternalInput")
    onesf = nc.dram_tensor("onesf", [1, 128], F32, kind="ExternalInput")
    dmD = nc.dram_tensor("dmD", [128, 128], F32, kind="ExternalInput")
    dmC = nc.dram_tensor("dmC", [128, 128], F32, kind="ExternalInput")

    out = nc.dram_tensor("out", [NSEQ, L, D], F32, kind="ExternalOutput")
    vvdbg = nc.dram_tensor("vvdbg", [128, NDC * L], BF16, kind="ExternalInput") if DIRECT_VV else None
    x1kind = "ExternalOutput" if "x1" in taps else "Internal"
    x1s = nc.dram_tensor("x1s", [NSEQ, L, D], F32, kind=x1kind)

    tap_t = {}
    if "qkv" in taps:
        tap_t["tq"] = nc.dram_tensor("tap_q", [128, NDC * L], BF16, kind="ExternalOutput")
        tap_t["tk"] = nc.dram_tensor("tap_k", [128, NDC * L], BF16, kind="ExternalOutput")
        tap_t["tv"] = nc.dram_tensor("tap_v", [128, NT * D], BF16, kind="ExternalOutput")
    if "vv" in taps:
        tap_t["tvv"] = nc.dram_tensor("tap_vv", [128, NDC * L], BF16, kind="ExternalOutput")
    if "comb" in taps:
        tap_t["tcomb"] = nc.dram_tensor("tap_comb", [128, NT * E], F32, kind="ExternalOutput")
    if "vv2" in taps:
        tap_t["tvv2"] = nc.dram_tensor("tap_vv2", [128, NDC * L], BF16, kind="ExternalOutput")
    if "y" in taps:
        tap_t["ty"] = nc.dram_tensor("tap_y", [128, NT * D], F32, kind="ExternalOutput")
    if "rdbg" in taps:
        tap_t["trd"] = nc.dram_tensor("tap_rdbg", [128, 128], F32, kind="ExternalOutput")
    if "edbg" in taps:
        tap_t["th"] = nc.dram_tensor("tap_h", [128, 2048], BF16, kind="ExternalOutput")
        tap_t["ty0"] = nc.dram_tensor("tap_y0", [128, D], F32, kind="ExternalOutput")
        tap_t["tsl"] = nc.dram_tensor("tap_sl", [128, 4096], BF16, kind="ExternalOutput")
        tap_t["tp1"] = nc.dram_tensor("tap_p1", [128, 1024], F32, kind="ExternalOutput")

    A = nc.alloc_sbuf_tensor
    ident = A("ident", [128, 128], BF16)
    ones_b = A("ones_b", [1, 512], BF16)
    ones_f = A("ones_f", [1, 128], F32)
    dD = A("dD", [128, 128], F32)
    dC = A("dC", [128, 128], F32)
    dmask = A("dmask", [128, 128], F32)
    lg1 = A("lg1", [1, 1], F32)
    lgc = A("lgc", [128, 1], F32)
    bqkv_v = A("bqkv_v", [1, 1024], BF16)
    bproj_r = A("bproj_r", [1, 1024], BF16)
    gws = A("gws", [128, NDC * E], BF16)
    gbb = A("gbb", [128, E], F32)
    x_t = [A(f"x_t{i}", [128, D], F32) for i in range(2)]
    fin_x = [A(f"fin_x{i}", [128, D], F32) for i in range(2)]
    sqs = [A("sqs0", [128, D], F32)] * 2
    lns = [A(f"lns{i}", [128, 8], F32) for i in range(2)]
    sms = [A(f"sms{i}", [128, 4], F32) for i in range(2)]
    comb = A("comb", [128, NT * E], F32)
    r_orig = A("r_orig", [128, E], F32)
    r_s = A("r_s", [128, E], F32)
    r_gs = A("r_gs", [128, G], F32)
    r_t4 = A("r_t4", [128, G], F32)
    r_ms = A("r_ms", [128, E], F32)
    r_sc = A("r_sc", [128, 8], F32)
    rdbg = A("rdbg", [128, 128], F32) if "rdbg" in taps else None

    psS = nc.alloc_psum_tensor("psS", [128, 2048], F32)
    pbx = [nc.alloc_psum_tensor(f"pb{i}", [128, 512], F32) for i in range(2)]
    ptx = [nc.alloc_psum_tensor(f"pt{i}", [128, 1024], BF16) for i in range(2)]

    def PS(i):
        if i < 4:
            return psS[:, i * 512:(i + 1) * 512], f"bk{i}"
        return pbx[i - 4][:, :], f"bk{i}"

    def PS128(i, j):
        if i < 4:
            return psS[:, i * 512 + j * 128: i * 512 + (j + 1) * 128]
        return pbx[i - 4][:, j * 128:(j + 1) * 128]

    def PSB(i):
        return ptx[i - 6][:, :], f"bk{i}"

    def PSB128(i, j):
        return ptx[i - 6][:, j * 128:(j + 1) * 128]

    def fm3(src2d, p=128):
        return src2d.rearrange("(c p) o -> p c o", p=p)

    def sb3(t, c):
        return t[:, :].rearrange("p (c o) -> p c o", c=c)

    def dma(queue, outap, inap, r, w, tag=""):
        P.op(queue, lambda e, o=outap, i=inap: e.dma_start(o, i), r=r, w=w,
             dma=True, tag=tag)

    def dma_slow(queue, outap, inap, r, w, tag=""):
        P.op(queue, lambda e, o=outap, i=inap: e.dma_start(
            o, i, allow_slow_non_contiguous=True), r=r, w=w, dma=True, tag=tag)

    def mm(outap, bank, lhsT, rhs, r, start, stop):
        P.op("tensor", lambda e, o=outap, l=lhsT, rr=rhs, s1=start, s2=stop:
             e.matmul(o, l, rr, start=s1, stop=s2), r=r, w=[bank])

    def tp(outap, bank, inap, r):
        P.op("tensor", lambda e, o=outap, i=inap: e.transpose(o, i, ident[:, :]),
             r=list(r) + ["ident"], w=[bank])

    def vop(fn, r, w, tag=""):
        P.op("vector", fn, r=r, w=w, tag=tag)

    def sop(fn, r, w, tag=""):
        P.op("scalar", fn, r=r, w=w, tag=tag)

    def ln_tile(src_ap, src_keys, dst_ap, dst_key, slot, extra_scale=1.0):
        ls = lns[slot]
        lk = f"lns{slot}"
        sq = sqs[0]
        sk = "sqs0"
        su, mean, msq, var, inv, nb = (ls[:, 0:1], ls[:, 1:2], ls[:, 2:3],
                                       ls[:, 3:4], ls[:, 4:5], ls[:, 5:6])
        sop(lambda e: e.activation(sq[:, :], src_ap, AF.Square, accum_out=su),
            r=src_keys, w=[sk, lk])
        vop(lambda e: e.reduce_sum(mean, src_ap, axis=AX.X), r=src_keys, w=[lk])
        vop(lambda e: e.tensor_scalar(mean, mean, 1.0 / D, None, ALU.mult),
            r=[lk], w=[lk])
        vop(lambda e: e.tensor_scalar(msq, mean, mean, -EPS, ALU.mult, op1=ALU.add),
            r=[lk], w=[lk])
        vop(lambda e: e.scalar_tensor_tensor(var, su, 1.0 / D, msq, ALU.mult,
                                             ALU.subtract), r=[lk], w=[lk])
        sop(lambda e: e.activation(var, var, AF.Ln), r=[lk], w=[lk])
        sop(lambda e: e.activation(inv, var, AF.Exp, scale=-0.5), r=[lk], w=[lk])
        if extra_scale != 1.0:
            vop(lambda e: e.tensor_scalar(inv, inv, extra_scale, None, ALU.mult),
                r=[lk], w=[lk])
        vop(lambda e: e.tensor_scalar(nb, mean, inv, -1.0, ALU.mult, op1=ALU.mult),
            r=[lk], w=[lk])
        sop(lambda e: e.activation(dst_ap, src_ap, AF.Identity, bias=nb, scale=inv),
            r=list(src_keys) + [lk], w=[dst_key])

    dma("sync", ident[:, :], identc.ap(), r=[], w=["ident"])
    dma("sync", ones_b[:, :], onesb.ap(), r=[], w=["ones_b"])
    dma("sync", ones_f[:, :], onesf.ap(), r=[], w=["ones_f"])
    dma("sync", dD[:, :], dmD.ap(), r=[], w=["dD"])
    dma("sync", dC[:, :], dmC.ap(), r=[], w=["dC"])
    rowify = lambda apx, n: apx.rearrange("(a b) -> a b", a=1)
    dma("gpsimd", bqkv_v[:, :], bqkv.ap()[2048:3072].rearrange("(a b) -> a b", a=1),
        r=[], w=["bqkv_v"])
    dma("gpsimd", bproj_r[:, :], rowify(bproj.ap(), D), r=[], w=["bproj_r"])

    dma("sync", lgc[:, :], bass.AP(gamma, 0, [[0, 128], [1, 1]]),
        r=[], w=["lgc"])
    sop(lambda e: e.activation(lgc[:, :], lgc[:, :], AF.Ln), r=["lgc"], w=["lgc"])
    sop(lambda e: e.activation(dmask[:, :], dD[:, :], AF.Exp, scale=lgc[:, :]),
        r=["dD", "lgc"], w=["dmask"])
    vop(lambda e: e.tensor_tensor(dmask[:, :], dmask[:, :], dC[:, :], ALU.mult),
        r=["dmask", "dC"], w=["dmask"])

    dma("sync", gbb[:, :], bass.AP(gate_b, 0, [[0, 128], [1, E]]),
        r=[], w=["gbb"])

    bqkv_qk = A("bqkv_qk", [1, 2048], BF16)
    dma("gpsimd", bqkv_qk[:, :], bqkv.ap()[0:2048].rearrange("(a b) -> a b", a=1),
        r=[], w=["bqkv_qk"])

    gwtmp_cm = nc.sbuf_tensor("gwtmp", [16, D], BF16, side="right")
    gwtmp = gwtmp_cm.__enter__()
    P.new_buf("gwtmp")
    dma("gpsimd", gwtmp[:, :], gate_w.ap(), r=[], w=["gwtmp"])
    for dc in range(NDC):
        apo, ko = PSB(6 + dc % 2)
        P.op("tensor", lambda e, dc=dc, apo=apo: e.transpose(
            apo[:, 0:16], gwtmp[0:16, dc * 128:(dc + 1) * 128], ident[0:16, 0:16]),
            r=["gwtmp", "ident"], w=[ko])
        vop(lambda e, dc=dc, apo=apo: e.tensor_copy(
            gws[:, dc * E:(dc + 1) * E], apo[:, 0:16]), r=[ko], w=["gws"])
    gwtmp_cm.__exit__(None, None, None)
    P.retire("gwtmp")

    for s in range([1] if DIRECT_VV else [NSEQ])[0] if False else range(1 if DIRECT_VV else NSEQ):
        xseq = xs.ap()[s]
        x1seq = x1s.ap()[s]
        outseq = out.ap()[s]

        k_cm = nc.sbuf_tensor(f"k_fm{s}", [128, NDC * L], BF16, side="left")
        k_fm = k_cm.__enter__(); P.new_buf("k_fm")
        v_cm = nc.sbuf_tensor(f"v_tm{s}", [128, NT * D], BF16, side="left")
        v_tm = v_cm.__enter__(); P.new_buf("v_tm")
        q_cm = nc.sbuf_tensor(f"q_fm{s}", [128, NDC * L], BF16, side="left")
        q_fm = q_cm.__enter__(); P.new_buf("q_fm")
        wq_cm = nc.sbuf_tensor(f"wqkv{s}", [128, NDC * 3 * D], BF16, side="right")
        wqkv_sb = wq_cm.__enter__(); P.new_buf("wqkv")
        uf_cm = nc.sbuf_tensor(f"u_fm{s}", [128, NDC * 512], BF16, side="right")
        u_fm = uf_cm.__enter__(); P.new_buf("u_fm")
        ut_cm = nc.sbuf_tensor(f"u_tm{s}", [128, 2 * D], BF16, side="right")
        u_tm = ut_cm.__enter__(); P.new_buf("u_tm")

        if DIRECT_VV == 1:
            P1_RANGE = []
        elif DIRECT_VV == 2:
            P1_RANGE = list(range(NG))
        else:
            P1_RANGE = list(range(NG))
        dma("gpsimd", sb3(wqkv_sb, NDC), fm3(Wqkv.ap()),
            r=[], w=["wqkv"], tag="wqkv")

        for g in P1_RANGE:
            for tt in range(4):
                t = g * 4 + tt
                xk = f"x_t{t % 2}"
                dma("sync", x_t[t % 2][:, :], xseq[t * 128:(t + 1) * 128, :],
                    r=[], w=[xk], tag="x")
                uk = f"u_tm{tt % 2}"
                u_ap = u_tm[:, (tt % 2) * D:(tt % 2) * D + D]
                ln_tile(x_t[t % 2][:, :], [xk], u_ap, "u_tm", tt % 2)
                bi = 6 + tt % 2
                apo, ko = PSB(bi)
                for dc in range(NDC):
                    tp(PSB128(bi, dc), ko, u_ap[:, dc * 128:(dc + 1) * 128],
                       r=["u_tm"])
                dst = u_fm[:, :].rearrange("p (c t) -> p c t", c=NDC)[
                    :, :, tt * 128:(tt + 1) * 128]
                src = apo.rearrange("p (c t) -> p c t", c=NDC)
                vop(lambda e, d0=dst, s0=src: e.tensor_copy(d0, s0),
                    r=[ko], w=["u_fm"])
            for which, base, kdst in (("q", 0, q_fm), ("k", D, k_fm)):
                for oc in range(NDC):
                    bi = 4 + oc % 2
                    apo, ko = PS(bi)
                    for dc in range(NDC):
                        mm(apo, ko,
                           wqkv_sb[:, dc * 3 * D + base + oc * 128:
                                   dc * 3 * D + base + (oc + 1) * 128],
                           u_fm[:, dc * 512:(dc + 1) * 512],
                           r=["wqkv", "u_fm"], start=(dc == 0), stop=False)
                    mm(apo, ko, bqkv_qk[:, base + oc * 128: base + (oc + 1) * 128],
                       ones_b[:, :], r=["bqkv_qk", "ones_b"], start=False, stop=True)
                    dst = kdst[:, oc * L + g * 512: oc * L + (g + 1) * 512]
                    scl = INV_SQRT_D if which == "q" else 1.0
                    sop(lambda e, d0=dst, a0=apo, s0=scl: e.activation(
                        d0, a0, AF.Identity, scale=s0),
                        r=[ko], w=["q_fm" if which == "q" else "k_fm"])
            for tt in range(4):
                t = g * 4 + tt
                for half in range(2):
                    bi = (tt * 2 + half) % 4
                    apo, ko = PS(bi)
                    for dc in range(NDC):
                        mm(apo, ko, u_fm[:, dc * 512 + tt * 128: dc * 512 + (tt + 1) * 128],
                           wqkv_sb[:, dc * 3 * D + 2 * D + half * 512:
                                   dc * 3 * D + 2 * D + (half + 1) * 512],
                           r=["wqkv", "u_fm"], start=(dc == 0), stop=False)
                    mm(apo, ko, ones_b[:, 0:128], bqkv_v[:, half * 512:(half + 1) * 512],
                       r=["ones_b", "bqkv_v"], start=False, stop=True)
                    dst = v_tm[:, t * D + half * 512: t * D + (half + 1) * 512]
                    vop(lambda e, d0=dst, a0=apo: e.tensor_copy(d0, a0),
                        r=[ko], w=["v_tm"])

        ut_cm.__exit__(None, None, None); P.retire("u_tm")
        uf_cm.__exit__(None, None, None); P.retire("u_fm")
        wq_cm.__exit__(None, None, None); P.retire("wqkv")

        if "qkv" in taps and s == 0:
            dma("sync", tap_t["tq"].ap(), q_fm[:, :], r=["q_fm"], w=["tapq"])
            dma("sync", tap_t["tk"].ap(), k_fm[:, :], r=["k_fm"], w=["tapk"])
            dma("sync", tap_t["tv"].ap(), v_tm[:, :], r=["v_tm"], w=["tapv"])

        vv_cm = nc.sbuf_tensor(f"vv_fm{s}", [128, NDC * L], BF16, side="right")
        vv_fm = vv_cm.__enter__(); P.new_buf("vv_fm")
        wp_cm = nc.sbuf_tensor(f"wproj{s}", [128, NDC * D], BF16, side="right")
        wproj_sb = wp_cm.__enter__(); P.new_buf("wproj")
        P_cm = nc.sbuf_tensor(f"P_sb{s}", [128, 2 * L], BF16, side="right")
        P_sb = P_cm.__enter__(); P.new_buf("P_sb0"); P.new_buf("P_sb1")
        PT_cm = nc.sbuf_tensor(f"PT_sb{s}", [128, 2 * L], BF16, side="right")
        PT_sb = PT_cm.__enter__(); P.new_buf("PT_sb0"); P.new_buf("PT_sb1")
        atm_cm = nc.sbuf_tensor(f"a_tm{s}", [128, D], BF16, side="right")
        a_tm = atm_cm.__enter__(); P.new_buf("a_tm")
        afm_cm = nc.sbuf_tensor(f"a_fm{s}", [128, D], BF16, side="right")
        a_fm = afm_cm.__enter__(); P.new_buf("a_fm")
        a2_cm = nc.sbuf_tensor(f"a2{s}", [128, D], F32, side="right")
        a2 = a2_cm.__enter__(); P.new_buf("a2")
        a2n_cm = nc.sbuf_tensor(f"a2n{s}", [128, D], F32, side="right")
        a2n = a2n_cm.__enter__(); P.new_buf("a2n")

        dma("gpsimd", sb3(wproj_sb, NDC), fm3(Wproj.ap()),
            r=[], w=["wproj"], tag="wproj")

        if DIRECT_VV in (1, 2) and s == 0:
            dma("sync", vv_fm[:, :], vvdbg.ap(), r=[], w=["vv_fm"])
        P2A_RANGE = [] if DIRECT_VV in (1, 2) else list(range(NT))

        def p2a_tail(t):
            sl_ = t % 2
            Pap = P_sb[:, sl_ * L: sl_ * L + L]
            Pk = f"P_sb{sl_}"
            PTap = PT_sb[:, sl_ * L: sl_ * L + L]
            PTk = f"PT_sb{sl_}"
            sm = sms[sl_]
            smk = f"sms{sl_}"
            rs = sm[:, 2:3]
            if 'pv' in P2A_PARTS:
                for b in range(2):
                    apo, ko = PSB(6 + b)
                    for j in range(8):
                        tk = b * 8 + j
                        tp(PSB128(6 + b, j), ko,
                           Pap[:, tk * 128:(tk + 1) * 128], r=[Pk])
                    vop(lambda e, b=b, a0=apo, d0=PTap: e.tensor_copy(
                        d0[:, b * 1024:(b + 1) * 1024], a0), r=[ko], w=[PTk])
            for half in (range(2) if 'pv' in P2A_PARTS else []):
                apo, ko = PS(4 + half)
                for tk in range(NT):
                    mm(apo, ko, PTap[:, tk * 128:(tk + 1) * 128],
                       v_tm[:, tk * D + half * 512: tk * D + (half + 1) * 512],
                       r=[PTk, "v_tm"], start=(tk == 0), stop=(tk == NT - 1))
                sop(lambda e, a0=apo, h=half, r0=rs: e.activation(
                    a_tm[:, h * 512:(h + 1) * 512], a0, AF.Identity, scale=r0),
                    r=[ko, smk], w=["a_tm"])
            if 'proj' in P2A_PARTS:
                apo, ko = PSB(6)
                for dc in range(NDC):
                    tp(PSB128(6, dc), ko, a_tm[:, dc * 128:(dc + 1) * 128],
                       r=["a_tm"])
                vop(lambda e, a0=apo: e.tensor_copy(a_fm[:, :], a0),
                    r=[ko], w=["a_fm"])
            for half in (range(2) if 'proj' in P2A_PARTS else []):
                apo, ko = PS(4 + half)
                for dc in range(NDC):
                    mm(apo, ko, a_fm[:, dc * 128:(dc + 1) * 128],
                       wproj_sb[:, dc * D + half * 512: dc * D + (half + 1) * 512],
                       r=["a_fm", "wproj"], start=(dc == 0), stop=False)
                mm(apo, ko, ones_b[:, 0:128], bproj_r[:, half * 512:(half + 1) * 512],
                   r=["ones_b", "bproj_r"], start=False, stop=True)
                vop(lambda e, a0=apo, h=half: e.tensor_copy(
                    a2[:, h * 512:(h + 1) * 512], a0), r=[ko], w=["a2"])
            xk = f"x_t{sl_}"
            if 'x1' in P2A_PARTS:
                dma("sync", x_t[sl_][:, :], xseq[t * 128:(t + 1) * 128, :],
                    r=[], w=[xk], tag="x2")
                ln_tile(a2[:, :], ["a2"], a2n[:, :], "a2n", sl_)
                vop(lambda e, xa=x_t[sl_]: e.tensor_tensor(
                    xa[:, :], a2n[:, :], xa[:, :], ALU.add), r=["a2n", xk], w=[xk])
                dma("sync", x1seq[t * 128:(t + 1) * 128, :], x_t[sl_][:, :],
                    r=[xk], w=[f"x1s{s}_{t}"], tag="x1w")
                ln_tile(x_t[sl_][:, :], [xk], a_tm[:, :], "a_tm", sl_)
            if 'vvt' in P2A_PARTS:
                apo, ko = PSB(7)
                for dc in range(NDC):
                    tp(PSB128(7, dc), ko, a_tm[:, dc * 128:(dc + 1) * 128],
                       r=["a_tm"])
                dst = vv_fm[:, :].rearrange("p (c t) -> p c t", c=NDC)[
                    :, :, t * 128:(t + 1) * 128]
                src = apo.rearrange("p (c t) -> p c t", c=NDC)
                vop(lambda e, d0=dst, s0=src: e.tensor_copy(d0, s0),
                    r=[ko], w=["vv_fm"])

        for t in P2A_RANGE:
            for tc in (range(4) if 'sc' in P2A_PARTS else []):
                apo, ko = PS(tc)
                for dc in range(NDC):
                    mm(apo, ko,
                       q_fm[:, dc * L + t * 128: dc * L + (t + 1) * 128],
                       k_fm[:, dc * L + tc * 512: dc * L + (tc + 1) * 512],
                       r=["q_fm", "k_fm"], start=(dc == 0), stop=(dc == NDC - 1))
            dtc, dj = t // 4, t % 4
            apd = PS128(dtc, dj)
            if 'sc' in P2A_PARTS:
                vop(lambda e, a0=apd: e.tensor_tensor(a0, a0, dmask[:, :], ALU.mult),
                    r=[f"bk{dtc}", "dmask"], w=[f"bk{dtc}"])
            sl_ = t % 2
            sm = sms[sl_]
            smk = f"sms{sl_}"
            nmax, ssum, rs = sm[:, 0:1], sm[:, 1:2], sm[:, 2:3]
            if 'sc' in P2A_PARTS:
                vop(lambda e, n0=nmax: e.reduce_max(n0, psS[:, :], axis=AX.X,
                                                    negate=True),
                    r=["bk0", "bk1", "bk2", "bk3"], w=[smk])
                sop(lambda e, n0=nmax, s0=ssum, d0=P_sb[:, sl_ * L: sl_ * L + L]:
                    e.activation(d0, psS[:, :], AF.Exp, bias=n0, accum_out=s0),
                    r=["bk0", "bk1", "bk2", "bk3", smk],
                    w=[f"P_sb{sl_}", smk])
                vop(lambda e, s0=ssum, r0=rs: e.reciprocal(r0, s0), r=[smk], w=[smk])
            if t >= 1:
                p2a_tail(t - 1)
        if P2A_RANGE:
            p2a_tail(NT - 1)

        if DIRECT_VV == 3 and s == 0:
            dma("sync", vv_fm[:, :], vvdbg.ap(), r=["vv_fm"], w=["vv_fm"])
        a2n_cm.__exit__(None, None, None); P.retire("a2n")
        a2_cm.__exit__(None, None, None); P.retire("a2")
        afm_cm.__exit__(None, None, None); P.retire("a_fm")
        atm_cm.__exit__(None, None, None); P.retire("a_tm")
        PT_cm.__exit__(None, None, None); P.retire("PT_sb0", "PT_sb1")
        P_cm.__exit__(None, None, None); P.retire("P_sb0", "P_sb1")
        wp_cm.__exit__(None, None, None); P.retire("wproj")
        q_cm.__exit__(None, None, None); P.retire("q_fm")

        if "vv" in taps and s == 0:
            dma("sync", tap_t["tvv"].ap(), vv_fm[:, :], r=["vv_fm"], w=["tapvv"])

        apg, kg = PS(4)
        for t in range(NT):
            for dc in range(NDC):
                mm(apg[:, t * E:(t + 1) * E], kg,
                   vv_fm[:, dc * L + t * 128: dc * L + (t + 1) * 128],
                   gws[:, dc * E:(dc + 1) * E],
                   r=["vv_fm", "gws"], start=(dc == 0), stop=(dc == NDC - 1))
        for t in range(NT):
            gsl = apg[:, t * E:(t + 1) * E]
            nmax, ssum, rs = r_sc[:, 0:1], r_sc[:, 1:2], r_sc[:, 2:3]
            thr = r_sc[:, 3:4]
            m1, m2 = r_sc[:, 4:5], r_sc[:, 5:6]
            vop(lambda e, g0=gsl, n0=nmax: e.reduce_max(n0, g0, axis=AX.X,
                                                        negate=True),
                r=[kg], w=["r_sc"])
            sop(lambda e, g0=gsl, n0=nmax, s0=ssum: e.activation(
                r_orig[:, :], g0, AF.Exp, bias=n0, accum_out=s0),
                r=[kg, "r_sc"], w=["r_orig", "r_sc"])
            vop(lambda e, s0=ssum, r0=rs: e.reciprocal(r0, s0),
                r=["r_sc"], w=["r_sc"])
            vop(lambda e, r0=rs: e.tensor_scalar(r_orig[:, :], r_orig[:, :], r0,
                                                 None, ALU.mult),
                r=["r_orig", "r_sc"], w=["r_orig"])
            vop(lambda e: e.tensor_tensor(r_s[:, :], r_orig[:, :], gbb[:, :],
                                          ALU.add), r=["r_orig", "gbb"], w=["r_s"])
            first = True
            for (i, j) in ((0, 1), (0, 2), (0, 3), (1, 2), (1, 3), (2, 3)):
                a0 = r_s[:, i::4]
                b0 = r_s[:, j::4]
                if first:
                    vop(lambda e, a0=a0, b0=b0: e.tensor_tensor(
                        r_gs[:, :], a0, b0, ALU.add), r=["r_s"], w=["r_gs"])
                    first = False
                else:
                    vop(lambda e, a0=a0, b0=b0: e.tensor_tensor(
                        r_t4[:, :], a0, b0, ALU.add), r=["r_s"], w=["r_t4"])
                    vop(lambda e: e.tensor_tensor(r_gs[:, :], r_gs[:, :],
                                                  r_t4[:, :], ALU.max),
                        r=["r_gs", "r_t4"], w=["r_gs"])
            first = True
            for (i, j) in ((0, 1), (0, 2), (0, 3), (1, 2), (1, 3), (2, 3)):
                vop(lambda e, i=i, j=j: e.tensor_tensor(
                    r_t4[:, 0:1], r_gs[:, i:i + 1], r_gs[:, j:j + 1], ALU.min),
                    r=["r_gs"], w=["r_t4"])
                if first:
                    vop(lambda e: e.tensor_copy(thr, r_t4[:, 0:1]),
                        r=["r_t4"], w=["r_sc"])
                    first = False
                else:
                    vop(lambda e: e.tensor_tensor(thr, thr, r_t4[:, 0:1], ALU.max),
                        r=["r_sc", "r_t4"], w=["r_sc"])
            vop(lambda e: e.tensor_scalar(r_t4[:, :], r_gs[:, :], thr, None,
                                          ALU.is_ge), r=["r_gs", "r_sc"], w=["r_t4"])
            for i in range(4):
                vop(lambda e, i=i: e.tensor_scalar(r_ms[:, i::4], r_t4[:, :], 1.0,
                                                   1e30, ALU.subtract, op1=ALU.mult),
                    r=["r_t4"], w=["r_ms"])
            vop(lambda e: e.tensor_tensor(r_ms[:, :], r_ms[:, :], r_s[:, :],
                                          ALU.add), r=["r_ms", "r_s"], w=["r_ms"])
            vop(lambda e, m0=m1: e.reduce_max(m0, r_ms[:, :], axis=AX.X),
                r=["r_ms"], w=["r_sc"])
            vop(lambda e, m0=m1: e.tensor_scalar(r_s[:, :], r_ms[:, :], m0, -2e30,
                                                 ALU.is_ge, op1=ALU.mult),
                r=["r_ms", "r_sc"], w=["r_s"])
            vop(lambda e: e.tensor_tensor(r_s[:, :], r_s[:, :], r_ms[:, :],
                                          ALU.add), r=["r_s", "r_ms"], w=["r_s"])
            vop(lambda e, m0=m2: e.reduce_max(m0, r_s[:, :], axis=AX.X),
                r=["r_s"], w=["r_sc"])
            vop(lambda e, m0=m2: e.tensor_scalar(r_s[:, :], r_ms[:, :], m0, None,
                                                 ALU.is_ge), r=["r_ms", "r_sc"],
                w=["r_s"])
            vop(lambda e, t=t: e.tensor_tensor(comb[:, t * E:(t + 1) * E],
                                               r_orig[:, :], r_s[:, :], ALU.mult),
                r=["r_orig", "r_s"], w=["comb"])
            if "rdbg" in taps and s == 0 and t == 0:
                for i_, (tt_, kk_, n_) in enumerate([
                        (r_orig, "r_orig", E), (r_s, "r_s", E), (r_gs, "r_gs", G),
                        (r_t4, "r_t4", G), (r_ms, "r_ms", E), (gbb, "gbb", E),
                        (r_sc, "r_sc", 8)]):
                    vop(lambda e, i_=i_, tt_=tt_, n_=n_: e.tensor_copy(
                        rdbg[:, 16 * i_:16 * i_ + n_], tt_[:, 0:n_]),
                        r=[kk_], w=["rdbg"])
                dma("sync", tap_t["trd"].ap(), rdbg[:, :], r=["rdbg"], w=["taprd"])

        if "comb" in taps and s == 0:
            dma("sync", tap_t["tcomb"].ap(), comb[:, :], r=["comb"], w=["tapc"])

        v_cm.__exit__(None, None, None); P.retire("v_tm")
        k_cm.__exit__(None, None, None); P.retire("k_fm")
        ylo_cm = nc.sbuf_tensor(f"ylo{s}", [128, 8 * D], F32, side="right")
        y_lo = ylo_cm.__enter__(); P.new_buf("y_lo")
        yhi_cm = nc.sbuf_tensor(f"yhi{s}", [128, 8 * D], F32, side="right")
        y_hi = yhi_cm.__enter__(); P.new_buf("y_hi")

        def y_ap(t):
            if t < 8:
                return y_lo[:, t * D:(t + 1) * D], "y_lo"
            return y_hi[:, (t - 8) * D:(t - 8) * D + D], "y_hi"

        es_cm = [nc.sbuf_tensor(f"eslot{s}_{i}", [128, 3 * 4096], BF16, side="right")
                 for i in range(2)]
        eslot = [c.__enter__() for c in es_cm]
        for _sl in range(2):
            for _kk in ("w1", "w3", "w2"):
                P.new_buf(f"eslot{_sl}_{_kk}")
        br_cm = [nc.sbuf_tensor(f"brow{s}_{i}", [1, 2048], BF16, side="right")
                 for i in range(2)]
        bc_cm = [nc.sbuf_tensor(f"bcol{s}_{i}", [128, 8], F32, side="right")
                 for i in range(2)]
        brow = [c.__enter__() for c in br_cm]
        for _sl in range(2):
            for _kk in ("b1", "b3", "b2"):
                P.new_buf(f"brow{_sl}_{_kk}")
        bcol = [c.__enter__() for c in bc_cm]
        P.new_buf("bcol0_b1"); P.new_buf("bcol0_b3")
        P.new_buf("bcol1_b1"); P.new_buf("bcol1_b3")
        h_cm = nc.sbuf_tensor(f"h_sb{s}", [128, 4 * 512], BF16, side="right")
        h_sb = h_cm.__enter__(); P.new_buf("h_sb")
        s1_cm = nc.sbuf_tensor(f"s1_sb{s}", [128, 512], BF16, side="right")
        s1_sb = s1_cm.__enter__(); P.new_buf("s1_sb")
        fn_cm = nc.sbuf_tensor(f"fn_sb{s}", [128, D], F32, side="right")
        fn_sb = fn_cm.__enter__(); P.new_buf("fn_sb")

        def load_expert(e_i, slot):
            sl = eslot[slot]
            k1w = f"eslot{slot}_w1"
            k3w = f"eslot{slot}_w3"
            k2w = f"eslot{slot}_w2"
            kb1 = f"brow{slot}_b1"
            kb3 = f"brow{slot}_b3"
            kb2 = f"brow{slot}_b2"
            if e_i < E:
                w1 = fm3(We1.ap()[e_i])
                w3 = fm3(We3.ap()[e_i])
                w2 = fm3(We2.ap()[e_i])
                dma("gpsimd", sl[:, 0:4096].rearrange("p (c o) -> p c o", c=8), w1,
                    r=[], w=[k1w], tag="we1")
                dma("gpsimd", sl[:, 4096:8192].rearrange("p (c o) -> p c o", c=8), w3,
                    r=[], w=[k3w], tag="we3")
                dma("gpsimd", sl[:, 8192:12288].rearrange("p (c o) -> p c o", c=4), w2,
                    r=[], w=[k2w], tag="we2")
                dma_slow("sync", bcol[slot][:, 0:4],
                         be1.ap()[e_i].rearrange("(c p) -> p c", p=128),
                         r=[], w=[kb1])
                dma_slow("sync", bcol[slot][:, 4:8],
                         be3.ap()[e_i].rearrange("(c p) -> p c", p=128),
                         r=[], w=[kb3])
                dma("gpsimd", brow[slot][:, 1024:2048],
                    rowify(be2.ap()[e_i], D), r=[], w=[kb2])
            else:
                hh = e_i - E
                w1 = fm3(ws1.ap()[:, hh * 512:(hh + 1) * 512])
                w3 = fm3(ws3.ap()[:, hh * 512:(hh + 1) * 512])
                w2 = fm3(ws2.ap()[hh * 512:(hh + 1) * 512, :])
                dma("gpsimd", sl[:, 0:4096].rearrange("p (c o) -> p c o", c=8), w1,
                    r=[], w=[k1w], tag="ws1")
                dma("gpsimd", sl[:, 4096:8192].rearrange("p (c o) -> p c o", c=8), w3,
                    r=[], w=[k3w], tag="ws3")
                dma("gpsimd", sl[:, 8192:12288].rearrange("p (c o) -> p c o", c=4), w2,
                    r=[], w=[k2w], tag="ws2")
                dma_slow("sync", bcol[slot][:, 0:4],
                         bs1.ap()[hh * 512:(hh + 1) * 512].rearrange("(c p) -> p c", p=128),
                         r=[], w=[kb1])
                dma_slow("sync", bcol[slot][:, 4:8],
                         bs3.ap()[hh * 512:(hh + 1) * 512].rearrange("(c p) -> p c", p=128),
                         r=[], w=[kb3])
                if hh == 0:
                    dma("gpsimd", brow[slot][:, 1024:2048], rowify(bs2.ap(), D),
                        r=[], w=[kb2])

        if BARRIER_P2B:
            _BARN[0] += 1
            bn = _BARN[0]
            for _e in ("tensor", "vector", "scalar"):
                P.op(_e, lambda e: e.nop(), r=[], w=[f"bar{bn}_{_e}"])
            for _e in ("tensor", "vector", "scalar"):
                P.op(_e, lambda e: e.nop(),
                     r=[f"bar{bn}_tensor", f"bar{bn}_vector", f"bar{bn}_scalar"], w=[])
        order = EXPERT_ORDER
        load_expert(order[0], 0)
        for idx in range(len(order)):
            e_i = order[idx]
            slot = idx % 2
            sl = eslot[slot]
            k1w = f"eslot{slot}_w1"
            k3w = f"eslot{slot}_w3"
            k2w = f"eslot{slot}_w2"
            kb1 = f"brow{slot}_b1"
            kb3 = f"brow{slot}_b3"
            kb2 = f"brow{slot}_b2"
            if idx + 1 < len(order):
                load_expert(order[idx + 1], 1 - slot)
            if "edbg" in taps and s == 0 and e_i == EDBG_Y:
                dma("sync", tap_t["ty0"].ap(), y_lo[:, 0:D], r=["y_lo"], w=["tapy0"])
            first = idx == 0
            shared = e_i >= E
            sh_h = e_i - E
            for g in range(NG):
                for ic in range(4):
                    b1, k1 = PS(ic % 2)
                    b3, k3 = PS(2 + ic % 2)
                    for dc in range(NDC):
                        mm(b1, k1, sl[:, dc * 512 + ic * 128: dc * 512 + (ic + 1) * 128],
                           vv_fm[:, dc * L + g * 512: dc * L + (g + 1) * 512],
                           r=[k1w, "vv_fm"], start=(dc == 0), stop=(dc == NDC - 1))
                    for dc in range(NDC):
                        mm(b3, k3, sl[:, 4096 + dc * 512 + ic * 128:
                                      4096 + dc * 512 + (ic + 1) * 128],
                           vv_fm[:, dc * L + g * 512: dc * L + (g + 1) * 512],
                           r=[k3w, "vv_fm"], start=(dc == 0), stop=(dc == NDC - 1))
                    sop(lambda e, a0=b1, ic=ic, sl_=slot: e.activation(
                        s1_sb[:, :], a0, AF.Silu, bias=bcol[sl_][:, ic:ic + 1]),
                        r=[k1, kb1], w=["s1_sb"])
                    vop(lambda e, a0=b3, ic=ic, sl_=slot: e.scalar_tensor_tensor(
                        h_sb[:, ic * 512:(ic + 1) * 512], a0,
                        bcol[sl_][:, 4 + ic: 5 + ic], s1_sb[:, :],
                        ALU.add, ALU.mult), r=["s1_sb", k3, kb3], w=["h_sb"])
                if "edbg" in taps and s == 0 and e_i == EDBG_E and g == 0:
                    dma("sync", tap_t["th"].ap(), h_sb[:, :], r=["h_sb"], w=["taph"])
                    dma("sync", tap_t["tsl"].ap(), sl[:, 0:4096], r=[k1w], w=["tapsl"])
                for tt in range(4):
                    t = g * 4 + tt
                    ya, yk = y_ap(t)
                    addb = (not shared) or sh_h == 0
                    for half in range(2):
                        bo, ko = PS(4 + half)
                        for ic in range(4):
                            mm(bo, ko, h_sb[:, ic * 512 + tt * 128:
                                            ic * 512 + (tt + 1) * 128],
                               sl[:, 8192 + ic * D + half * 512:
                                  8192 + ic * D + (half + 1) * 512],
                               r=["h_sb", k2w], start=(ic == 0),
                               stop=(ic == 3 and not addb))
                        if addb:
                            mm(bo, ko, ones_b[:, 0:128],
                               brow[slot][:, 1024 + half * 512: 1024 + (half + 1) * 512],
                               r=["ones_b", kb2], start=False, stop=True)
                        dst = ya[:, half * 512:(half + 1) * 512]
                        if first and shared:
                            vop(lambda e, d0=dst, a0=bo: e.tensor_copy(d0, a0),
                                r=[ko], w=[yk])
                        elif first:
                            vop(lambda e, d0=dst, a0=bo, t=t, e_i=e_i: e.tensor_scalar(
                                d0, a0, comb[:, t * E + e_i: t * E + e_i + 1], None, ALU.mult),
                                r=[ko, "comb"], w=[yk])
                        elif shared:
                            vop(lambda e, d0=dst, a0=bo: e.tensor_tensor(
                                d0, a0, d0, ALU.add), r=[ko, yk], w=[yk])
                        else:
                            vop(lambda e, d0=dst, a0=bo, t=t, e_i=e_i:
                                e.scalar_tensor_tensor(
                                    d0, a0, comb[:, t * E + e_i: t * E + e_i + 1],
                                    d0, ALU.mult, ALU.add),
                                r=[ko, "comb", yk], w=[yk])
                    if idx == len(order) - 1 and not DIRECT_VV:
                        ln_tile(ya, [yk], fn_sb[:, :], "fn_sb", t % 2)
                        fxk = f"fin_x{t % 2}"
                        dma("sync", fin_x[t % 2][:, :],
                            x1seq[t * 128:(t + 1) * 128, :],
                            r=[f"x1s{s}_{t}"], w=[fxk], tag="x1r")
                        vop(lambda e, xa=fin_x[t % 2]: e.tensor_tensor(
                            xa[:, :], fn_sb[:, :], xa[:, :], ALU.add),
                            r=["fn_sb", fxk], w=[fxk])
                        dma("sync", outseq[t * 128:(t + 1) * 128, :],
                            fin_x[t % 2][:, :],
                            r=[fxk], w=[f"outhbm{s}_{t}"], tag="outw")

        if "vv2" in taps and s == 0:
            dma("sync", tap_t["tvv2"].ap(), vv_fm[:, :], r=["vv_fm"], w=["tapvv2"])
        if "y" in taps and s == 0:
            dma("sync", tap_t["ty"].ap()[:, 0:8 * D], y_lo[:, :], r=["y_lo"], w=["tapy"])
            dma("sync", tap_t["ty"].ap()[:, 8 * D:], y_hi[:, :], r=["y_hi"], w=["tapy"])

        fn_cm.__exit__(None, None, None); P.retire("fn_sb")
        s1_cm.__exit__(None, None, None); P.retire("s1_sb")
        h_cm.__exit__(None, None, None); P.retire("h_sb")
        for c in reversed(bc_cm):
            c.__exit__(None, None, None)
        P.retire("bcol0_b1", "bcol0_b3", "bcol1_b1", "bcol1_b3")
        for c in reversed(br_cm):
            c.__exit__(None, None, None)
        P.retire(*[f"brow{i}_{k}" for i in range(2) for k in ("b1", "b3", "b2")])
        for c in reversed(es_cm):
            c.__exit__(None, None, None)
        P.retire(*[f"eslot{i}_{k}" for i in range(2) for k in ("w1", "w3", "w2")])
        yhi_cm.__exit__(None, None, None); P.retire("y_hi")
        ylo_cm.__exit__(None, None, None); P.retire("y_lo")
        vv_cm.__exit__(None, None, None); P.retire("vv_fm")

    with ExitStack() as stack:
        block = stack.enter_context(nc.Block())
        P.finalize(nc, block, stack)
    return nc


_HOST_CONSTS = None


def _host_consts():
    global _HOST_CONSTS
    if _HOST_CONSTS is None:
        i = np.arange(128)
        dD_ = np.maximum(i[:, None] - i[None, :], 0).astype(np.float32)
        dC_ = (i[:, None] >= i[None, :]).astype(np.float32)
        _HOST_CONSTS = {
            "identc": np.eye(128, dtype=ml_dtypes.bfloat16),
            "onesb": np.ones((1, 512), dtype=ml_dtypes.bfloat16),
            "onesf": np.ones((1, 128), dtype=np.float32),
            "dmD": dD_,
            "dmC": dC_,
        }
    return _HOST_CONSTS


def make_in_maps(inputs):
    x = np.ascontiguousarray(np.asarray(inputs["L0_attn_x"], dtype=np.float32))
    consts = _host_consts()
    common = {
        "Wqkv": np.ascontiguousarray(inputs["L0_attn_Wqkv"]),
        "bqkv": np.ascontiguousarray(inputs["L0_attn_bqkv"]),
        "Wproj": np.ascontiguousarray(inputs["L0_attn_Wproj"]),
        "bproj": np.ascontiguousarray(inputs["L0_attn_bproj"]),
        "gamma": np.asarray(inputs["L0_attn_gamma"], dtype=np.float32).reshape(1, 1),
        "gate_w": np.ascontiguousarray(inputs["L0_ffn_gate_w"]),
        "gate_b": np.ascontiguousarray(inputs["L0_ffn_gate_b"]),
        "We1": np.ascontiguousarray(inputs["L0_ffn_We1"]),
        "be1": np.ascontiguousarray(inputs["L0_ffn_be1"]),
        "We2": np.ascontiguousarray(inputs["L0_ffn_We2"]),
        "be2": np.ascontiguousarray(inputs["L0_ffn_be2"]),
        "We3": np.ascontiguousarray(inputs["L0_ffn_We3"]),
        "be3": np.ascontiguousarray(inputs["L0_ffn_be3"]),
        "ws1": np.ascontiguousarray(inputs["L0_ffn_ws1"]),
        "bs1": np.ascontiguousarray(inputs["L0_ffn_bs1"]),
        "ws2": np.ascontiguousarray(inputs["L0_ffn_ws2"]),
        "bs2": np.ascontiguousarray(inputs["L0_ffn_bs2"]),
        "ws3": np.ascontiguousarray(inputs["L0_ffn_ws3"]),
        "bs3": np.ascontiguousarray(inputs["L0_ffn_bs3"]),
    }
    common.update(consts)
    in_maps = []
    for c in range(8):
        m = dict(common)
        m["xs"] = np.ascontiguousarray(x[2 * c: 2 * c + 2])
        in_maps.append(m)
    return in_maps


def kernel(**inputs):
    nc = build_nc()
    in_maps = make_in_maps(inputs)
    res = run_bass_kernel_spmd(nc, in_maps, core_ids=list(range(8)))
    return np.concatenate([r["out"] for r in res.results], axis=0)


# version 06
# speedup vs baseline: 2.5906x; 1.1105x over previous
import numpy as np
import ml_dtypes
from contextlib import ExitStack
import concourse.bass as bass
import concourse.mybir as mybir
from concourse.bass_utils import run_bass_kernel_spmd

F32 = mybir.dt.float32
BF16 = mybir.dt.bfloat16
AF = mybir.ActivationFunctionType
ALU = mybir.AluOpType
AX = mybir.AxisListType

D = 1024
L = 2048
NSEQ = 2
NT = 16
NG = 4
NDC = 8
E = 16
G = 4
INTER = 512
NEXP = 18
INV_SQRT_D = 1.0 / 32.0
EPS = 1e-5
DRAIN_DIST = 3
EDBG_E = 0
EDBG_Y = 1
EDBG_IC = 1
BARRIER_P2B = False
DIRECT_VV = False
P2A_PARTS = {'sc','pv','proj','x1','vvt'}
_BARN = [0]
EXPERT_ORDER = list(range(18))

ENGINES = ("sync", "tensor", "vector", "scalar", "gpsimd")


class _Op:
    __slots__ = ("eng", "fn", "reads", "writes", "dma", "pos", "epos", "waits",
                 "drain", "need_inc", "semval", "clock", "tag", "qslot", "qcount")

    def __init__(self):
        self.need_inc = False
        self.semval = None


class Prog:
    def __init__(self):
        self.ops = []
        self.epos = {e: 0 for e in ENGINES}
        self.freed = []
        self.alias_events = []

    def op(self, eng, fn, r=(), w=(), dma=False, tag=""):
        o = _Op()
        o.eng = eng
        o.fn = fn
        o.reads = tuple(r)
        o.writes = tuple(w)
        o.dma = dma
        o.tag = tag
        o.pos = len(self.ops)
        o.epos = self.epos[eng]
        self.epos[eng] += 1
        self.ops.append(o)
        return o

    def retire(self, *keys):
        self.freed.extend(keys)

    def new_buf(self, key):
        self.alias_events.append((len(self.ops), key, len(self.freed)))

    def finalize(self, nc, block, stack):
        POOL = 8
        sems = {}
        for e in ENGINES:
            sems[("c", e)] = stack.enter_context(nc.semaphore(f"cs_{e}"))
        for e in ("sync", "gpsimd", "scalar", "vector"):
            for p in range(POOL):
                sems[("q", e, p)] = stack.enter_context(nc.semaphore(f"qs_{e}{p}"))

        qidx = {e: 0 for e in ENGINES}
        qslotcnt = {}
        for o in self.ops:
            if o.dma:
                i = qidx[o.eng]
                qidx[o.eng] += 1
                o.qslot = i % POOL
                kqs = (o.eng, o.qslot)
                qslotcnt[kqs] = qslotcnt.get(kqs, 0) + 1
                o.qcount = qslotcnt[kqs]
                o.semval = 16 * o.qcount

        writers = {}
        readers = {}
        clocks = {e: {} for e in ENGINES}
        ev_i = 0
        pending_alias = {}

        def skey(o):
            return ("q", o.eng, o.qslot) if o.dma else o.eng

        def covered(clk, d):
            return clk.get(skey(d), -1) >= d.epos

        for o in self.ops:
            while ev_i < len(self.alias_events) and self.alias_events[ev_i][0] <= o.pos:
                _, _k, _n = self.alias_events[ev_i]
                pending_alias[_k] = _n
                ev_i += 1
            deps = {}
            for k in o.reads:
                for d in writers.get(k, ()):
                    deps[d.pos] = d
            for k in o.writes:
                for d in writers.get(k, ()):
                    deps[d.pos] = d
                for r in readers.get(k, ()):
                    deps[r.pos] = r
            for k in o.writes + o.reads:
                n = pending_alias.pop(k, None)
                if n is not None:
                    for fk in self.freed[:n]:
                        for d in writers.get(fk, ()):
                            deps[d.pos] = d
                        for r in readers.get(fk, ()):
                            deps[r.pos] = r
            deps.pop(o.pos, None)

            o.waits = []
            o.drain = False
            myclock = dict(clocks[o.eng])
            for p in sorted(deps):
                d = deps[p]
                dk = skey(d)
                if d.eng == o.eng and not d.dma:
                    if o.eng != "tensor" and o.epos - d.epos <= DRAIN_DIST:
                        o.drain = True
                    for e2, v in d.clock.items():
                        if v > myclock.get(e2, -1):
                            myclock[e2] = v
                    continue
                if myclock.get(dk, -1) >= d.epos:
                    continue
                o.waits.append(d)
                d.need_inc = True
                for e2, v in d.clock.items():
                    if v > myclock.get(e2, -1):
                        myclock[e2] = v
                myclock[dk] = d.epos
            o.clock = dict(myclock)
            o.clock[skey(o)] = o.epos
            clocks[o.eng] = dict(myclock) if o.dma else dict(o.clock)

            for k in o.reads:
                readers.setdefault(k, []).append(o)
            for k in o.writes:
                prev = writers.get(k, [])
                writers[k] = [d for d in prev if not covered(o.clock, d)] + [o]
                readers[k] = []

        ccnt = {e: 0 for e in ENGINES}
        for o in self.ops:
            if o.dma:
                o.need_inc = True
            elif o.need_inc:
                ccnt[o.eng] += 1
                o.semval = ccnt[o.eng]

        per_eng = {e: [o for o in self.ops if o.eng == e] for e in ENGINES}
        qfinal = {(e, p): 16 * c for (e, p), c in qslotcnt.items()}

        def mk_body(ename):
            ops = per_eng[ename]

            def body(eng):
                for o in ops:
                    if o.drain:
                        eng.drain()
                    best = {}
                    if o.waits:
                        for d in o.waits:
                            kk = ("q", d.eng, d.qslot) if d.dma else ("c", d.eng)
                            if best.get(kk, -1) < d.semval:
                                best[kk] = d.semval
                    if o.dma and o.qcount > 1:
                        kk = ("q", o.eng, o.qslot)
                        if best.get(kk, -1) < 16 * (o.qcount - 1):
                            best[kk] = 16 * (o.qcount - 1)
                    for kk, v in best.items():
                        eng.wait_ge(sems[kk], v)
                    inst = o.fn(eng)
                    if o.dma:
                        inst.then_inc(sems[("q", o.eng, o.qslot)], 16)
                    elif o.need_inc:
                        inst.then_inc(sems[("c", o.eng)], 1)
                if ename == "sync":
                    for (e, p), v in qfinal.items():
                        eng.wait_ge(sems[("q", e, p)], v)
            return body

        for ename in ENGINES:
            if per_eng[ename] or ename == "sync":
                getattr(block, ename)(mk_body(ename))



def build_nc(taps=()):
    taps = set(taps)
    nc = bass.Bass("TRN2", target_bir_lowering=False)
    P = Prog()

    xs = nc.dram_tensor("xs", [NSEQ, L, D], F32, kind="ExternalInput")
    Wqkv = nc.dram_tensor("Wqkv", [D, 3 * D], F32, kind="ExternalInput")
    bqkv = nc.dram_tensor("bqkv", [3 * D], F32, kind="ExternalInput")
    Wproj = nc.dram_tensor("Wproj", [D, D], F32, kind="ExternalInput")
    bproj = nc.dram_tensor("bproj", [D], F32, kind="ExternalInput")
    gamma = nc.dram_tensor("gamma", [1, 1], F32, kind="ExternalInput")
    gate_w = nc.dram_tensor("gate_w", [E, D], F32, kind="ExternalInput")
    gate_b = nc.dram_tensor("gate_b", [E], F32, kind="ExternalInput")
    We1 = nc.dram_tensor("We1", [E, D, INTER], F32, kind="ExternalInput")
    be1 = nc.dram_tensor("be1", [E, INTER], F32, kind="ExternalInput")
    We2 = nc.dram_tensor("We2", [E, INTER, D], F32, kind="ExternalInput")
    be2 = nc.dram_tensor("be2", [E, D], F32, kind="ExternalInput")
    We3 = nc.dram_tensor("We3", [E, D, INTER], F32, kind="ExternalInput")
    be3 = nc.dram_tensor("be3", [E, INTER], F32, kind="ExternalInput")
    ws1 = nc.dram_tensor("ws1", [D, 2 * INTER], F32, kind="ExternalInput")
    bs1 = nc.dram_tensor("bs1", [2 * INTER], F32, kind="ExternalInput")
    ws2 = nc.dram_tensor("ws2", [2 * INTER, D], F32, kind="ExternalInput")
    bs2 = nc.dram_tensor("bs2", [D], F32, kind="ExternalInput")
    ws3 = nc.dram_tensor("ws3", [D, 2 * INTER], F32, kind="ExternalInput")
    bs3 = nc.dram_tensor("bs3", [2 * INTER], F32, kind="ExternalInput")
    identc = nc.dram_tensor("identc", [128, 128], BF16, kind="ExternalInput")
    onesb = nc.dram_tensor("onesb", [1, 512], BF16, kind="ExternalInput")
    onesf = nc.dram_tensor("onesf", [1, 128], F32, kind="ExternalInput")
    dmD = nc.dram_tensor("dmD", [128, 128], F32, kind="ExternalInput")
    dmC = nc.dram_tensor("dmC", [128, 128], F32, kind="ExternalInput")

    out = nc.dram_tensor("out", [NSEQ, L, D], F32, kind="ExternalOutput")
    vvdbg = nc.dram_tensor("vvdbg", [128, NDC * L], BF16, kind="ExternalInput") if DIRECT_VV else None
    x1kind = "ExternalOutput" if "x1" in taps else "Internal"
    x1s = nc.dram_tensor("x1s", [NSEQ, L, D], F32, kind=x1kind)

    tap_t = {}
    if "qkv" in taps:
        tap_t["tq"] = nc.dram_tensor("tap_q", [128, NDC * L], BF16, kind="ExternalOutput")
        tap_t["tk"] = nc.dram_tensor("tap_k", [128, NDC * L], BF16, kind="ExternalOutput")
        tap_t["tv"] = nc.dram_tensor("tap_v", [128, NT * D], BF16, kind="ExternalOutput")
    if "vv" in taps:
        tap_t["tvv"] = nc.dram_tensor("tap_vv", [128, NDC * L], BF16, kind="ExternalOutput")
    if "comb" in taps:
        tap_t["tcomb"] = nc.dram_tensor("tap_comb", [128, NT * E], F32, kind="ExternalOutput")
    if "vv2" in taps:
        tap_t["tvv2"] = nc.dram_tensor("tap_vv2", [128, NDC * L], BF16, kind="ExternalOutput")
    if "y" in taps:
        tap_t["ty"] = nc.dram_tensor("tap_y", [128, NT * D], F32, kind="ExternalOutput")
    if "rdbg" in taps:
        tap_t["trd"] = nc.dram_tensor("tap_rdbg", [128, 128], F32, kind="ExternalOutput")
    if "edbg" in taps:
        tap_t["th"] = nc.dram_tensor("tap_h", [128, 2048], BF16, kind="ExternalOutput")
        tap_t["ty0"] = nc.dram_tensor("tap_y0", [128, D], F32, kind="ExternalOutput")
        tap_t["tsl"] = nc.dram_tensor("tap_sl", [128, 4096], BF16, kind="ExternalOutput")
        tap_t["tp1"] = nc.dram_tensor("tap_p1", [128, 1024], F32, kind="ExternalOutput")

    A = nc.alloc_sbuf_tensor
    ident = A("ident", [128, 128], BF16)
    ones_b = A("ones_b", [1, 512], BF16)
    ones_f = A("ones_f", [1, 128], F32)
    dD = A("dD", [128, 128], F32)
    dC = A("dC", [128, 128], F32)
    dmask = A("dmask", [128, 128], F32)
    lg1 = A("lg1", [1, 1], F32)
    lgc = A("lgc", [128, 1], F32)
    bqkv_v = A("bqkv_v", [1, 1024], BF16)
    bproj_r = A("bproj_r", [1, 1024], BF16)
    gws = A("gws", [128, NDC * E], BF16)
    gbb = A("gbb", [128, E], F32)
    x_t = [A(f"x_t{i}", [128, D], F32) for i in range(2)]
    fin_x = [A(f"fin_x{i}", [128, D], F32) for i in range(2)]
    sqs = [A("sqs0", [128, D], F32)] * 2
    lns = [A(f"lns{i}", [128, 8], F32) for i in range(2)]
    sms = [A(f"sms{i}", [128, 4], F32) for i in range(2)]
    comb = A("comb", [128, NT * E], F32)
    r_orig = A("r_orig", [128, E], F32)
    r_s = A("r_s", [128, E], F32)
    r_gs = A("r_gs", [128, G], F32)
    r_t4 = A("r_t4", [128, G], F32)
    r_ms = A("r_ms", [128, E], F32)
    r_sc = A("r_sc", [128, 8], F32)
    rdbg = A("rdbg", [128, 128], F32) if "rdbg" in taps else None

    psS = nc.alloc_psum_tensor("psS", [128, 2048], F32)
    pbx = [nc.alloc_psum_tensor(f"pb{i}", [128, 512], F32) for i in range(2)]
    ptx = [nc.alloc_psum_tensor(f"pt{i}", [128, 1024], BF16) for i in range(2)]

    def PS(i):
        if i < 4:
            return psS[:, i * 512:(i + 1) * 512], f"bk{i}"
        return pbx[i - 4][:, :], f"bk{i}"

    def PS128(i, j):
        if i < 4:
            return psS[:, i * 512 + j * 128: i * 512 + (j + 1) * 128]
        return pbx[i - 4][:, j * 128:(j + 1) * 128]

    def PSB(i):
        return ptx[i - 6][:, :], f"bk{i}"

    def PSB128(i, j):
        return ptx[i - 6][:, j * 128:(j + 1) * 128]

    def fm3(src2d, p=128):
        return src2d.rearrange("(c p) o -> p c o", p=p)

    def sb3(t, c):
        return t[:, :].rearrange("p (c o) -> p c o", c=c)

    def dma(queue, outap, inap, r, w, tag=""):
        P.op(queue, lambda e, o=outap, i=inap: e.dma_start(o, i), r=r, w=w,
             dma=True, tag=tag)

    def dma_slow(queue, outap, inap, r, w, tag=""):
        P.op(queue, lambda e, o=outap, i=inap: e.dma_start(
            o, i, allow_slow_non_contiguous=True), r=r, w=w, dma=True, tag=tag)

    def mm(outap, bank, lhsT, rhs, r, start, stop):
        P.op("tensor", lambda e, o=outap, l=lhsT, rr=rhs, s1=start, s2=stop:
             e.matmul(o, l, rr, start=s1, stop=s2), r=r, w=[bank])

    def tp(outap, bank, inap, r):
        P.op("tensor", lambda e, o=outap, i=inap: e.transpose(o, i, ident[:, :]),
             r=list(r) + ["ident"], w=[bank])

    def vop(fn, r, w, tag=""):
        P.op("vector", fn, r=r, w=w, tag=tag)

    def sop(fn, r, w, tag=""):
        P.op("scalar", fn, r=r, w=w, tag=tag)

    def ln_tile(src_ap, src_keys, dst_ap, dst_key, slot, extra_scale=1.0):
        ls = lns[slot]
        lk = f"lns{slot}"
        sq = sqs[0]
        sk = "sqs0"
        su, mean, msq, var, inv, nb = (ls[:, 0:1], ls[:, 1:2], ls[:, 2:3],
                                       ls[:, 3:4], ls[:, 4:5], ls[:, 5:6])
        sop(lambda e: e.activation(sq[:, :], src_ap, AF.Square, accum_out=su),
            r=src_keys, w=[sk, lk])
        vop(lambda e: e.reduce_sum(mean, src_ap, axis=AX.X), r=src_keys, w=[lk])
        vop(lambda e: e.tensor_scalar(mean, mean, 1.0 / D, None, ALU.mult),
            r=[lk], w=[lk])
        vop(lambda e: e.tensor_scalar(msq, mean, mean, -EPS, ALU.mult, op1=ALU.add),
            r=[lk], w=[lk])
        vop(lambda e: e.scalar_tensor_tensor(var, su, 1.0 / D, msq, ALU.mult,
                                             ALU.subtract), r=[lk], w=[lk])
        sop(lambda e: e.activation(var, var, AF.Ln), r=[lk], w=[lk])
        sop(lambda e: e.activation(inv, var, AF.Exp, scale=-0.5), r=[lk], w=[lk])
        if extra_scale != 1.0:
            vop(lambda e: e.tensor_scalar(inv, inv, extra_scale, None, ALU.mult),
                r=[lk], w=[lk])
        vop(lambda e: e.tensor_scalar(nb, mean, inv, -1.0, ALU.mult, op1=ALU.mult),
            r=[lk], w=[lk])
        sop(lambda e: e.activation(dst_ap, src_ap, AF.Identity, bias=nb, scale=inv),
            r=list(src_keys) + [lk], w=[dst_key])

    dma("sync", ident[:, :], identc.ap(), r=[], w=["ident"])
    dma("sync", ones_b[:, :], onesb.ap(), r=[], w=["ones_b"])
    dma("sync", ones_f[:, :], onesf.ap(), r=[], w=["ones_f"])
    dma("sync", dD[:, :], dmD.ap(), r=[], w=["dD"])
    dma("sync", dC[:, :], dmC.ap(), r=[], w=["dC"])
    rowify = lambda apx, n: apx.rearrange("(a b) -> a b", a=1)
    dma("gpsimd", bqkv_v[:, :], bqkv.ap()[2048:3072].rearrange("(a b) -> a b", a=1),
        r=[], w=["bqkv_v"])
    dma("gpsimd", bproj_r[:, :], rowify(bproj.ap(), D), r=[], w=["bproj_r"])

    dma("sync", lgc[:, :], bass.AP(gamma, 0, [[0, 128], [1, 1]]),
        r=[], w=["lgc"])
    sop(lambda e: e.activation(lgc[:, :], lgc[:, :], AF.Ln), r=["lgc"], w=["lgc"])
    sop(lambda e: e.activation(dmask[:, :], dD[:, :], AF.Exp, scale=lgc[:, :]),
        r=["dD", "lgc"], w=["dmask"])
    vop(lambda e: e.tensor_tensor(dmask[:, :], dmask[:, :], dC[:, :], ALU.mult),
        r=["dmask", "dC"], w=["dmask"])

    dma("sync", gbb[:, :], bass.AP(gate_b, 0, [[0, 128], [1, E]]),
        r=[], w=["gbb"])



    gwtmp_cm = nc.sbuf_tensor("gwtmp", [16, D], BF16, side="right")
    gwtmp = gwtmp_cm.__enter__()
    P.new_buf("gwtmp")
    dma("gpsimd", gwtmp[:, :], gate_w.ap(), r=[], w=["gwtmp"])
    for dc in range(NDC):
        apo, ko = PSB(6 + dc % 2)
        P.op("tensor", lambda e, dc=dc, apo=apo: e.transpose(
            apo[:, 0:16], gwtmp[0:16, dc * 128:(dc + 1) * 128], ident[0:16, 0:16]),
            r=["gwtmp", "ident"], w=[ko])
        vop(lambda e, dc=dc, apo=apo: e.tensor_copy(
            gws[:, dc * E:(dc + 1) * E], apo[:, 0:16]), r=[ko], w=["gws"])
    gwtmp_cm.__exit__(None, None, None)
    P.retire("gwtmp")

    for s in range([1] if DIRECT_VV else [NSEQ])[0] if False else range(1 if DIRECT_VV else NSEQ):
        xseq = xs.ap()[s]
        x1seq = x1s.ap()[s]
        outseq = out.ap()[s]

        k_cm = nc.sbuf_tensor(f"k_fm{s}", [128, NDC * L], BF16, side="left")
        k_fm = k_cm.__enter__(); P.new_buf("k_fm")
        v_cm = nc.sbuf_tensor(f"v_tm{s}", [128, NT * D], BF16, side="left")
        v_tm = v_cm.__enter__(); P.new_buf("v_tm")
        q_cm = nc.sbuf_tensor(f"q_fm{s}", [128, NDC * L], BF16, side="left")
        q_fm = q_cm.__enter__(); P.new_buf("q_fm")
        wq_cm = nc.sbuf_tensor(f"wqkv{s}", [128, NDC * 3 * D], BF16, side="right")
        wqkv_sb = wq_cm.__enter__(); P.new_buf("wqkv")
        uf_cm = nc.sbuf_tensor(f"u_fm{s}", [128, NDC * 512], BF16, side="right")
        u_fm = uf_cm.__enter__(); P.new_buf("u_fm")
        ut_cm = nc.sbuf_tensor(f"u_tm{s}", [128, 2 * D], BF16, side="right")
        u_tm = ut_cm.__enter__(); P.new_buf("u_tm")
        bqk_cm = nc.sbuf_tensor(f"bqkv_qk{s}", [1, 2048], BF16, side="right")
        bqkv_qk = bqk_cm.__enter__(); P.new_buf("bqkv_qk")
        dma("gpsimd", bqkv_qk[:, :], bqkv.ap()[0:2048].rearrange("(a b) -> a b", a=1),
            r=[], w=["bqkv_qk"])

        if DIRECT_VV == 1:
            P1_RANGE = []
        elif DIRECT_VV == 2:
            P1_RANGE = list(range(NG))
        else:
            P1_RANGE = list(range(NG))
        dma("gpsimd", sb3(wqkv_sb, NDC), fm3(Wqkv.ap()),
            r=[], w=["wqkv"], tag="wqkv")

        for g in P1_RANGE:
            for tt in range(4):
                t = g * 4 + tt
                xk = f"x_t{t % 2}"
                dma("sync", x_t[t % 2][:, :], xseq[t * 128:(t + 1) * 128, :],
                    r=[], w=[xk], tag="x")
                uk = f"u_tm{tt % 2}"
                u_ap = u_tm[:, (tt % 2) * D:(tt % 2) * D + D]
                ln_tile(x_t[t % 2][:, :], [xk], u_ap, "u_tm", tt % 2)
                bi = 6 + tt % 2
                apo, ko = PSB(bi)
                for dc in range(NDC):
                    tp(PSB128(bi, dc), ko, u_ap[:, dc * 128:(dc + 1) * 128],
                       r=["u_tm"])
                dst = u_fm[:, :].rearrange("p (c t) -> p c t", c=NDC)[
                    :, :, tt * 128:(tt + 1) * 128]
                src = apo.rearrange("p (c t) -> p c t", c=NDC)
                vop(lambda e, d0=dst, s0=src: e.tensor_copy(d0, s0),
                    r=[ko], w=["u_fm"])
            for which, base, kdst in (("q", 0, q_fm), ("k", D, k_fm)):
                for oc in range(NDC):
                    bi = 4 + oc % 2
                    apo, ko = PS(bi)
                    for dc in range(NDC):
                        mm(apo, ko,
                           wqkv_sb[:, dc * 3 * D + base + oc * 128:
                                   dc * 3 * D + base + (oc + 1) * 128],
                           u_fm[:, dc * 512:(dc + 1) * 512],
                           r=["wqkv", "u_fm"], start=(dc == 0), stop=False)
                    mm(apo, ko, bqkv_qk[:, base + oc * 128: base + (oc + 1) * 128],
                       ones_b[:, :], r=["bqkv_qk", "ones_b"], start=False, stop=True)
                    dst = kdst[:, oc * L + g * 512: oc * L + (g + 1) * 512]
                    scl = INV_SQRT_D if which == "q" else 1.0
                    sop(lambda e, d0=dst, a0=apo, s0=scl: e.activation(
                        d0, a0, AF.Identity, scale=s0),
                        r=[ko], w=["q_fm" if which == "q" else "k_fm"])
            for tt in range(4):
                t = g * 4 + tt
                for half in range(2):
                    bi = (tt * 2 + half) % 4
                    apo, ko = PS(bi)
                    for dc in range(NDC):
                        mm(apo, ko, u_fm[:, dc * 512 + tt * 128: dc * 512 + (tt + 1) * 128],
                           wqkv_sb[:, dc * 3 * D + 2 * D + half * 512:
                                   dc * 3 * D + 2 * D + (half + 1) * 512],
                           r=["wqkv", "u_fm"], start=(dc == 0), stop=False)
                    mm(apo, ko, ones_b[:, 0:128], bqkv_v[:, half * 512:(half + 1) * 512],
                       r=["ones_b", "bqkv_v"], start=False, stop=True)
                    dst = v_tm[:, t * D + half * 512: t * D + (half + 1) * 512]
                    vop(lambda e, d0=dst, a0=apo: e.tensor_copy(d0, a0),
                        r=[ko], w=["v_tm"])

        bqk_cm.__exit__(None, None, None); P.retire("bqkv_qk")
        ut_cm.__exit__(None, None, None); P.retire("u_tm")
        uf_cm.__exit__(None, None, None); P.retire("u_fm")
        wq_cm.__exit__(None, None, None); P.retire("wqkv")

        if "qkv" in taps and s == 0:
            dma("sync", tap_t["tq"].ap(), q_fm[:, :], r=["q_fm"], w=["tapq"])
            dma("sync", tap_t["tk"].ap(), k_fm[:, :], r=["k_fm"], w=["tapk"])
            dma("sync", tap_t["tv"].ap(), v_tm[:, :], r=["v_tm"], w=["tapv"])

        vv_cm = nc.sbuf_tensor(f"vv_fm{s}", [128, NDC * L], BF16, side="right")
        vv_fm = vv_cm.__enter__(); P.new_buf("vv_fm")
        wp_cm = nc.sbuf_tensor(f"wproj{s}", [128, NDC * D], BF16, side="right")
        wproj_sb = wp_cm.__enter__(); P.new_buf("wproj")
        P_cm = nc.sbuf_tensor(f"P_sb{s}", [128, 2 * L], BF16, side="right")
        P_sb = P_cm.__enter__(); P.new_buf("P_sb0"); P.new_buf("P_sb1")
        PT_cm = nc.sbuf_tensor(f"PT_sb{s}", [128, 2 * L], BF16, side="right")
        PT_sb = PT_cm.__enter__(); P.new_buf("PT_sb0"); P.new_buf("PT_sb1")
        atm_cm = nc.sbuf_tensor(f"a_tm{s}", [128, D], BF16, side="right")
        a_tm = atm_cm.__enter__(); P.new_buf("a_tm")
        afm_cm = nc.sbuf_tensor(f"a_fm{s}", [128, D], BF16, side="right")
        a_fm = afm_cm.__enter__(); P.new_buf("a_fm")
        a2_cm = nc.sbuf_tensor(f"a2{s}", [128, D], F32, side="right")
        a2 = a2_cm.__enter__(); P.new_buf("a2")
        a2n_cm = nc.sbuf_tensor(f"a2n{s}", [128, D], F32, side="right")
        a2n = a2n_cm.__enter__(); P.new_buf("a2n")
        vvt_cm = nc.sbuf_tensor(f"vvt{s}", [128, 2 * D], BF16, side="right")
        vvt = vvt_cm.__enter__(); P.new_buf("vvt0"); P.new_buf("vvt1")

        dma("gpsimd", sb3(wproj_sb, NDC), fm3(Wproj.ap()),
            r=[], w=["wproj"], tag="wproj")

        if DIRECT_VV in (1, 2) and s == 0:
            dma("sync", vv_fm[:, :], vvdbg.ap(), r=[], w=["vv_fm"])
        P2A_RANGE = [] if DIRECT_VV in (1, 2) else list(range(NT))

        def p2a_tail(t):
            sl_ = t % 2
            Pap = P_sb[:, sl_ * L: sl_ * L + L]
            Pk = f"P_sb{sl_}"
            PTap = PT_sb[:, sl_ * L: sl_ * L + L]
            PTk = f"PT_sb{sl_}"
            sm = sms[sl_]
            smk = f"sms{sl_}"
            rs = sm[:, 2:3]
            if 'pv' in P2A_PARTS:
                for b in range(2):
                    apo, ko = PSB(6 + b)
                    for j in range(8):
                        tk = b * 8 + j
                        tp(PSB128(6 + b, j), ko,
                           Pap[:, tk * 128:(tk + 1) * 128], r=[Pk])
                    vop(lambda e, b=b, a0=apo, d0=PTap: e.tensor_copy(
                        d0[:, b * 1024:(b + 1) * 1024], a0), r=[ko], w=[PTk])
            for half in (range(2) if 'pv' in P2A_PARTS else []):
                apo, ko = PS(4 + half)
                for tk in range(NT):
                    mm(apo, ko, PTap[:, tk * 128:(tk + 1) * 128],
                       v_tm[:, tk * D + half * 512: tk * D + (half + 1) * 512],
                       r=[PTk, "v_tm"], start=(tk == 0), stop=(tk == NT - 1))
                sop(lambda e, a0=apo, h=half, r0=rs: e.activation(
                    a_tm[:, h * 512:(h + 1) * 512], a0, AF.Identity, scale=r0),
                    r=[ko, smk], w=["a_tm"])
            if 'proj' in P2A_PARTS:
                apo, ko = PSB(6)
                for dc in range(NDC):
                    tp(PSB128(6, dc), ko, a_tm[:, dc * 128:(dc + 1) * 128],
                       r=["a_tm"])
                vop(lambda e, a0=apo: e.tensor_copy(a_fm[:, :], a0),
                    r=[ko], w=["a_fm"])
            for half in (range(2) if 'proj' in P2A_PARTS else []):
                apo, ko = PS(4 + half)
                for dc in range(NDC):
                    mm(apo, ko, a_fm[:, dc * 128:(dc + 1) * 128],
                       wproj_sb[:, dc * D + half * 512: dc * D + (half + 1) * 512],
                       r=["a_fm", "wproj"], start=(dc == 0), stop=False)
                mm(apo, ko, ones_b[:, 0:128], bproj_r[:, half * 512:(half + 1) * 512],
                   r=["ones_b", "bproj_r"], start=False, stop=True)
                vop(lambda e, a0=apo, h=half: e.tensor_copy(
                    a2[:, h * 512:(h + 1) * 512], a0), r=[ko], w=["a2"])
            xk = f"x_t{sl_}"
            if 'x1' in P2A_PARTS:
                dma("sync", x_t[sl_][:, :], xseq[t * 128:(t + 1) * 128, :],
                    r=[], w=[xk], tag="x2")
                ln_tile(a2[:, :], ["a2"], a2n[:, :], "a2n", sl_)
                vop(lambda e, xa=x_t[sl_]: e.tensor_tensor(
                    xa[:, :], a2n[:, :], xa[:, :], ALU.add), r=["a2n", xk], w=[xk])
                dma("sync", x1seq[t * 128:(t + 1) * 128, :], x_t[sl_][:, :],
                    r=[xk], w=[f"x1s{s}_{t}"], tag="x1w")
                ln_tile(x_t[sl_][:, :], [xk],
                        vvt[:, sl_ * D: sl_ * D + D], f"vvt{sl_}", sl_)

        def p2a_vvtrans(t):
            if 'vvt' not in P2A_PARTS or 'x1' not in P2A_PARTS:
                return
            sl_ = t % 2
            vsrc = vvt[:, sl_ * D: sl_ * D + D]
            vk = f"vvt{sl_}"
            apo, ko = PSB(7)
            for dc in range(NDC):
                tp(PSB128(7, dc), ko, vsrc[:, dc * 128:(dc + 1) * 128], r=[vk])
            dst = vv_fm[:, :].rearrange("p (c t) -> p c t", c=NDC)[
                :, :, t * 128:(t + 1) * 128]
            src = apo.rearrange("p (c t) -> p c t", c=NDC)
            vop(lambda e, d0=dst, s0=src: e.tensor_copy(d0, s0),
                r=[ko], w=["vv_fm"])

        for t in P2A_RANGE:
            for tc in (range(4) if 'sc' in P2A_PARTS else []):
                apo, ko = PS(tc)
                for dc in range(NDC):
                    mm(apo, ko,
                       q_fm[:, dc * L + t * 128: dc * L + (t + 1) * 128],
                       k_fm[:, dc * L + tc * 512: dc * L + (tc + 1) * 512],
                       r=["q_fm", "k_fm"], start=(dc == 0), stop=(dc == NDC - 1))
            dtc, dj = t // 4, t % 4
            apd = PS128(dtc, dj)
            if 'sc' in P2A_PARTS:
                vop(lambda e, a0=apd: e.tensor_tensor(a0, a0, dmask[:, :], ALU.mult),
                    r=[f"bk{dtc}", "dmask"], w=[f"bk{dtc}"])
            sl_ = t % 2
            sm = sms[sl_]
            smk = f"sms{sl_}"
            nmax, ssum, rs = sm[:, 0:1], sm[:, 1:2], sm[:, 2:3]
            if 'sc' in P2A_PARTS:
                vop(lambda e, n0=nmax: e.reduce_max(n0, psS[:, :], axis=AX.X,
                                                    negate=True),
                    r=["bk0", "bk1", "bk2", "bk3"], w=[smk])
                sop(lambda e, n0=nmax, s0=ssum, d0=P_sb[:, sl_ * L: sl_ * L + L]:
                    e.activation(d0, psS[:, :], AF.Exp, bias=n0, accum_out=s0),
                    r=["bk0", "bk1", "bk2", "bk3", smk],
                    w=[f"P_sb{sl_}", smk])
                vop(lambda e, s0=ssum, r0=rs: e.reciprocal(r0, s0), r=[smk], w=[smk])
            if t >= 2:
                p2a_vvtrans(t - 2)
            if t >= 1:
                p2a_tail(t - 1)
        if P2A_RANGE:
            p2a_vvtrans(NT - 2)
            p2a_tail(NT - 1)
            p2a_vvtrans(NT - 1)

        if DIRECT_VV == 3 and s == 0:
            dma("sync", vv_fm[:, :], vvdbg.ap(), r=["vv_fm"], w=["vv_fm"])
        vvt_cm.__exit__(None, None, None); P.retire("vvt0", "vvt1")
        a2n_cm.__exit__(None, None, None); P.retire("a2n")
        a2_cm.__exit__(None, None, None); P.retire("a2")
        afm_cm.__exit__(None, None, None); P.retire("a_fm")
        atm_cm.__exit__(None, None, None); P.retire("a_tm")
        PT_cm.__exit__(None, None, None); P.retire("PT_sb0", "PT_sb1")
        P_cm.__exit__(None, None, None); P.retire("P_sb0", "P_sb1")
        wp_cm.__exit__(None, None, None); P.retire("wproj")
        q_cm.__exit__(None, None, None); P.retire("q_fm")

        if "vv" in taps and s == 0:
            dma("sync", tap_t["tvv"].ap(), vv_fm[:, :], r=["vv_fm"], w=["tapvv"])

        apg, kg = PS(4)
        for t in range(NT):
            for dc in range(NDC):
                mm(apg[:, t * E:(t + 1) * E], kg,
                   vv_fm[:, dc * L + t * 128: dc * L + (t + 1) * 128],
                   gws[:, dc * E:(dc + 1) * E],
                   r=["vv_fm", "gws"], start=(dc == 0), stop=(dc == NDC - 1))
        for t in range(NT):
            gsl = apg[:, t * E:(t + 1) * E]
            nmax, ssum, rs = r_sc[:, 0:1], r_sc[:, 1:2], r_sc[:, 2:3]
            thr = r_sc[:, 3:4]
            m1, m2 = r_sc[:, 4:5], r_sc[:, 5:6]
            vop(lambda e, g0=gsl, n0=nmax: e.reduce_max(n0, g0, axis=AX.X,
                                                        negate=True),
                r=[kg], w=["r_sc"])
            sop(lambda e, g0=gsl, n0=nmax, s0=ssum: e.activation(
                r_orig[:, :], g0, AF.Exp, bias=n0, accum_out=s0),
                r=[kg, "r_sc"], w=["r_orig", "r_sc"])
            vop(lambda e, s0=ssum, r0=rs: e.reciprocal(r0, s0),
                r=["r_sc"], w=["r_sc"])
            vop(lambda e, r0=rs: e.tensor_scalar(r_orig[:, :], r_orig[:, :], r0,
                                                 None, ALU.mult),
                r=["r_orig", "r_sc"], w=["r_orig"])
            vop(lambda e: e.tensor_tensor(r_s[:, :], r_orig[:, :], gbb[:, :],
                                          ALU.add), r=["r_orig", "gbb"], w=["r_s"])
            first = True
            for (i, j) in ((0, 1), (0, 2), (0, 3), (1, 2), (1, 3), (2, 3)):
                a0 = r_s[:, i::4]
                b0 = r_s[:, j::4]
                if first:
                    vop(lambda e, a0=a0, b0=b0: e.tensor_tensor(
                        r_gs[:, :], a0, b0, ALU.add), r=["r_s"], w=["r_gs"])
                    first = False
                else:
                    vop(lambda e, a0=a0, b0=b0: e.tensor_tensor(
                        r_t4[:, :], a0, b0, ALU.add), r=["r_s"], w=["r_t4"])
                    vop(lambda e: e.tensor_tensor(r_gs[:, :], r_gs[:, :],
                                                  r_t4[:, :], ALU.max),
                        r=["r_gs", "r_t4"], w=["r_gs"])
            first = True
            for (i, j) in ((0, 1), (0, 2), (0, 3), (1, 2), (1, 3), (2, 3)):
                vop(lambda e, i=i, j=j: e.tensor_tensor(
                    r_t4[:, 0:1], r_gs[:, i:i + 1], r_gs[:, j:j + 1], ALU.min),
                    r=["r_gs"], w=["r_t4"])
                if first:
                    vop(lambda e: e.tensor_copy(thr, r_t4[:, 0:1]),
                        r=["r_t4"], w=["r_sc"])
                    first = False
                else:
                    vop(lambda e: e.tensor_tensor(thr, thr, r_t4[:, 0:1], ALU.max),
                        r=["r_sc", "r_t4"], w=["r_sc"])
            vop(lambda e: e.tensor_scalar(r_t4[:, :], r_gs[:, :], thr, None,
                                          ALU.is_ge), r=["r_gs", "r_sc"], w=["r_t4"])
            for i in range(4):
                vop(lambda e, i=i: e.tensor_scalar(r_ms[:, i::4], r_t4[:, :], 1.0,
                                                   1e30, ALU.subtract, op1=ALU.mult),
                    r=["r_t4"], w=["r_ms"])
            vop(lambda e: e.tensor_tensor(r_ms[:, :], r_ms[:, :], r_s[:, :],
                                          ALU.add), r=["r_ms", "r_s"], w=["r_ms"])
            vop(lambda e, m0=m1: e.reduce_max(m0, r_ms[:, :], axis=AX.X),
                r=["r_ms"], w=["r_sc"])
            vop(lambda e, m0=m1: e.tensor_scalar(r_s[:, :], r_ms[:, :], m0, -2e30,
                                                 ALU.is_ge, op1=ALU.mult),
                r=["r_ms", "r_sc"], w=["r_s"])
            vop(lambda e: e.tensor_tensor(r_s[:, :], r_s[:, :], r_ms[:, :],
                                          ALU.add), r=["r_s", "r_ms"], w=["r_s"])
            vop(lambda e, m0=m2: e.reduce_max(m0, r_s[:, :], axis=AX.X),
                r=["r_s"], w=["r_sc"])
            vop(lambda e, m0=m2: e.tensor_scalar(r_s[:, :], r_ms[:, :], m0, None,
                                                 ALU.is_ge), r=["r_ms", "r_sc"],
                w=["r_s"])
            vop(lambda e, t=t: e.tensor_tensor(comb[:, t * E:(t + 1) * E],
                                               r_orig[:, :], r_s[:, :], ALU.mult),
                r=["r_orig", "r_s"], w=["comb"])
            if "rdbg" in taps and s == 0 and t == 0:
                for i_, (tt_, kk_, n_) in enumerate([
                        (r_orig, "r_orig", E), (r_s, "r_s", E), (r_gs, "r_gs", G),
                        (r_t4, "r_t4", G), (r_ms, "r_ms", E), (gbb, "gbb", E),
                        (r_sc, "r_sc", 8)]):
                    vop(lambda e, i_=i_, tt_=tt_, n_=n_: e.tensor_copy(
                        rdbg[:, 16 * i_:16 * i_ + n_], tt_[:, 0:n_]),
                        r=[kk_], w=["rdbg"])
                dma("sync", tap_t["trd"].ap(), rdbg[:, :], r=["rdbg"], w=["taprd"])

        if "comb" in taps and s == 0:
            dma("sync", tap_t["tcomb"].ap(), comb[:, :], r=["comb"], w=["tapc"])

        v_cm.__exit__(None, None, None); P.retire("v_tm")
        k_cm.__exit__(None, None, None); P.retire("k_fm")
        ylo_cm = nc.sbuf_tensor(f"ylo{s}", [128, 8 * D], F32, side="right")
        y_lo = ylo_cm.__enter__(); P.new_buf("y_lo")
        yhi_cm = nc.sbuf_tensor(f"yhi{s}", [128, 8 * D], F32, side="right")
        y_hi = yhi_cm.__enter__(); P.new_buf("y_hi")

        def y_ap(t):
            if t < 8:
                return y_lo[:, t * D:(t + 1) * D], "y_lo"
            return y_hi[:, (t - 8) * D:(t - 8) * D + D], "y_hi"

        es_cm = [nc.sbuf_tensor(f"eslot{s}_{i}", [128, 3 * 4096], BF16, side="right")
                 for i in range(2)]
        eslot = [c.__enter__() for c in es_cm]
        for _sl in range(2):
            for _kk in ("w1", "w3", "w2"):
                P.new_buf(f"eslot{_sl}_{_kk}")
        br_cm = [nc.sbuf_tensor(f"brow{s}_{i}", [1, 2048], BF16, side="right")
                 for i in range(2)]
        bc_cm = [nc.sbuf_tensor(f"bcol{s}_{i}", [128, 8], F32, side="right")
                 for i in range(2)]
        brow = [c.__enter__() for c in br_cm]
        for _sl in range(2):
            for _kk in ("b1", "b3", "b2"):
                P.new_buf(f"brow{_sl}_{_kk}")
        bcol = [c.__enter__() for c in bc_cm]
        P.new_buf("bcol0_b1"); P.new_buf("bcol0_b3")
        P.new_buf("bcol1_b1"); P.new_buf("bcol1_b3")
        h_cm = nc.sbuf_tensor(f"h_sb{s}", [128, 4 * 512], BF16, side="right")
        h_sb = h_cm.__enter__(); P.new_buf("h_sb")
        s1_cm = nc.sbuf_tensor(f"s1_sb{s}", [128, 512], BF16, side="right")
        s1_sb = s1_cm.__enter__(); P.new_buf("s1_sb")
        fn_cm = nc.sbuf_tensor(f"fn_sb{s}", [128, D], F32, side="right")
        fn_sb = fn_cm.__enter__(); P.new_buf("fn_sb")

        def load_expert(e_i, slot):
            sl = eslot[slot]
            k1w = f"eslot{slot}_w1"
            k3w = f"eslot{slot}_w3"
            k2w = f"eslot{slot}_w2"
            kb1 = f"brow{slot}_b1"
            kb3 = f"brow{slot}_b3"
            kb2 = f"brow{slot}_b2"
            if e_i < E:
                w1 = fm3(We1.ap()[e_i])
                w3 = fm3(We3.ap()[e_i])
                w2 = fm3(We2.ap()[e_i])
                dma("gpsimd", sl[:, 0:4096].rearrange("p (c o) -> p c o", c=8), w1,
                    r=[], w=[k1w], tag="we1")
                dma("gpsimd", sl[:, 4096:8192].rearrange("p (c o) -> p c o", c=8), w3,
                    r=[], w=[k3w], tag="we3")
                dma("gpsimd", sl[:, 8192:12288].rearrange("p (c o) -> p c o", c=4), w2,
                    r=[], w=[k2w], tag="we2")
                dma_slow("sync", bcol[slot][:, 0:4],
                         be1.ap()[e_i].rearrange("(c p) -> p c", p=128),
                         r=[], w=[kb1])
                dma_slow("sync", bcol[slot][:, 4:8],
                         be3.ap()[e_i].rearrange("(c p) -> p c", p=128),
                         r=[], w=[kb3])
                dma("gpsimd", brow[slot][:, 1024:2048],
                    rowify(be2.ap()[e_i], D), r=[], w=[kb2])
            else:
                hh = e_i - E
                w1 = fm3(ws1.ap()[:, hh * 512:(hh + 1) * 512])
                w3 = fm3(ws3.ap()[:, hh * 512:(hh + 1) * 512])
                w2 = fm3(ws2.ap()[hh * 512:(hh + 1) * 512, :])
                dma("gpsimd", sl[:, 0:4096].rearrange("p (c o) -> p c o", c=8), w1,
                    r=[], w=[k1w], tag="ws1")
                dma("gpsimd", sl[:, 4096:8192].rearrange("p (c o) -> p c o", c=8), w3,
                    r=[], w=[k3w], tag="ws3")
                dma("gpsimd", sl[:, 8192:12288].rearrange("p (c o) -> p c o", c=4), w2,
                    r=[], w=[k2w], tag="ws2")
                dma_slow("sync", bcol[slot][:, 0:4],
                         bs1.ap()[hh * 512:(hh + 1) * 512].rearrange("(c p) -> p c", p=128),
                         r=[], w=[kb1])
                dma_slow("sync", bcol[slot][:, 4:8],
                         bs3.ap()[hh * 512:(hh + 1) * 512].rearrange("(c p) -> p c", p=128),
                         r=[], w=[kb3])
                if hh == 0:
                    dma("gpsimd", brow[slot][:, 1024:2048], rowify(bs2.ap(), D),
                        r=[], w=[kb2])

        if BARRIER_P2B:
            _BARN[0] += 1
            bn = _BARN[0]
            for _e in ("tensor", "vector", "scalar"):
                P.op(_e, lambda e: e.nop(), r=[], w=[f"bar{bn}_{_e}"])
            for _e in ("tensor", "vector", "scalar"):
                P.op(_e, lambda e: e.nop(),
                     r=[f"bar{bn}_tensor", f"bar{bn}_vector", f"bar{bn}_scalar"], w=[])
        order = EXPERT_ORDER
        load_expert(order[0], 0)
        for idx in range(len(order)):
            e_i = order[idx]
            slot = idx % 2
            sl = eslot[slot]
            k1w = f"eslot{slot}_w1"
            k3w = f"eslot{slot}_w3"
            k2w = f"eslot{slot}_w2"
            kb1 = f"brow{slot}_b1"
            kb3 = f"brow{slot}_b3"
            kb2 = f"brow{slot}_b2"
            if idx + 1 < len(order):
                load_expert(order[idx + 1], 1 - slot)
            if "edbg" in taps and s == 0 and e_i == EDBG_Y:
                dma("sync", tap_t["ty0"].ap(), y_lo[:, 0:D], r=["y_lo"], w=["tapy0"])
            first = idx == 0
            shared = e_i >= E
            sh_h = e_i - E
            for g in range(NG):
                for ic in range(4):
                    b1, k1 = PS(ic % 2)
                    b3, k3 = PS(2 + ic % 2)
                    for dc in range(NDC):
                        mm(b1, k1, sl[:, dc * 512 + ic * 128: dc * 512 + (ic + 1) * 128],
                           vv_fm[:, dc * L + g * 512: dc * L + (g + 1) * 512],
                           r=[k1w, "vv_fm"], start=(dc == 0), stop=(dc == NDC - 1))
                    for dc in range(NDC):
                        mm(b3, k3, sl[:, 4096 + dc * 512 + ic * 128:
                                      4096 + dc * 512 + (ic + 1) * 128],
                           vv_fm[:, dc * L + g * 512: dc * L + (g + 1) * 512],
                           r=[k3w, "vv_fm"], start=(dc == 0), stop=(dc == NDC - 1))
                    sop(lambda e, a0=b1, ic=ic, sl_=slot: e.activation(
                        s1_sb[:, :], a0, AF.Silu, bias=bcol[sl_][:, ic:ic + 1]),
                        r=[k1, kb1], w=["s1_sb"])
                    vop(lambda e, a0=b3, ic=ic, sl_=slot: e.scalar_tensor_tensor(
                        h_sb[:, ic * 512:(ic + 1) * 512], a0,
                        bcol[sl_][:, 4 + ic: 5 + ic], s1_sb[:, :],
                        ALU.add, ALU.mult), r=["s1_sb", k3, kb3], w=["h_sb"])
                if "edbg" in taps and s == 0 and e_i == EDBG_E and g == 0:
                    dma("sync", tap_t["th"].ap(), h_sb[:, :], r=["h_sb"], w=["taph"])
                    dma("sync", tap_t["tsl"].ap(), sl[:, 0:4096], r=[k1w], w=["tapsl"])
                for tt in range(4):
                    t = g * 4 + tt
                    ya, yk = y_ap(t)
                    addb = (not shared) or sh_h == 0
                    for half in range(2):
                        bo, ko = PS(4 + half)
                        for ic in range(4):
                            mm(bo, ko, h_sb[:, ic * 512 + tt * 128:
                                            ic * 512 + (tt + 1) * 128],
                               sl[:, 8192 + ic * D + half * 512:
                                  8192 + ic * D + (half + 1) * 512],
                               r=["h_sb", k2w], start=(ic == 0),
                               stop=(ic == 3 and not addb))
                        if addb:
                            mm(bo, ko, ones_b[:, 0:128],
                               brow[slot][:, 1024 + half * 512: 1024 + (half + 1) * 512],
                               r=["ones_b", kb2], start=False, stop=True)
                        dst = ya[:, half * 512:(half + 1) * 512]
                        if first and shared:
                            vop(lambda e, d0=dst, a0=bo: e.tensor_copy(d0, a0),
                                r=[ko], w=[yk])
                        elif first:
                            vop(lambda e, d0=dst, a0=bo, t=t, e_i=e_i: e.tensor_scalar(
                                d0, a0, comb[:, t * E + e_i: t * E + e_i + 1], None, ALU.mult),
                                r=[ko, "comb"], w=[yk])
                        elif shared:
                            vop(lambda e, d0=dst, a0=bo: e.tensor_tensor(
                                d0, a0, d0, ALU.add), r=[ko, yk], w=[yk])
                        else:
                            vop(lambda e, d0=dst, a0=bo, t=t, e_i=e_i:
                                e.scalar_tensor_tensor(
                                    d0, a0, comb[:, t * E + e_i: t * E + e_i + 1],
                                    d0, ALU.mult, ALU.add),
                                r=[ko, "comb", yk], w=[yk])
                    if idx == len(order) - 1 and not DIRECT_VV:
                        ln_tile(ya, [yk], fn_sb[:, :], "fn_sb", t % 2)
                        fxk = f"fin_x{t % 2}"
                        dma("sync", fin_x[t % 2][:, :],
                            x1seq[t * 128:(t + 1) * 128, :],
                            r=[f"x1s{s}_{t}"], w=[fxk], tag="x1r")
                        vop(lambda e, xa=fin_x[t % 2]: e.tensor_tensor(
                            xa[:, :], fn_sb[:, :], xa[:, :], ALU.add),
                            r=["fn_sb", fxk], w=[fxk])
                        dma("sync", outseq[t * 128:(t + 1) * 128, :],
                            fin_x[t % 2][:, :],
                            r=[fxk], w=[f"outhbm{s}_{t}"], tag="outw")

        if "vv2" in taps and s == 0:
            dma("sync", tap_t["tvv2"].ap(), vv_fm[:, :], r=["vv_fm"], w=["tapvv2"])
        if "y" in taps and s == 0:
            dma("sync", tap_t["ty"].ap()[:, 0:8 * D], y_lo[:, :], r=["y_lo"], w=["tapy"])
            dma("sync", tap_t["ty"].ap()[:, 8 * D:], y_hi[:, :], r=["y_hi"], w=["tapy"])

        fn_cm.__exit__(None, None, None); P.retire("fn_sb")
        s1_cm.__exit__(None, None, None); P.retire("s1_sb")
        h_cm.__exit__(None, None, None); P.retire("h_sb")
        for c in reversed(bc_cm):
            c.__exit__(None, None, None)
        P.retire("bcol0_b1", "bcol0_b3", "bcol1_b1", "bcol1_b3")
        for c in reversed(br_cm):
            c.__exit__(None, None, None)
        P.retire(*[f"brow{i}_{k}" for i in range(2) for k in ("b1", "b3", "b2")])
        for c in reversed(es_cm):
            c.__exit__(None, None, None)
        P.retire(*[f"eslot{i}_{k}" for i in range(2) for k in ("w1", "w3", "w2")])
        yhi_cm.__exit__(None, None, None); P.retire("y_hi")
        ylo_cm.__exit__(None, None, None); P.retire("y_lo")
        vv_cm.__exit__(None, None, None); P.retire("vv_fm")

    with ExitStack() as stack:
        block = stack.enter_context(nc.Block())
        P.finalize(nc, block, stack)
    return nc


_HOST_CONSTS = None


def _host_consts():
    global _HOST_CONSTS
    if _HOST_CONSTS is None:
        i = np.arange(128)
        dD_ = np.maximum(i[:, None] - i[None, :], 0).astype(np.float32)
        dC_ = (i[:, None] >= i[None, :]).astype(np.float32)
        _HOST_CONSTS = {
            "identc": np.eye(128, dtype=ml_dtypes.bfloat16),
            "onesb": np.ones((1, 512), dtype=ml_dtypes.bfloat16),
            "onesf": np.ones((1, 128), dtype=np.float32),
            "dmD": dD_,
            "dmC": dC_,
        }
    return _HOST_CONSTS


def make_in_maps(inputs):
    x = np.ascontiguousarray(np.asarray(inputs["L0_attn_x"], dtype=np.float32))
    consts = _host_consts()
    common = {
        "Wqkv": np.ascontiguousarray(inputs["L0_attn_Wqkv"]),
        "bqkv": np.ascontiguousarray(inputs["L0_attn_bqkv"]),
        "Wproj": np.ascontiguousarray(inputs["L0_attn_Wproj"]),
        "bproj": np.ascontiguousarray(inputs["L0_attn_bproj"]),
        "gamma": np.asarray(inputs["L0_attn_gamma"], dtype=np.float32).reshape(1, 1),
        "gate_w": np.ascontiguousarray(inputs["L0_ffn_gate_w"]),
        "gate_b": np.ascontiguousarray(inputs["L0_ffn_gate_b"]),
        "We1": np.ascontiguousarray(inputs["L0_ffn_We1"]),
        "be1": np.ascontiguousarray(inputs["L0_ffn_be1"]),
        "We2": np.ascontiguousarray(inputs["L0_ffn_We2"]),
        "be2": np.ascontiguousarray(inputs["L0_ffn_be2"]),
        "We3": np.ascontiguousarray(inputs["L0_ffn_We3"]),
        "be3": np.ascontiguousarray(inputs["L0_ffn_be3"]),
        "ws1": np.ascontiguousarray(inputs["L0_ffn_ws1"]),
        "bs1": np.ascontiguousarray(inputs["L0_ffn_bs1"]),
        "ws2": np.ascontiguousarray(inputs["L0_ffn_ws2"]),
        "bs2": np.ascontiguousarray(inputs["L0_ffn_bs2"]),
        "ws3": np.ascontiguousarray(inputs["L0_ffn_ws3"]),
        "bs3": np.ascontiguousarray(inputs["L0_ffn_bs3"]),
    }
    common.update(consts)
    in_maps = []
    for c in range(8):
        m = dict(common)
        m["xs"] = np.ascontiguousarray(x[2 * c: 2 * c + 2])
        in_maps.append(m)
    return in_maps


def kernel(**inputs):
    nc = build_nc()
    in_maps = make_in_maps(inputs)
    res = run_bass_kernel_spmd(nc, in_maps, core_ids=list(range(8)))
    return np.concatenate([r["out"] for r in res.results], axis=0)
